# version 04
# baseline (speedup 1.0000x reference)
import numpy as np
import ml_dtypes
from contextlib import ExitStack

import concourse.bass as bass
import concourse.mybir as mybir
from concourse.bass_utils import run_bass_kernel_spmd

F32 = mybir.dt.float32
BF16 = mybir.dt.bfloat16
AF = mybir.ActivationFunctionType
ALU = mybir.AluOpType
AX = mybir.AxisListType

T = 2048
TK = 2560
D = 2048
DH = 128
HQ = 16
G = 4
RMS_EPS = 1.1920929e-07
LN_EPS = 1e-5
NT = 16
NTK = 20
KC = 16
QBIAS = float(-0.5 * np.log(128.0))
EXPBIAS = -16.0

ENGINES = ("tensor", "vector", "scalar", "gpsimd", "sync")


class Prog:
    def __init__(self):
        self.stream = {e: [] for e in ENGINES}
        self.sem = {}
        self.count = {}
        self.lastwait = {}

    def add_sem(self, name, handle):
        self.sem[name] = handle
        self.count[name] = 0

    def wait(self, eng, token):
        if token is None:
            return
        name, val = token
        if val <= 0:
            return
        if self.lastwait.get((eng, name), 0) >= val:
            return
        self.lastwait[(eng, name)] = val
        h = self.sem[name]
        self.stream[eng].append(lambda E, h=h, v=val: E.wait_ge(h, v))

    def op(self, eng, fn, waits=(), inc="default", amt=1):
        for t in waits:
            self.wait(eng, t)
        if inc == "default":
            inc = eng if eng != "sync" else None
        tok = None
        if inc is not None:
            self.count[inc] += amt
            tok = (inc, self.count[inc])
            h = self.sem[inc]
            self.stream[eng].append(
                lambda E, fn=fn, h=h, a=amt: fn(E).then_inc(h, a))
        else:
            self.stream[eng].append(lambda E, fn=fn: fn(E))
        return tok

    def dma(self, eng, qname, out, in_, waits=()):
        return self.op(eng, lambda E: E.dma_start(out=out, in_=in_),
                       waits=waits, inc=qname, amt=16)

    def barrier(self):
        snapshot = [(n, c) for n, c in self.count.items() if c > 0]
        for e in ENGINES:
            for n, c in snapshot:
                self.wait(e, (n, c))

    def token(self, name):
        return (name, self.count[name])


class Arena:

    def __init__(self, nc, size_bytes):
        self.nc = nc
        self.size = size_bytes
        self.pend = []

    def alloc(self, name, shape, dtype, off):
        per_part = int(np.prod(shape[1:])) * mybir.dt.size(dtype)
        assert off % 32 == 0, (name, off)
        assert off + per_part <= self.size, (name, off, per_part)
        hnd = self.nc.virtual_sbuf_tensor(name, shape, dtype)
        self.pend.append((hnd, off))
        return hnd

    def materialize(self, ctx):
        handle = ctx.enter_context(
            self.nc.sbuf_tensor("arena", [128, self.size], mybir.dt.uint8))
        base = self.nc.lookup_mloc(handle).addr
        for hnd, off in self.pend:
            self.nc._tensor(hnd.name, hnd.shape, hnd.dtype, type="SB")
            mloc = self.nc.lookup_mloc(hnd)
            mloc.allocated = True
            mloc.addr = base + off


def _ev(ap_):
    return ap_.rearrange("p (a two) -> p a two", two=2)[:, :, 0]


def _od(ap_):
    return ap_.rearrange("p (a two) -> p a two", two=2)[:, :, 1]


ARENA = 206144
A0_SERIAL = False


def build(debug=False, phases="full"):
    nc = bass.Bass("TRN2", target_bir_lowering=False)

    dx = nc.dram_tensor("x_own", [T, D], F32, kind="ExternalInput").ap()
    dxc = nc.dram_tensor("x_ctx", [512, D], F32, kind="ExternalInput").ap()
    dwqkvT = nc.dram_tensor("wqkvT", [D, 3072], F32, kind="ExternalInput").ap()
    dwpT = nc.dram_tensor("wpT", [D, D], F32, kind="ExternalInput").ap()
    dw1T = nc.dram_tensor("w1T", [D, 4096], F32, kind="ExternalInput").ap()
    dw2T = nc.dram_tensor("w2T", [4096, 4096], F32, kind="ExternalInput").ap()
    dw3T = nc.dram_tensor("w3T", [4096, D], F32, kind="ExternalInput").ap()
    db1c = nc.dram_tensor("b1c", [128, 32], F32, kind="ExternalInput").ap()
    db2c = nc.dram_tensor("b2c", [128, 32], F32, kind="ExternalInput").ap()
    db3r = nc.dram_tensor("b3r", [1, D], F32, kind="ExternalInput").ap()
    dmask = nc.dram_tensor("masks", [128, 5 * 640], BF16, kind="ExternalInput").ap()
    dropeC = nc.dram_tensor("ropeC", [128, NTK * DH], F32, kind="ExternalInput").ap()
    dropeS = nc.dram_tensor("ropeS", [128, NTK * DH], F32, kind="ExternalInput").ap()
    dident = nc.dram_tensor("ident", [128, 128], BF16, kind="ExternalInput").ap()
    dones = nc.dram_tensor("ones_row", [1, 128], F32, kind="ExternalInput").ap()

    dy = nc.dram_tensor("y", [T, D], F32, kind="ExternalOutput").ap()
    sck = "ExternalOutput" if debug else "Internal"
    dscY = nc.dram_tensor("scY", [HQ, 128, T], BF16, kind=sck).ap()
    dscXM = nc.dram_tensor("scXM", [T, D], F32, kind=sck).ap()
    dscH2 = nc.dram_tensor("scH2", [32, 128, T], BF16, kind=sck).ap()
    if debug:
        drr = nc.dram_tensor("dbg_rr", [NT, 128, 32], F32, kind="ExternalOutput").ap()
        dsqb = nc.dram_tensor("dbg_sqb", [NT, 128, 512], F32, kind="ExternalOutput").ap()
        dln1 = nc.dram_tensor("dbg_ln1T", [128, KC * TK], BF16, kind="ExternalOutput").ap()
        dqn = nc.dram_tensor("dbg_qn", [G, 128, 4 * T], BF16, kind="ExternalOutput").ap()
        dkn = nc.dram_tensor("dbg_kn", [G, 128, TK], BF16, kind="ExternalOutput").ap()
        dvt = nc.dram_tensor("dbg_vt", [G, 128, TK], BF16, kind="ExternalOutput").ap()
        dln2 = nc.dram_tensor("dbg_ln2T", [128, KC * T], BF16, kind="ExternalOutput").ap()
        dh1d = nc.dram_tensor("dbg_h1", [128, 32 * T], BF16, kind="ExternalOutput").ap()

    p = Prog()
    ar = Arena(nc, ARENA)

    ident = ar.alloc("identt", [128, 128], BF16, 0)
    onesr = ar.alloc("onesrr", [1, 128], F32, 256)
    b1c = ar.alloc("b1cc", [128, 32], F32, 768)
    b2c = ar.alloc("b2cc", [128, 32], F32, 896)
    constv = ar.alloc("constv", [128, 8], F32, 1024)
    P0 = 1088
    ln1T = ar.alloc("ln1T", [128, KC * TK], BF16, P0)
    A2R = P0 + 81920
    cur = A2R

    def nxt(sz):
        nonlocal cur
        off = cur
        cur += sz
        assert cur <= ARENA
        return off
    ropeC = ar.alloc("ropeCs", [128, NTK * DH], F32, nxt(10240))
    ropeS = ar.alloc("ropeSs", [128, NTK * DH], F32, nxt(10240))
    masks = ar.alloc("maskss", [128, 5 * 640], BF16, nxt(6400))
    A0ALIAS = cur
    wTq = ar.alloc("wTq", [128, KC * 512], BF16, nxt(16384))
    wTkv = ar.alloc("wTkv", [128, KC * 256], BF16, nxt(8192))
    stg = ar.alloc("stg", [128, 1024], F32, nxt(4096))
    sqb = ar.alloc("sqb", [128, 1024], F32, nxt(4096))
    rr = ar.alloc("rr", [128, 32], F32, nxt(128))
    ropet = ar.alloc("ropet", [128, 256], F32, nxt(1024))
    ntm = ar.alloc("ntm", [128, 512], BF16, nxt(1024))
    qnf = ar.alloc("qnf", [128, 4 * T], BF16, nxt(16384))
    knf = ar.alloc("knf", [128, TK], BF16, nxt(5120))
    vtm = ar.alloc("vtm", [128, TK], BF16, nxt(5120))
    pbuf = ar.alloc("pbuf", [128, 2 * 640], BF16, nxt(2560))
    ptsb = ar.alloc("ptsb", [128, 2 * 640], BF16, nxt(2560))
    yevb = ar.alloc("yevb", [128, 2 * 512], BF16, nxt(2048))
    attst = ar.alloc("attst", [128, 8], F32, nxt(32))
    assert cur <= ARENA
    a0x = ar.alloc("a0x", [128, 2 * D], F32, A0ALIAS)
    a0ln = ar.alloc("a0ln", [128, D], BF16, A0ALIAS + 16384)
    a0st = ar.alloc("a0st", [128, 64], F32, A0ALIAS + 20480)
    wpT = ar.alloc("wpTs", [128, KC * D], BF16, P0)
    A3B = P0 + 65536
    a3yld = ar.alloc("a3yld", [128, 2 * HQ * 256], BF16, A3B)
    a3y2 = ar.alloc("a3y2", [128, D], F32, A3B + 16384)
    a3xr = ar.alloc("a3xr", [128, 2 * D], F32, A3B + 24576)
    a3xm = ar.alloc("a3xm", [128, D], F32, A3B + 40960)
    a3ln2 = ar.alloc("a3ln2", [128, D], BF16, A3B + 49152)
    a3st = ar.alloc("a3st", [128, 128], F32, A3B + 53248)
    LN2T = P0 + 131072
    ln2T = ar.alloc("ln2T", [128, KC * T], BF16, LN2T)
    h1 = ar.alloc("h1s", [128, 32 * T], BF16, P0)
    B1B = LN2T + 65536
    wT1 = ar.alloc("wT1", [128, 2 * KC * 128], BF16, B1B)
    wT2 = ar.alloc("wT2", [128, 2 * 32 * 128], BF16, LN2T)
    h2b = ar.alloc("h2b", [128, 2 * D], BF16, LN2T + 16384)
    w3Ts = ar.alloc("w3Ts", [128, 32 * T], BF16, P0)
    b3h2 = ar.alloc("b3h2", [128, 2 * 32 * 256], BF16, LN2T)
    b3y3 = ar.alloc("b3y3", [128, D], F32, LN2T + 32768)
    b3xm = ar.alloc("b3xm", [128, D], F32, LN2T + 40960)
    b3bc = ar.alloc("b3bc", [128, D], F32, LN2T + 49152)
    b3st = ar.alloc("b3st", [128, 64], F32, LN2T + 57344)
    b3r = ar.alloc("b3rr", [1, D], F32, LN2T + 57600)
    b3ob = ar.alloc("b3ob", [128, D], F32, LN2T + 65792)

    ctx = ExitStack()
    with ctx:
        for s in ("tensor", "vector", "scalar", "gpsimd", "q_in", "q_wt", "q_out"):
            p.add_sem(s, ctx.enter_context(nc.semaphore(s)))
        ar.materialize(ctx)
        psQ = ctx.enter_context(nc.psum_tensor("psQ", [128, 1024], F32))
        psS = ctx.enter_context(nc.psum_tensor("psS", [128, 1024], F32))
        psP = ctx.enter_context(nc.psum_tensor("psP", [128, 1024], F32))
        psY = ctx.enter_context(nc.psum_tensor("psY", [128, 512], F32))
        blk = ctx.enter_context(nc.Block())

        def emit_ln(stbuf, c0, src, waits, out_ap, out_src=None, extra_ap_waits=()):
            stt = None
            for chn in range(4):
                stt = p.op("vector", lambda E, chn=chn: E.bn_stats(
                    out=stbuf[:, c0 + chn * 6:c0 + chn * 6 + 6],
                    in_=src[:, chn * 512:(chn + 1) * 512]),
                    waits=(list(waits) if chn == 0 else []))
            agg = p.op("vector", lambda E: E.bn_aggr(
                out=stbuf[:, c0 + 24:c0 + 26], in_=stbuf[:, c0:c0 + 24]), waits=[stt])
            l_ = p.op("scalar", lambda E: E.activation(
                out=stbuf[:, c0 + 26:c0 + 27], in_=stbuf[:, c0 + 25:c0 + 26],
                func=AF.Ln, bias=LN_EPS), waits=[agg])
            e_ = p.op("scalar", lambda E: E.activation(
                out=stbuf[:, c0 + 27:c0 + 28], in_=stbuf[:, c0 + 26:c0 + 27],
                func=AF.Exp, scale=-0.5), waits=[l_])
            nm_ = p.op("vector", lambda E: E.scalar_tensor_tensor(
                out=stbuf[:, c0 + 28:c0 + 29], in0=stbuf[:, c0 + 24:c0 + 25],
                scalar=-1.0, in1=stbuf[:, c0 + 27:c0 + 28],
                op0=ALU.mult, op1=ALU.mult), waits=[e_])
            src2 = out_src if out_src is not None else src
            ap_ = p.op("vector", lambda E: E.tensor_scalar(
                out_ap, src2, stbuf[:, c0 + 27:c0 + 28], stbuf[:, c0 + 28:c0 + 29],
                ALU.mult, ALU.add), waits=[nm_] + list(extra_ap_waits))
            return ap_

        CONSTS = (LN_EPS, RMS_EPS, QBIAS, -0.5, 1.0 / 128.0, EXPBIAS)
        cst = None
        for k, v in enumerate(CONSTS):
            nc.const_aps.register(F32, v, constv[:, k:k + 1])
            cst = p.op("vector", lambda E, k=k, v=v: E.memset(constv[:, k:k + 1], v))
        for e in ENGINES:
            p.wait(e, cst)

        p.dma("sync", "q_in", ident[:], dident)
        p.dma("sync", "q_in", onesr[:], dones)
        p.dma("sync", "q_in", b1c[:], db1c)
        p.dma("sync", "q_in", b2c[:], db2c)
        tok_id = p.token("q_in")
        p.dma("sync", "q_in", ropeC[:], dropeC)
        p.dma("sync", "q_in", ropeS[:], dropeS)
        p.dma("sync", "q_in", masks[:], dmask)
        tok_consts = p.token("q_in")

        ln1Tv = ln1T[:].rearrange("p (kc t) -> p kc t", t=TK)
        xv = dx.rearrange("(i p) d -> i p d", p=128)
        xcv = dxc.rearrange("(i p) d -> i p d", p=128)

        ldtok = [None] * NTK
        evtok = [None] * NTK

        def a0_load(jk):
            src = xcv[jk] if jk < 4 else xv[jk - 4]
            ldtok[jk] = p.dma("sync", "q_in",
                              a0x[:, (jk % 2) * D:(jk % 2) * D + D], src,
                              waits=[evtok[jk - 2]] if jk >= 2 else [])
        a0_load(0)
        a0_load(1)
        prev_tr2 = None
        for jk in range(NTK):
            xt = a0x[:, (jk % 2) * D:(jk % 2) * D + D]
            c0 = (jk % 2) * 30
            aptok = emit_ln(a0st, c0, xt, [ldtok[jk]], a0ln[:],
                            extra_ap_waits=[prev_tr2])
            tr1 = tr2 = None
            for j in range(KC):
                pst = psS if j < 8 else psP
                tr = p.op("tensor", lambda E, j=j, pst=pst: E.matmul(
                    pst[:, (j % 8) * 128:(j % 8) * 128 + 128],
                    lhsT=a0ln[:, j * 128:(j + 1) * 128], rhs=ident[:],
                    start=True, stop=True),
                    waits=([aptok, tok_id, evtok[jk - 1] if jk >= 1 else None]
                           if j == 0 else []))
                if j == 7:
                    tr1 = tr
                if j == 15:
                    tr2 = tr
            prev_tr2 = tr2
            e1 = p.op("vector", lambda E, jk=jk: E.tensor_copy(
                out=ln1Tv[:, 0:8, jk * 128:(jk + 1) * 128],
                in_=psS[:].rearrange("p (kc c) -> p kc c", c=128)), waits=[tr1])
            evtok[jk] = p.op("vector", lambda E, jk=jk: E.tensor_copy(
                out=ln1Tv[:, 8:16, jk * 128:(jk + 1) * 128],
                in_=psP[:].rearrange("p (kc c) -> p kc c", c=128)), waits=[tr2, e1])
            if jk + 2 < NTK:
                a0_load(jk + 2)
            if A0_SERIAL:
                p.barrier()
        p.barrier()
        if debug:
            p.dma("sync", "q_out", dln1, ln1T[:])
            p.barrier()
        if phases == "A0":
            return _finish(nc, p, blk)

        wTqv = wTq[:].rearrange("p (kc f) -> p kc f", f=512)
        wTkvv = wTkv[:].rearrange("p (kc f) -> p kc f", f=256)
        qnfv = qnf[:].rearrange("p (hh t) -> p hh t", t=T)
        maskv = masks[:].rearrange("p (i c) -> p i c", c=640)
        wqkvTv = dwqkvT.rearrange("(kc p) f -> p kc f", p=128)

        for g in range(G):
            wtoks = [
                p.dma("gpsimd", "q_wt", wTqv[:, 0:8, :],
                      wqkvTv[:, 0:8, 512 * g:512 * g + 512]),
                p.dma("gpsimd", "q_wt", wTqv[:, 8:16, :],
                      wqkvTv[:, 8:16, 512 * g:512 * g + 512]),
                p.dma("gpsimd", "q_wt", wTkvv[:, :, 0:128],
                      wqkvTv[:, :, 2048 + 128 * g:2048 + 128 * g + 128]),
                p.dma("gpsimd", "q_wt", wTkvv[:, :, 128:256],
                      wqkvTv[:, :, 2560 + 128 * g:2560 + 128 * g + 128]),
            ]
            wtok = wtoks[-1]

            kn_done = [None] * NTK
            qn_done = [None] * NT
            stg_free = [None, None]
            sq_free = [None, None]
            prev_cpk = None
            prev_cpq = None
            prev_kev = None
            prev_qev = None
            prev_trk = None
            prev_trq = None
            for jk in range(NTK):
                half = jk % 2
                stgk = stg[:, half * 512:half * 512 + 256]
                mmk = None
                for kcc in range(KC):
                    mmk = p.op("tensor", lambda E, kcc=kcc, jk=jk: E.matmul(
                        psQ[:, 512:768],
                        lhsT=ln1Tv[:, kcc, jk * 128:(jk + 1) * 128],
                        rhs=wTkvv[:, kcc, :],
                        start=(kcc == 0), stop=(kcc == KC - 1)),
                        waits=([wtok, prev_cpk] if kcc == 0 else []))
                cpk = p.op("vector", lambda E, stgk=stgk: E.tensor_copy(
                    out=stgk, in_=psQ[:, 512:768]),
                    waits=[mmk, stg_free[half], sq_free[half]])
                prev_cpk = cpk
                sqk = p.op("scalar", lambda E, stgk=stgk, half=half: E.activation(
                    out=sqb[:, 512 + half * 128:512 + half * 128 + 128],
                    in_=stgk[:, 0:128], func=AF.Square), waits=[cpk])
                red = p.op("vector", lambda E, half=half: E.reduce_sum(
                    out=rr[:, half:half + 1],
                    in_=sqb[:, 512 + half * 128:512 + half * 128 + 128],
                    axis=AX.X), waits=[sqk])
                ln_ = p.op("scalar", lambda E, half=half: E.activation(
                    out=rr[:, 2 + half:3 + half], in_=rr[:, half:half + 1],
                    func=AF.Ln, scale=1.0 / 128.0, bias=RMS_EPS), waits=[red])
                exr = p.op("scalar", lambda E, half=half: E.activation(
                    out=rr[:, 4 + half:5 + half], in_=rr[:, 2 + half:3 + half],
                    func=AF.Exp, scale=-0.5), waits=[ln_])
                rK = rr[:, 4 + half:5 + half]
                cs = ropeC[:, jk * 128:(jk + 1) * 128]
                ss = ropeS[:, jk * 128:(jk + 1) * 128]
                t1 = ropet[:, 0:128]
                t2 = ropet[:, 128:256]
                ktm = ntm[:, 0:128]
                a_ = p.op("vector", lambda E, stgk=stgk, t1=t1, cs=cs, rK=rK:
                          E.scalar_tensor_tensor(out=t1, in0=stgk[:, 0:128], scalar=rK,
                                                 in1=cs, op0=ALU.mult, op1=ALU.mult),
                          waits=[exr, tok_consts])
                b_ = p.op("vector", lambda E, stgk=stgk, t2=t2, ss=ss, rK=rK:
                          E.scalar_tensor_tensor(out=_ev(t2), in0=_od(stgk[:, 0:128]),
                                                 scalar=rK, in1=_ev(ss),
                                                 op0=ALU.mult, op1=ALU.mult), waits=[exr])
                c_ = p.op("vector", lambda E, stgk=stgk, t2=t2, ss=ss, rK=rK:
                          E.scalar_tensor_tensor(out=_od(t2), in0=_ev(stgk[:, 0:128]),
                                                 scalar=rK, in1=_od(ss),
                                                 op0=ALU.mult, op1=ALU.mult), waits=[exr])
                d_ = p.op("vector", lambda E, t1=t1, t2=t2, ktm=ktm: E.tensor_add(
                    out=ktm, in0=t1, in1=t2), waits=[a_, b_, c_, prev_trk])
                trk = p.op("tensor", lambda E, ktm=ktm: E.matmul(
                    psP[:, 0:128], lhsT=ktm, rhs=ident[:], start=True, stop=True),
                    waits=[d_, prev_kev])
                prev_trk = trk
                kev = p.op("vector", lambda E, jk=jk: E.tensor_copy(
                    out=knf[:, jk * 128:(jk + 1) * 128], in_=psP[:, 0:128]), waits=[trk])
                prev_kev = kev
                vcp = p.op("vector", lambda E, stgk=stgk, jk=jk: E.tensor_copy(
                    out=vtm[:, jk * 128:(jk + 1) * 128], in_=stgk[:, 128:256]),
                    waits=[cpk])
                kn_done[jk] = vcp
                if jk < 4:
                    stg_free[half] = vcp
                    sq_free[half] = sqk

                if jk >= 4:
                    i = jk - 4
                    qstg = stg[:, half * 512:half * 512 + 512]
                    mmq = None
                    for kcc in range(KC):
                        mmq = p.op("tensor", lambda E, kcc=kcc, jk=jk: E.matmul(
                            psQ[:, 0:512],
                            lhsT=ln1Tv[:, kcc, jk * 128:(jk + 1) * 128],
                            rhs=wTqv[:, kcc, :],
                            start=(kcc == 0), stop=(kcc == KC - 1)),
                            waits=([wtok, prev_cpq] if kcc == 0 else []))
                    cpq = p.op("vector", lambda E, qstg=qstg: E.tensor_copy(
                        out=qstg, in_=psQ[:, 0:512]), waits=[mmq, sqk])
                    prev_cpq = cpq
                    sqq = p.op("scalar", lambda E, qstg=qstg: E.activation(
                        out=sqb[:, 0:512], in_=qstg, func=AF.Square), waits=[cpq])
                    redq = p.op("vector", lambda E: E.reduce_sum(
                        out=rr[:, 8:12], in_=sqb[:, 0:512].rearrange(
                            "p (h c) -> p h c", c=128), axis=AX.X), waits=[sqq])
                    lnq = p.op("scalar", lambda E: E.activation(
                        out=rr[:, 12:16], in_=rr[:, 8:12], func=AF.Ln,
                        scale=1.0 / 128.0, bias=RMS_EPS), waits=[redq])
                    exq = p.op("scalar", lambda E: E.activation(
                        out=rr[:, 16:20], in_=rr[:, 12:16], func=AF.Exp,
                        scale=-0.5, bias=QBIAS), waits=[lnq])
                    if debug and g == 0:
                        p.dma("sync", "q_out", drr[i], rr[:], waits=[exq])
                        p.dma("sync", "q_out", dsqb[i], sqb[:, 0:512], waits=[exq])
                    qev = None
                    for hh in range(4):
                        rQ = rr[:, 16 + hh:17 + hh]
                        qs = qstg[:, hh * 128:hh * 128 + 128]
                        qt1 = ropet[:, 0:128]
                        qt2 = ropet[:, 128:256]
                        qtm = ntm[:, 128:256]
                        a2_ = p.op("vector", lambda E, qs=qs, qt1=qt1, cs=cs, rQ=rQ:
                                   E.scalar_tensor_tensor(out=qt1, in0=qs, scalar=rQ,
                                                          in1=cs, op0=ALU.mult,
                                                          op1=ALU.mult),
                                   waits=[exq, d_])
                        b2_ = p.op("vector", lambda E, qs=qs, qt2=qt2, ss=ss, rQ=rQ:
                                   E.scalar_tensor_tensor(out=_ev(qt2), in0=_od(qs),
                                                          scalar=rQ, in1=_ev(ss),
                                                          op0=ALU.mult, op1=ALU.mult),
                                   waits=[exq])
                        c2_ = p.op("vector", lambda E, qs=qs, qt2=qt2, ss=ss, rQ=rQ:
                                   E.scalar_tensor_tensor(out=_od(qt2), in0=_ev(qs),
                                                          scalar=rQ, in1=_od(ss),
                                                          op0=ALU.mult, op1=ALU.mult),
                                   waits=[exq])
                        d2_ = p.op("vector", lambda E, qt1=qt1, qt2=qt2, qtm=qtm:
                                   E.tensor_add(out=qtm, in0=qt1, in1=qt2),
                                   waits=[a2_, b2_, c2_, prev_trq])
                        trq = p.op("tensor", lambda E, qtm=qtm: E.matmul(
                            psP[:, 128:256], lhsT=qtm, rhs=ident[:], start=True,
                            stop=True), waits=[d2_, prev_qev])
                        prev_trq = trq
                        qev = p.op("vector", lambda E, hh=hh, i=i: E.tensor_copy(
                            out=qnfv[:, hh, i * 128:(i + 1) * 128], in_=psP[:, 128:256]),
                            waits=[trq])
                        prev_qev = qev
                    qn_done[i] = qev
                    stg_free[half] = qev
                    sq_free[half] = sqq
            if debug:
                p.barrier()
                p.dma("sync", "q_out", dqn[g], qnf[:])
                p.dma("sync", "q_out", dkn[g], knf[:])
                p.dma("sync", "q_out", dvt[g], vtm[:])
                p.barrier()

            diagrv = ntm[:].rearrange("p (a c) -> p a c", c=128)
            last_exp = [None, None]
            prev_trp = [qn_done[NT - 1], qn_done[NT - 1]]
            prev_ptev = prev_qev
            last_rc = [None, None]
            last_dg = [None, None]
            last_mv = [None, None]
            last_ye = [None, None]
            last_ydma = [None, None]
            dg_tok = {}

            def att_head(t):
                hh, i = divmod(t, NT)
                ib = t % 2
                pS = psS if ib == 0 else psQ
                mm1 = p.op("tensor", lambda E, hh=hh, i=i, pS=pS: E.matmul(
                    pS[:, 0:512], lhsT=qnfv[:, hh, i * 128:(i + 1) * 128],
                    rhs=knf[:, i * 128:i * 128 + 512], start=True, stop=True),
                    waits=[qn_done[i], kn_done[min(i + 4, NTK - 1)], last_exp[ib]])
                mm2 = p.op("tensor", lambda E, hh=hh, i=i, pS=pS: E.matmul(
                    pS[:, 512:640], lhsT=qnfv[:, hh, i * 128:(i + 1) * 128],
                    rhs=knf[:, i * 128 + 512:i * 128 + 640], start=True, stop=True))
                pb = pbuf[:, ib * 640:ib * 640 + 640]
                ex = p.op("scalar", lambda E, pb=pb, pS=pS: E.activation(
                    out=pb, in_=pS[:, 0:640], func=AF.Exp, bias=EXPBIAS),
                    waits=[mm1, mm2, prev_trp[ib]])
                last_exp[ib] = ex
                mi = min(i, 4)
                stt = p.op("vector", lambda E, pb=pb, mi=mi, ib=ib:
                           E.scalar_tensor_tensor(
                               out=pb, in0=pb, scalar=1.0, in1=maskv[:, mi, :],
                               op0=ALU.mult, op1=ALU.mult,
                               accum_out=attst[:, ib * 4 + 1:ib * 4 + 2]),
                           waits=[ex, last_rc[ib]])
                rc = p.op("vector", lambda E, ib=ib: E.reciprocal(
                    out=attst[:, ib * 4 + 2:ib * 4 + 3],
                    in_=attst[:, ib * 4 + 1:ib * 4 + 2]),
                    waits=[stt, last_dg[ib]])
                last_rc[ib] = rc
                dg = p.op("vector", lambda E, ib=ib: E.tensor_scalar(
                    diagrv[:, 2 + ib, :], ident[:],
                    attst[:, ib * 4 + 2:ib * 4 + 3], None, ALU.mult),
                    waits=[rc, prev_trp[ib]])
                last_dg[ib] = dg
                dg_tok[t] = dg

            def emit_tail(t):
                nonlocal prev_ptev
                hh, i = divmod(t, NT)
                head = 4 * g + hh
                ib = t % 2
                pb = pbuf[:, ib * 640:ib * 640 + 640]
                trp = None
                for j in range(5):
                    trp = p.op("tensor", lambda E, pb=pb, j=j, ib=ib: E.matmul(
                        psP[:, j * 128:(j + 1) * 128],
                        lhsT=pb[:, j * 128:(j + 1) * 128],
                        rhs=diagrv[:, 2 + ib, :],
                        start=True, stop=True),
                        waits=([dg_tok.pop(t), prev_ptev] if j == 0 else []))
                prev_trp[ib] = trp
                pts = ptsb[:, ib * 640:ib * 640 + 640]
                ptev = p.op("vector", lambda E, pts=pts: E.tensor_copy(
                    out=pts, in_=psP[:, 0:640]), waits=[trp, last_mv[ib]])
                prev_ptev = ptev
                mv = None
                for j in range(5):
                    mv = p.op("tensor", lambda E, pts=pts, i=i, j=j: E.matmul(
                        psY[:, (i % 4) * 128:(i % 4) * 128 + 128],
                        lhsT=vtm[:, (i + j) * 128:(i + j) * 128 + 128],
                        rhs=pts[:, j * 128:(j + 1) * 128],
                        start=(i % 4 == 0 and j == 0),
                        stop=(i % 4 == 3 and j == 4)),
                        waits=([ptev, last_ye[(i // 4) % 2]] if j == 0 else []))
                last_mv[ib] = mv
                if i % 4 == 3:
                    q4 = i // 4
                    yv_ = yevb[:, (q4 % 2) * 512:(q4 % 2) * 512 + 512]
                    ye = p.op("vector", lambda E, yv_=yv_: E.tensor_copy(
                        out=yv_, in_=psY[:]), waits=[mv, last_ydma[q4 % 2]])
                    last_ye[q4 % 2] = ye
                    last_ydma[q4 % 2] = p.dma(
                        "scalar", "q_out", dscY[head][:, q4 * 512:(q4 + 1) * 512],
                        yv_, waits=[ye])

            for t in range(4 * NT):
                if t >= 2:
                    emit_tail(t - 2)
                att_head(t)
            emit_tail(4 * NT - 2)
            emit_tail(4 * NT - 1)
            p.barrier()

        if phases == "A2":
            return _finish(nc, p, blk)

        wpTv = wpT[:].rearrange("p (h f) -> p h f", f=D)
        dwpTv = dwpT.rearrange("(h p) f -> p h f", p=128)
        wl = None
        for c4 in range(4):
            wl = p.dma("gpsimd", "q_wt", wpTv[:, c4 * 4:(c4 + 1) * 4, :],
                       dwpTv[:, c4 * 4:(c4 + 1) * 4, :])
        wptok = wl
        scYv = dscY.rearrange("h p t -> p h t")
        a3yldv = a3yld[:].rearrange("p (b h c) -> p b h c", b=2, h=HQ)
        xmv = dscXM.rearrange("(i p) d -> i p d", p=128)
        ln2Tv = ln2T[:].rearrange("p (kc t) -> p kc t", t=T)
        lastxr = [None, None]
        prev_add = None
        prev_xmdma = None
        prev_le2 = None
        yld_dma = [None, None]
        evs_prev = [None] * 4
        for i in range(NT):
            ib2 = (i // 2) % 2
            if i % 2 == 0:
                yld_dma[ib2] = p.dma(
                    "sync", "q_in", a3yldv[:, ib2, :, :],
                    scYv[:, :, i * 128:i * 128 + 256],
                    waits=[prev_add] if i >= 4 else [])
            xrt = p.dma("sync", "q_in", a3xr[:, (i % 2) * D:(i % 2) * D + D],
                        xv[i], waits=[lastxr[i % 2]])
            chunk_mm = []
            for oc in range(4):
                pst = psS if oc < 2 else psP
                cols = (oc % 2) * 512
                mm = None
                for h2 in range(HQ):
                    mm = p.op("tensor", lambda E, h2=h2, oc=oc, i=i, pst=pst, cols=cols:
                              E.matmul(pst[:, cols:cols + 512],
                                       lhsT=a3yldv[:, (i // 2) % 2, h2,
                                                   (i % 2) * 128:(i % 2) * 128 + 128],
                                       rhs=wpTv[:, h2, oc * 512:(oc + 1) * 512],
                                       start=(h2 == 0), stop=(h2 == HQ - 1)),
                              waits=([yld_dma[(i // 2) % 2], wptok, evs_prev[oc]]
                                     if h2 == 0 else []))
                chunk_mm.append(mm)
            evs = []
            for oc in range(4):
                pst = psS if oc < 2 else psP
                cols = (oc % 2) * 512
                evs.append(p.op("vector", lambda E, oc=oc, pst=pst, cols=cols:
                                E.tensor_copy(out=a3y2[:, oc * 512:(oc + 1) * 512],
                                              in_=pst[:, cols:cols + 512]),
                                waits=[chunk_mm[oc]]))
                evs_prev[oc] = evs[oc]
            ap_ = emit_ln(a3st, (i % 2) * 30, a3y2[:], evs, a3y2[:])
            xr = a3xr[:, (i % 2) * D:(i % 2) * D + D]
            add = p.op("vector", lambda E, xr=xr: E.tensor_add(
                out=a3xm[:], in0=xr, in1=a3y2[:]),
                waits=[ap_, xrt, prev_xmdma])
            prev_add = add
            lastxr[i % 2] = add
            xmdma = p.dma("scalar", "q_out", xmv[i], a3xm[:], waits=[add])
            prev_xmdma = xmdma
            ap2 = emit_ln(a3st, 60 + (i % 2) * 30, a3xm[:], [add], a3ln2[:],
                          extra_ap_waits=[prev_le2])
            le1 = prev_le2
            for r in range(2):
                tr = None
                for j in range(8):
                    tr = p.op("tensor", lambda E, r=r, j=j: E.matmul(
                        psQ[:, j * 128:(j + 1) * 128],
                        lhsT=a3ln2[:, (r * 8 + j) * 128:(r * 8 + j + 1) * 128],
                        rhs=ident[:], start=True, stop=True),
                        waits=([ap2, le1] if j == 0 else []))
                le1 = p.op("vector", lambda E, i=i, r=r: E.tensor_copy(
                    out=ln2Tv[:, r * 8:r * 8 + 8, i * 128:(i + 1) * 128],
                    in_=psQ[:].rearrange("p (kc c) -> p kc c", c=128)), waits=[tr])
            prev_le2 = le1
        p.barrier()
        if debug:
            p.dma("sync", "q_out", dln2, ln2T[:])
            p.barrier()
        if phases == "A3":
            return _finish(nc, p, blk)

        w1Tv = dw1T.rearrange("(kc p) f -> p kc f", p=128)
        wT1v = wT1[:].rearrange("p (b kc c) -> p b kc c", b=2, c=128)
        h1v = h1[:].rearrange("p (m t) -> p m t", t=T)
        wtk = [None, None]
        rel_tok = [None, None]
        prev_rel_chunk = [None] * 4
        for m in range(32):
            mb = m % 2
            wtk[mb] = p.dma("gpsimd", "q_wt", wT1v[:, mb, :, :],
                            w1Tv[:, :, m * 128:(m + 1) * 128],
                            waits=[rel_tok[mb]])
            rel = None
            for chn in range(4):
                pst = psS if chn < 2 else psP
                cols = (chn % 2) * 512
                mm = None
                for kcc in range(KC):
                    mm = p.op("tensor", lambda E, kcc=kcc, chn=chn, pst=pst, cols=cols,
                              mb=mb: E.matmul(
                                  pst[:, cols:cols + 512],
                                  lhsT=wT1v[:, mb, kcc, :],
                                  rhs=ln2Tv[:, kcc, chn * 512:(chn + 1) * 512],
                                  start=(kcc == 0), stop=(kcc == KC - 1)),
                              waits=([wtk[mb], prev_rel_chunk[chn]] if kcc == 0 else []))
                rel = p.op("scalar", lambda E, m=m, chn=chn, pst=pst, cols=cols:
                           E.activation(out=h1v[:, m, chn * 512:(chn + 1) * 512],
                                        in_=pst[:, cols:cols + 512],
                                        func=AF.Relu, bias=b1c[:, m:m + 1]),
                           waits=[mm, tok_id])
                prev_rel_chunk[chn] = rel
            rel_tok[mb] = rel
        p.barrier()
        if debug:
            p.dma("sync", "q_out", dh1d, h1[:])
            p.barrier()
        if phases == "B1":
            return _finish(nc, p, blk)

        w2Tv = dw2T.rearrange("(kc p) f -> p kc f", p=128)
        wT2v = wT2[:].rearrange("p (b kc c) -> p b kc c", b=2, c=128)
        wtk = [None, None]
        rel_tok = [None, None]
        h2dma = [None, None]
        prev_rel_chunk = [None] * 4
        for m in range(32):
            mb = m % 2
            p.dma("gpsimd", "q_wt", wT2v[:, mb, 0:16, :],
                  w2Tv[:, 0:16, m * 128:(m + 1) * 128], waits=[rel_tok[mb]])
            wtk[mb] = p.dma("gpsimd", "q_wt", wT2v[:, mb, 16:32, :],
                            w2Tv[:, 16:32, m * 128:(m + 1) * 128])
            rel = None
            for chn in range(4):
                pst = psS if chn < 2 else psP
                cols = (chn % 2) * 512
                mm = None
                for kcc in range(32):
                    mm = p.op("tensor", lambda E, kcc=kcc, chn=chn, pst=pst, cols=cols,
                              mb=mb: E.matmul(
                                  pst[:, cols:cols + 512],
                                  lhsT=wT2v[:, mb, kcc, :],
                                  rhs=h1v[:, kcc, chn * 512:(chn + 1) * 512],
                                  start=(kcc == 0), stop=(kcc == 31)),
                              waits=([wtk[mb], prev_rel_chunk[chn]] if kcc == 0 else []))
                rel = p.op("scalar", lambda E, chn=chn, pst=pst, cols=cols, mb=mb, m=m:
                           E.activation(out=h2b[:, mb * D + chn * 512:mb * D + (chn + 1) * 512],
                                        in_=pst[:, cols:cols + 512],
                                        func=AF.Relu, bias=b2c[:, m:m + 1]),
                           waits=([mm, h2dma[mb]] if chn == 0 else [mm]))
                prev_rel_chunk[chn] = rel
            rel_tok[mb] = rel
            h2dma[mb] = p.dma("scalar", "q_out", dscH2[m], h2b[:, mb * D:mb * D + D],
                              waits=[rel])
        p.barrier()
        if phases == "B2":
            return _finish(nc, p, blk)

        w3Tv = dw3T.rearrange("(kc p) f -> p kc f", p=128)
        w3sv = w3Ts[:].rearrange("p (kc f) -> p kc f", f=T)
        w3tok_half = None
        w3tok = None
        for c8 in range(4):
            w3tok = p.dma("gpsimd", "q_wt", w3sv[:, c8 * 8:(c8 + 1) * 8, :],
                          w3Tv[:, c8 * 8:(c8 + 1) * 8, :])
            if c8 == 1:
                w3tok_half = w3tok
        b3ld = p.dma("sync", "q_in", b3r[:], db3r)
        btoks = []
        for oc in range(4):
            mmb = p.op("tensor", lambda E, oc=oc: E.matmul(
                psQ[:, (oc % 2) * 512:(oc % 2) * 512 + 512], lhsT=onesr[:],
                rhs=b3r[:, oc * 512:(oc + 1) * 512], start=True, stop=True),
                waits=[b3ld, tok_id, btoks[oc - 2] if oc >= 2 else None])
            btoks.append(p.op("vector", lambda E, oc=oc: E.tensor_copy(
                out=b3bc[:, oc * 512:(oc + 1) * 512],
                in_=psQ[:, (oc % 2) * 512:(oc % 2) * 512 + 512]), waits=[mmb]))
        b3tok = btoks[-1]
        scH2v = dscH2.rearrange("m p t -> p m t")
        b3h2v = b3h2[:].rearrange("p (b m c) -> p b m c", b=2, m=32)
        h2ld = [None, None]
        lastout = None
        prev_outdma = None
        pair_mm = [None, None]
        evs3_prev = [None] * 4
        for i in range(NT):
            ib2 = (i // 2) % 2
            if i % 2 == 0:
                p.dma("sync", "q_in", b3h2v[:, ib2, 0:16, :],
                      scH2v[:, 0:16, i * 128:i * 128 + 256],
                      waits=[pair_mm[ib2]])
                h2ld[ib2] = p.dma("sync", "q_in", b3h2v[:, ib2, 16:32, :],
                                  scH2v[:, 16:32, i * 128:i * 128 + 256])
            xmt = p.dma("sync", "q_in", b3xm[:], xmv[i], waits=[lastout])
            evs = []
            mm = None
            for oc in range(4):
                pst = psS if oc < 2 else psP
                cols = (oc % 2) * 512
                for kcc in range(32):
                    w = []
                    if kcc == 0:
                        w = [h2ld[ib2], w3tok_half, evs3_prev[oc]]
                    elif kcc == 16:
                        w = [w3tok]
                    mm = p.op("tensor", lambda E, kcc=kcc, oc=oc, i=i, pst=pst,
                              cols=cols, ib2=ib2: E.matmul(
                                  pst[:, cols:cols + 512],
                                  lhsT=b3h2v[:, ib2, kcc,
                                             (i % 2) * 128:(i % 2) * 128 + 128],
                                  rhs=w3sv[:, kcc, oc * 512:(oc + 1) * 512],
                                  start=(kcc == 0), stop=(kcc == 31)),
                              waits=w)
                evs.append(p.op("vector", lambda E, oc=oc, pst=pst, cols=cols:
                                E.tensor_add(out=b3y3[:, oc * 512:(oc + 1) * 512],
                                             in0=pst[:, cols:cols + 512],
                                             in1=b3bc[:, oc * 512:(oc + 1) * 512]),
                                waits=[mm, b3tok]))
                evs3_prev[oc] = evs[oc]
            pair_mm[ib2] = mm
            ap_ = emit_ln(b3st, (i % 2) * 30, b3y3[:], evs, b3y3[:])
            lastout = p.op("vector", lambda E: E.tensor_add(
                out=b3ob[:], in0=b3xm[:], in1=b3y3[:]),
                waits=[ap_, xmt, prev_outdma])
            prev_outdma = p.dma("scalar", "q_out",
                                dy.rearrange("(i p) d -> i p d", p=128)[i], b3ob[:],
                                waits=[lastout])
        p.barrier()
        return _finish(nc, p, blk)


def _finish(nc, p, blk):
    p.barrier()
    for ename in ENGINES:
        stream = p.stream[ename]

        def runner(E, stream=stream):
            for fn in stream:
                fn(E)
        getattr(blk, ename)(runner)
    return nc


def host_inputs(full_inputs):
    bf = ml_dtypes.bfloat16
    x = np.ascontiguousarray(full_inputs["L0_attn_x"], dtype=np.float32)
    wqkvT = np.ascontiguousarray(np.asarray(full_inputs["L0_attn_w_qkv"], np.float32).T)
    wpT = np.ascontiguousarray(np.asarray(full_inputs["L0_attn_w_proj"], np.float32).T)
    w1T = np.ascontiguousarray(np.asarray(full_inputs["L0_ffn_W1"], np.float32).T)
    w2T = np.ascontiguousarray(np.asarray(full_inputs["L0_ffn_W2"], np.float32).T)
    w3T = np.ascontiguousarray(np.asarray(full_inputs["L0_ffn_W3"], np.float32).T)
    b1c = np.ascontiguousarray(np.asarray(full_inputs["L0_ffn_b1"], np.float32).reshape(32, 128).T)
    b2c = np.ascontiguousarray(np.asarray(full_inputs["L0_ffn_b2"], np.float32).reshape(32, 128).T)
    b3r = np.ascontiguousarray(np.asarray(full_inputs["L0_ffn_b3"], np.float32).reshape(1, D))
    identt = np.eye(128, dtype=bf)
    ones_row = np.ones((1, 128), dtype=np.float32)

    inv = 1.0 / 10000.0 ** (np.arange(0, DH, 2, dtype=np.float64) / DH)
    rope = {}
    maskh = {}
    for h in range(2):
        start = h * T
        pos = start - 512 + np.arange(TK, dtype=np.float64)
        fr = pos[:, None] * inv[None, :]
        cos = np.repeat(np.cos(fr), 2, axis=-1)
        sin = np.repeat(np.sin(fr), 2, axis=-1)
        sgn = np.ones(DH)
        sgn[0::2] = -1.0
        ssgn = sin * sgn[None, :]
        rope[h] = (
            np.ascontiguousarray(cos.reshape(NTK, 128, DH).transpose(1, 0, 2)
                                 .reshape(128, NTK * DH).astype(np.float32)),
            np.ascontiguousarray(ssgn.reshape(NTK, 128, DH).transpose(1, 0, 2)
                                 .reshape(128, NTK * DH).astype(np.float32)))
        r = np.arange(128)[:, None]
        c = np.arange(640)[None, :]
        mk = np.zeros((128, 5 * 640), dtype=bf)
        for m in range(5):
            valid = (c > r) & (c <= r + 512)
            if h == 0 and m < 4:
                valid = valid & (m * 128 + c >= 512)
            mk[:, m * 640:(m + 1) * 640] = valid.astype(bf)
        maskh[h] = mk

    in_maps = []
    for core in range(8):
        b, h = core // 2, core % 2
        x_own = np.ascontiguousarray(x[b, h * T:(h + 1) * T])
        x_ctx = (np.ascontiguousarray(x[b, T - 512:T]) if h == 1
                 else np.zeros((512, D), np.float32))
        in_maps.append({
            "x_own": x_own, "x_ctx": x_ctx,
            "wqkvT": wqkvT, "wpT": wpT, "w1T": w1T, "w2T": w2T, "w3T": w3T,
            "b1c": b1c, "b2c": b2c, "b3r": b3r,
            "masks": maskh[h], "ropeC": rope[h][0], "ropeS": rope[h][1],
            "ident": identt, "ones_row": ones_row,
        })
    return in_maps


_CACHE = {}


def _get(debug=False, phases="full"):
    key = (debug, phases)
    if key not in _CACHE:
        _CACHE[key] = build(debug, phases)
    return _CACHE[key]


def kernel(**inputs):
    nc = _get(False, "full")
    in_maps = host_inputs(inputs)
    res = run_bass_kernel_spmd(nc, in_maps, core_ids=list(range(8)))
    out = np.empty((4, 4096, D), dtype=np.float32)
    for core in range(8):
        b, h = core // 2, core % 2
        out[b, h * T:(h + 1) * T] = res.results[core]["y"]
    return out


# version 07
# speedup vs baseline: 1.0161x; 1.0161x over previous
import numpy as np
import ml_dtypes
from contextlib import ExitStack

import concourse.bass as bass
import concourse.mybir as mybir
from concourse.bass_utils import run_bass_kernel_spmd

F32 = mybir.dt.float32
BF16 = mybir.dt.bfloat16
AF = mybir.ActivationFunctionType
ALU = mybir.AluOpType
AX = mybir.AxisListType

T = 2048
TK = 2560
D = 2048
DH = 128
HQ = 16
G = 4
RMS_EPS = 1.1920929e-07
LN_EPS = 1e-5
NT = 16
NTK = 20
KC = 16
QBIAS = float(-0.5 * np.log(128.0))
EXPBIAS = -16.0

ENGINES = ("tensor", "vector", "scalar", "gpsimd", "sync")


class Prog:
    def __init__(self):
        self.stream = {e: [] for e in ENGINES}
        self.sem = {}
        self.count = {}
        self.lastwait = {}

    def add_sem(self, name, handle):
        self.sem[name] = handle
        self.count[name] = 0

    def wait(self, eng, token):
        if token is None:
            return
        name, val = token
        if val <= 0:
            return
        if self.lastwait.get((eng, name), 0) >= val:
            return
        self.lastwait[(eng, name)] = val
        h = self.sem[name]
        self.stream[eng].append(lambda E, h=h, v=val: E.wait_ge(h, v))

    def op(self, eng, fn, waits=(), inc="default", amt=1):
        for t in waits:
            self.wait(eng, t)
        if inc == "default":
            inc = eng if eng != "sync" else None
        tok = None
        if inc is not None:
            self.count[inc] += amt
            tok = (inc, self.count[inc])
            h = self.sem[inc]
            self.stream[eng].append(
                lambda E, fn=fn, h=h, a=amt: fn(E).then_inc(h, a))
        else:
            self.stream[eng].append(lambda E, fn=fn: fn(E))
        return tok

    def dma(self, eng, qname, out, in_, waits=()):
        return self.op(eng, lambda E: E.dma_start(out=out, in_=in_),
                       waits=waits, inc=qname, amt=16)

    def barrier(self):
        snapshot = [(n, c) for n, c in self.count.items() if c > 0]
        for e in ENGINES:
            for n, c in snapshot:
                self.wait(e, (n, c))

    def token(self, name):
        return (name, self.count[name])


class Arena:

    def __init__(self, nc, size_bytes):
        self.nc = nc
        self.size = size_bytes
        self.pend = []

    def alloc(self, name, shape, dtype, off):
        per_part = int(np.prod(shape[1:])) * mybir.dt.size(dtype)
        assert off % 32 == 0, (name, off)
        assert off + per_part <= self.size, (name, off, per_part)
        hnd = self.nc.virtual_sbuf_tensor(name, shape, dtype)
        self.pend.append((hnd, off))
        return hnd

    def materialize(self, ctx):
        handle = ctx.enter_context(
            self.nc.sbuf_tensor("arena", [128, self.size], mybir.dt.uint8))
        base = self.nc.lookup_mloc(handle).addr
        for hnd, off in self.pend:
            self.nc._tensor(hnd.name, hnd.shape, hnd.dtype, type="SB")
            mloc = self.nc.lookup_mloc(hnd)
            mloc.allocated = True
            mloc.addr = base + off


def _ev(ap_):
    return ap_.rearrange("p (a two) -> p a two", two=2)[:, :, 0]


def _od(ap_):
    return ap_.rearrange("p (a two) -> p a two", two=2)[:, :, 1]


ARENA = 206144
A0_SERIAL = False


def build(debug=False, phases="full"):
    nc = bass.Bass("TRN2", target_bir_lowering=False)

    dx = nc.dram_tensor("x_own", [T, D], F32, kind="ExternalInput").ap()
    dxc = nc.dram_tensor("x_ctx", [512, D], F32, kind="ExternalInput").ap()
    dwqkvT = nc.dram_tensor("wqkvT", [D, 3072], F32, kind="ExternalInput").ap()
    dwpT = nc.dram_tensor("wpT", [D, D], F32, kind="ExternalInput").ap()
    dw1T = nc.dram_tensor("w1T", [D, 4096], F32, kind="ExternalInput").ap()
    dw2T = nc.dram_tensor("w2T", [4096, 4096], F32, kind="ExternalInput").ap()
    dw3T = nc.dram_tensor("w3T", [4096, D], F32, kind="ExternalInput").ap()
    db1c = nc.dram_tensor("b1c", [128, 32], F32, kind="ExternalInput").ap()
    db2c = nc.dram_tensor("b2c", [128, 32], F32, kind="ExternalInput").ap()
    db3r = nc.dram_tensor("b3r", [1, D], F32, kind="ExternalInput").ap()
    dmask = nc.dram_tensor("masks", [128, 5 * 640], BF16, kind="ExternalInput").ap()
    dropeC = nc.dram_tensor("ropeC", [128, NTK * DH], F32, kind="ExternalInput").ap()
    dropeS = nc.dram_tensor("ropeS", [128, NTK * DH], F32, kind="ExternalInput").ap()
    dident = nc.dram_tensor("ident", [128, 128], BF16, kind="ExternalInput").ap()
    dones = nc.dram_tensor("ones_row", [1, 128], F32, kind="ExternalInput").ap()

    dy = nc.dram_tensor("y", [T, D], F32, kind="ExternalOutput").ap()
    sck = "ExternalOutput" if debug else "Internal"
    dscY = nc.dram_tensor("scY", [HQ, 128, T], BF16, kind=sck).ap()
    dscXM = nc.dram_tensor("scXM", [T, D], F32, kind=sck).ap()
    dscH2 = nc.dram_tensor("scH2", [32, 128, T], BF16, kind=sck).ap()
    if debug:
        drr = nc.dram_tensor("dbg_rr", [NT, 128, 32], F32, kind="ExternalOutput").ap()
        dsqb = nc.dram_tensor("dbg_sqb", [NT, 128, 512], F32, kind="ExternalOutput").ap()
        dln1 = nc.dram_tensor("dbg_ln1T", [128, KC * TK], BF16, kind="ExternalOutput").ap()
        dqn = nc.dram_tensor("dbg_qn", [G, 128, 4 * T], BF16, kind="ExternalOutput").ap()
        dkn = nc.dram_tensor("dbg_kn", [G, 128, TK], BF16, kind="ExternalOutput").ap()
        dvt = nc.dram_tensor("dbg_vt", [G, 128, TK], BF16, kind="ExternalOutput").ap()
        dln2 = nc.dram_tensor("dbg_ln2T", [128, KC * T], BF16, kind="ExternalOutput").ap()
        dh1d = nc.dram_tensor("dbg_h1", [128, 32 * T], BF16, kind="ExternalOutput").ap()

    p = Prog()
    ar = Arena(nc, ARENA)

    ident = ar.alloc("identt", [128, 128], BF16, 0)
    onesr = ar.alloc("onesrr", [1, 128], F32, 256)
    b1c = ar.alloc("b1cc", [128, 32], F32, 768)
    b2c = ar.alloc("b2cc", [128, 32], F32, 896)
    constv = ar.alloc("constv", [128, 8], F32, 1024)
    P0 = 1088
    ln1T = ar.alloc("ln1T", [128, KC * TK], BF16, P0)
    A2R = P0 + 81920
    cur = A2R

    def nxt(sz):
        nonlocal cur
        off = cur
        cur += sz
        assert cur <= ARENA
        return off
    ropeC = ar.alloc("ropeCs", [128, NTK * DH], F32, nxt(10240))
    ropeS = ar.alloc("ropeSs", [128, NTK * DH], F32, nxt(10240))
    masks = ar.alloc("maskss", [128, 5 * 640], BF16, nxt(6400))
    A0ALIAS = cur
    wTq = ar.alloc("wTq", [128, KC * 512], BF16, nxt(16384))
    wTkv = ar.alloc("wTkv", [128, KC * 256], BF16, nxt(8192))
    stg = ar.alloc("stg", [128, 1024], F32, nxt(4096))
    sqb = ar.alloc("sqb", [128, 1024], F32, nxt(4096))
    rr = ar.alloc("rr", [128, 32], F32, nxt(128))
    ropet = ar.alloc("ropet", [128, 256], F32, nxt(1024))
    ntm = ar.alloc("ntm", [128, 512], BF16, nxt(1024))
    qnf = ar.alloc("qnf", [128, 4 * T], BF16, nxt(16384))
    knf = ar.alloc("knf", [128, TK], BF16, nxt(5120))
    vtm = ar.alloc("vtm", [128, TK], BF16, nxt(5120))
    pbuf = ar.alloc("pbuf", [128, 4 * 640], BF16, nxt(5120))
    ptsb = ar.alloc("ptsb", [128, 2 * 640], BF16, nxt(2560))
    diagb = ar.alloc("diagb", [128, 4 * 128], BF16, nxt(1024))
    yevb = ar.alloc("yevb", [128, 2 * 512], BF16, nxt(2048))
    attst = ar.alloc("attst", [128, 8], F32, nxt(32))
    assert cur <= ARENA
    a0x = ar.alloc("a0x", [128, 2 * D], F32, A0ALIAS)
    a0ln = ar.alloc("a0ln", [128, D], BF16, A0ALIAS + 16384)
    a0st = ar.alloc("a0st", [128, 64], F32, A0ALIAS + 20480)
    wpT = ar.alloc("wpTs", [128, KC * D], BF16, P0)
    A3B = P0 + 65536
    a3yld = ar.alloc("a3yld", [128, 2 * HQ * 256], BF16, A3B)
    a3y2 = ar.alloc("a3y2", [128, D], F32, A3B + 16384)
    a3xr = ar.alloc("a3xr", [128, 2 * D], F32, A3B + 24576)
    a3xm = ar.alloc("a3xm", [128, D], F32, A3B + 40960)
    a3ln2 = ar.alloc("a3ln2", [128, D], BF16, A3B + 49152)
    a3st = ar.alloc("a3st", [128, 128], F32, A3B + 53248)
    LN2T = P0 + 131072
    ln2T = ar.alloc("ln2T", [128, KC * T], BF16, LN2T)
    h1 = ar.alloc("h1s", [128, 32 * T], BF16, P0)
    B1B = LN2T + 65536
    wT1 = ar.alloc("wT1", [128, 2 * KC * 128], BF16, B1B)
    wT2 = ar.alloc("wT2", [128, 2 * 32 * 128], BF16, LN2T)
    h2b = ar.alloc("h2b", [128, 2 * D], BF16, LN2T + 16384)
    w3Ts = ar.alloc("w3Ts", [128, 32 * T], BF16, P0)
    b3h2 = ar.alloc("b3h2", [128, 2 * 32 * 256], BF16, LN2T)
    b3y3 = ar.alloc("b3y3", [128, D], F32, LN2T + 32768)
    b3xm = ar.alloc("b3xm", [128, D], F32, LN2T + 40960)
    b3bc = ar.alloc("b3bc", [128, D], F32, LN2T + 49152)
    b3st = ar.alloc("b3st", [128, 64], F32, LN2T + 57344)
    b3r = ar.alloc("b3rr", [1, D], F32, LN2T + 57600)
    b3ob = ar.alloc("b3ob", [128, D], F32, LN2T + 65792)

    ctx = ExitStack()
    with ctx:
        for s in ("tensor", "vector", "scalar", "gpsimd", "q_in", "q_wt", "q_out"):
            p.add_sem(s, ctx.enter_context(nc.semaphore(s)))
        ar.materialize(ctx)
        psQ = ctx.enter_context(nc.psum_tensor("psQ", [128, 1024], F32))
        psS = ctx.enter_context(nc.psum_tensor("psS", [128, 1024], F32))
        psP = ctx.enter_context(nc.psum_tensor("psP", [128, 1024], F32))
        psY = ctx.enter_context(nc.psum_tensor("psY", [128, 512], F32))
        blk = ctx.enter_context(nc.Block())

        def emit_ln(stbuf, c0, src, waits, out_ap, out_src=None, extra_ap_waits=()):
            stt = None
            for chn in range(4):
                stt = p.op("vector", lambda E, chn=chn: E.bn_stats(
                    out=stbuf[:, c0 + chn * 6:c0 + chn * 6 + 6],
                    in_=src[:, chn * 512:(chn + 1) * 512]),
                    waits=(list(waits) if chn == 0 else []))
            agg = p.op("vector", lambda E: E.bn_aggr(
                out=stbuf[:, c0 + 24:c0 + 26], in_=stbuf[:, c0:c0 + 24]), waits=[stt])
            l_ = p.op("scalar", lambda E: E.activation(
                out=stbuf[:, c0 + 26:c0 + 27], in_=stbuf[:, c0 + 25:c0 + 26],
                func=AF.Ln, bias=LN_EPS), waits=[agg])
            e_ = p.op("scalar", lambda E: E.activation(
                out=stbuf[:, c0 + 27:c0 + 28], in_=stbuf[:, c0 + 26:c0 + 27],
                func=AF.Exp, scale=-0.5), waits=[l_])
            nm_ = p.op("vector", lambda E: E.scalar_tensor_tensor(
                out=stbuf[:, c0 + 28:c0 + 29], in0=stbuf[:, c0 + 24:c0 + 25],
                scalar=-1.0, in1=stbuf[:, c0 + 27:c0 + 28],
                op0=ALU.mult, op1=ALU.mult), waits=[e_])
            src2 = out_src if out_src is not None else src
            ap_ = p.op("vector", lambda E: E.tensor_scalar(
                out_ap, src2, stbuf[:, c0 + 27:c0 + 28], stbuf[:, c0 + 28:c0 + 29],
                ALU.mult, ALU.add), waits=[nm_] + list(extra_ap_waits))
            return ap_

        CONSTS = (LN_EPS, RMS_EPS, QBIAS, -0.5, 1.0 / 128.0, EXPBIAS)
        cst = None
        for k, v in enumerate(CONSTS):
            nc.const_aps.register(F32, v, constv[:, k:k + 1])
            cst = p.op("vector", lambda E, k=k, v=v: E.memset(constv[:, k:k + 1], v))
        for e in ENGINES:
            p.wait(e, cst)

        p.dma("sync", "q_in", ident[:], dident)
        p.dma("sync", "q_in", onesr[:], dones)
        p.dma("sync", "q_in", b1c[:], db1c)
        p.dma("sync", "q_in", b2c[:], db2c)
        tok_id = p.token("q_in")
        p.dma("sync", "q_in", ropeC[:], dropeC)
        p.dma("sync", "q_in", ropeS[:], dropeS)
        p.dma("sync", "q_in", masks[:], dmask)
        tok_consts = p.token("q_in")

        ln1Tv = ln1T[:].rearrange("p (kc t) -> p kc t", t=TK)
        xv = dx.rearrange("(i p) d -> i p d", p=128)
        xcv = dxc.rearrange("(i p) d -> i p d", p=128)

        ldtok = [None] * NTK
        evtok = [None] * NTK

        def a0_load(jk):
            src = xcv[jk] if jk < 4 else xv[jk - 4]
            ldtok[jk] = p.dma("sync", "q_in",
                              a0x[:, (jk % 2) * D:(jk % 2) * D + D], src,
                              waits=[evtok[jk - 2]] if jk >= 2 else [])
        a0_load(0)
        a0_load(1)
        prev_tr2 = None
        for jk in range(NTK):
            xt = a0x[:, (jk % 2) * D:(jk % 2) * D + D]
            c0 = (jk % 2) * 30
            aptok = emit_ln(a0st, c0, xt, [ldtok[jk]], a0ln[:],
                            extra_ap_waits=[prev_tr2])
            tr1 = tr2 = None
            for j in range(KC):
                pst = psS if j < 8 else psP
                tr = p.op("tensor", lambda E, j=j, pst=pst: E.matmul(
                    pst[:, (j % 8) * 128:(j % 8) * 128 + 128],
                    lhsT=a0ln[:, j * 128:(j + 1) * 128], rhs=ident[:],
                    start=True, stop=True),
                    waits=([aptok, tok_id, evtok[jk - 1] if jk >= 1 else None]
                           if j == 0 else []))
                if j == 7:
                    tr1 = tr
                if j == 15:
                    tr2 = tr
            prev_tr2 = tr2
            e1 = p.op("vector", lambda E, jk=jk: E.tensor_copy(
                out=ln1Tv[:, 0:8, jk * 128:(jk + 1) * 128],
                in_=psS[:].rearrange("p (kc c) -> p kc c", c=128)), waits=[tr1])
            evtok[jk] = p.op("vector", lambda E, jk=jk: E.tensor_copy(
                out=ln1Tv[:, 8:16, jk * 128:(jk + 1) * 128],
                in_=psP[:].rearrange("p (kc c) -> p kc c", c=128)), waits=[tr2, e1])
            if jk + 2 < NTK:
                a0_load(jk + 2)
            if A0_SERIAL:
                p.barrier()
        p.barrier()
        if debug:
            p.dma("sync", "q_out", dln1, ln1T[:])
            p.barrier()
        if phases == "A0":
            return _finish(nc, p, blk)

        wTqv = wTq[:].rearrange("p (kc f) -> p kc f", f=512)
        wTkvv = wTkv[:].rearrange("p (kc f) -> p kc f", f=256)
        qnfv = qnf[:].rearrange("p (hh t) -> p hh t", t=T)
        maskv = masks[:].rearrange("p (i c) -> p i c", c=640)
        wqkvTv = dwqkvT.rearrange("(kc p) f -> p kc f", p=128)

        for g in range(G):
            wtoks = [
                p.dma("gpsimd", "q_wt", wTqv[:, 0:8, :],
                      wqkvTv[:, 0:8, 512 * g:512 * g + 512]),
                p.dma("gpsimd", "q_wt", wTqv[:, 8:16, :],
                      wqkvTv[:, 8:16, 512 * g:512 * g + 512]),
                p.dma("gpsimd", "q_wt", wTkvv[:, :, 0:128],
                      wqkvTv[:, :, 2048 + 128 * g:2048 + 128 * g + 128]),
                p.dma("gpsimd", "q_wt", wTkvv[:, :, 128:256],
                      wqkvTv[:, :, 2560 + 128 * g:2560 + 128 * g + 128]),
            ]
            wtok = wtoks[-1]

            kn_done = [None] * NTK
            qn_done = [None] * NT
            stg_free = [None, None]
            sq_free = [None, None]
            prev_cpk = None
            prev_cpq = None
            prev_kev = None
            prev_qev = None
            prev_trk = None
            prev_trq = None
            for jk in range(NTK):
                half = jk % 2
                stgk = stg[:, half * 512:half * 512 + 256]
                mmk = None
                for kcc in range(KC):
                    mmk = p.op("tensor", lambda E, kcc=kcc, jk=jk: E.matmul(
                        psQ[:, 512:768],
                        lhsT=ln1Tv[:, kcc, jk * 128:(jk + 1) * 128],
                        rhs=wTkvv[:, kcc, :],
                        start=(kcc == 0), stop=(kcc == KC - 1)),
                        waits=([wtok, prev_cpk] if kcc == 0 else []))
                cpk = p.op("vector", lambda E, stgk=stgk: E.tensor_copy(
                    out=stgk, in_=psQ[:, 512:768]),
                    waits=[mmk, stg_free[half], sq_free[half]])
                prev_cpk = cpk
                sqk = p.op("scalar", lambda E, stgk=stgk, half=half: E.activation(
                    out=sqb[:, 512 + half * 128:512 + half * 128 + 128],
                    in_=stgk[:, 0:128], func=AF.Square), waits=[cpk])
                red = p.op("vector", lambda E, half=half: E.reduce_sum(
                    out=rr[:, half:half + 1],
                    in_=sqb[:, 512 + half * 128:512 + half * 128 + 128],
                    axis=AX.X), waits=[sqk])
                ln_ = p.op("scalar", lambda E, half=half: E.activation(
                    out=rr[:, 2 + half:3 + half], in_=rr[:, half:half + 1],
                    func=AF.Ln, scale=1.0 / 128.0, bias=RMS_EPS), waits=[red])
                exr = p.op("scalar", lambda E, half=half: E.activation(
                    out=rr[:, 4 + half:5 + half], in_=rr[:, 2 + half:3 + half],
                    func=AF.Exp, scale=-0.5), waits=[ln_])
                rK = rr[:, 4 + half:5 + half]
                cs = ropeC[:, jk * 128:(jk + 1) * 128]
                ss = ropeS[:, jk * 128:(jk + 1) * 128]
                t1 = ropet[:, 0:128]
                t2 = ropet[:, 128:256]
                ktm = ntm[:, 0:128]
                a_ = p.op("vector", lambda E, stgk=stgk, t1=t1, cs=cs, rK=rK:
                          E.scalar_tensor_tensor(out=t1, in0=stgk[:, 0:128], scalar=rK,
                                                 in1=cs, op0=ALU.mult, op1=ALU.mult),
                          waits=[exr, tok_consts])
                b_ = p.op("vector", lambda E, stgk=stgk, t2=t2, ss=ss, rK=rK:
                          E.scalar_tensor_tensor(out=_ev(t2), in0=_od(stgk[:, 0:128]),
                                                 scalar=rK, in1=_ev(ss),
                                                 op0=ALU.mult, op1=ALU.mult), waits=[exr])
                c_ = p.op("vector", lambda E, stgk=stgk, t2=t2, ss=ss, rK=rK:
                          E.scalar_tensor_tensor(out=_od(t2), in0=_ev(stgk[:, 0:128]),
                                                 scalar=rK, in1=_od(ss),
                                                 op0=ALU.mult, op1=ALU.mult), waits=[exr])
                d_ = p.op("vector", lambda E, t1=t1, t2=t2, ktm=ktm: E.tensor_add(
                    out=ktm, in0=t1, in1=t2), waits=[a_, b_, c_, prev_trk])
                trk = p.op("tensor", lambda E, ktm=ktm: E.matmul(
                    psP[:, 0:128], lhsT=ktm, rhs=ident[:], start=True, stop=True),
                    waits=[d_, prev_kev])
                prev_trk = trk
                kev = p.op("vector", lambda E, jk=jk: E.tensor_copy(
                    out=knf[:, jk * 128:(jk + 1) * 128], in_=psP[:, 0:128]), waits=[trk])
                prev_kev = kev
                vcp = p.op("vector", lambda E, stgk=stgk, jk=jk: E.tensor_copy(
                    out=vtm[:, jk * 128:(jk + 1) * 128], in_=stgk[:, 128:256]),
                    waits=[cpk])
                kn_done[jk] = vcp
                if jk < 4:
                    stg_free[half] = vcp
                    sq_free[half] = sqk

                if jk >= 4:
                    i = jk - 4
                    qstg = stg[:, half * 512:half * 512 + 512]
                    mmq = None
                    for kcc in range(KC):
                        mmq = p.op("tensor", lambda E, kcc=kcc, jk=jk: E.matmul(
                            psQ[:, 0:512],
                            lhsT=ln1Tv[:, kcc, jk * 128:(jk + 1) * 128],
                            rhs=wTqv[:, kcc, :],
                            start=(kcc == 0), stop=(kcc == KC - 1)),
                            waits=([wtok, prev_cpq] if kcc == 0 else []))
                    cpq = p.op("vector", lambda E, qstg=qstg: E.tensor_copy(
                        out=qstg, in_=psQ[:, 0:512]), waits=[mmq, sqk])
                    prev_cpq = cpq
                    sqq = p.op("scalar", lambda E, qstg=qstg: E.activation(
                        out=sqb[:, 0:512], in_=qstg, func=AF.Square), waits=[cpq])
                    redq = p.op("vector", lambda E: E.reduce_sum(
                        out=rr[:, 8:12], in_=sqb[:, 0:512].rearrange(
                            "p (h c) -> p h c", c=128), axis=AX.X), waits=[sqq])
                    lnq = p.op("scalar", lambda E: E.activation(
                        out=rr[:, 12:16], in_=rr[:, 8:12], func=AF.Ln,
                        scale=1.0 / 128.0, bias=RMS_EPS), waits=[redq])
                    exq = p.op("scalar", lambda E: E.activation(
                        out=rr[:, 16:20], in_=rr[:, 12:16], func=AF.Exp,
                        scale=-0.5, bias=QBIAS), waits=[lnq])
                    if debug and g == 0:
                        p.dma("sync", "q_out", drr[i], rr[:], waits=[exq])
                        p.dma("sync", "q_out", dsqb[i], sqb[:, 0:512], waits=[exq])
                    qev = None
                    for hh in range(4):
                        rQ = rr[:, 16 + hh:17 + hh]
                        qs = qstg[:, hh * 128:hh * 128 + 128]
                        qt1 = ropet[:, 0:128]
                        qt2 = ropet[:, 128:256]
                        qtm = ntm[:, 128:256]
                        a2_ = p.op("vector", lambda E, qs=qs, qt1=qt1, cs=cs, rQ=rQ:
                                   E.scalar_tensor_tensor(out=qt1, in0=qs, scalar=rQ,
                                                          in1=cs, op0=ALU.mult,
                                                          op1=ALU.mult),
                                   waits=[exq, d_])
                        b2_ = p.op("vector", lambda E, qs=qs, qt2=qt2, ss=ss, rQ=rQ:
                                   E.scalar_tensor_tensor(out=_ev(qt2), in0=_od(qs),
                                                          scalar=rQ, in1=_ev(ss),
                                                          op0=ALU.mult, op1=ALU.mult),
                                   waits=[exq])
                        c2_ = p.op("vector", lambda E, qs=qs, qt2=qt2, ss=ss, rQ=rQ:
                                   E.scalar_tensor_tensor(out=_od(qt2), in0=_ev(qs),
                                                          scalar=rQ, in1=_od(ss),
                                                          op0=ALU.mult, op1=ALU.mult),
                                   waits=[exq])
                        d2_ = p.op("vector", lambda E, qt1=qt1, qt2=qt2, qtm=qtm:
                                   E.tensor_add(out=qtm, in0=qt1, in1=qt2),
                                   waits=[a2_, b2_, c2_, prev_trq])
                        trq = p.op("tensor", lambda E, qtm=qtm: E.matmul(
                            psP[:, 128:256], lhsT=qtm, rhs=ident[:], start=True,
                            stop=True), waits=[d2_, prev_qev])
                        prev_trq = trq
                        qev = p.op("vector", lambda E, hh=hh, i=i: E.tensor_copy(
                            out=qnfv[:, hh, i * 128:(i + 1) * 128], in_=psP[:, 128:256]),
                            waits=[trq])
                        prev_qev = qev
                    qn_done[i] = qev
                    stg_free[half] = qev
                    sq_free[half] = sqq
            if debug:
                p.barrier()
                p.dma("sync", "q_out", dqn[g], qnf[:])
                p.dma("sync", "q_out", dkn[g], knf[:])
                p.dma("sync", "q_out", dvt[g], vtm[:])
                p.barrier()

            diagv = diagb[:].rearrange("p (a c) -> p a c", c=128)
            last_exp = [None, None]
            trp4 = [qn_done[NT - 1]] * 4
            prev_ptev = [prev_qev, prev_qev]
            last_rc = [None, None]
            last_dg = [None, None]
            last_mv = [None, None]
            last_ye = [None, None]
            last_ydma = [None, None]
            dg_tok = {}

            def att_head(t):
                hh, i = divmod(t, NT)
                ib = t % 2
                ib4 = t % 4
                pS = psS if ib == 0 else psQ
                mm1 = p.op("tensor", lambda E, hh=hh, i=i, pS=pS: E.matmul(
                    pS[:, 0:512], lhsT=qnfv[:, hh, i * 128:(i + 1) * 128],
                    rhs=knf[:, i * 128:i * 128 + 512], start=True, stop=True),
                    waits=[qn_done[i], kn_done[min(i + 4, NTK - 1)], last_exp[ib]])
                mm2 = p.op("tensor", lambda E, hh=hh, i=i, pS=pS: E.matmul(
                    pS[:, 512:640], lhsT=qnfv[:, hh, i * 128:(i + 1) * 128],
                    rhs=knf[:, i * 128 + 512:i * 128 + 640], start=True, stop=True))
                pb = pbuf[:, ib4 * 640:ib4 * 640 + 640]
                ex = p.op("scalar", lambda E, pb=pb, pS=pS: E.activation(
                    out=pb, in_=pS[:, 0:640], func=AF.Exp, bias=EXPBIAS),
                    waits=[mm1, mm2, trp4[ib4]])
                last_exp[ib] = ex
                mi = min(i, 4)
                stt = p.op("vector", lambda E, pb=pb, mi=mi, ib=ib:
                           E.scalar_tensor_tensor(
                               out=pb, in0=pb, scalar=1.0, in1=maskv[:, mi, :],
                               op0=ALU.mult, op1=ALU.mult,
                               accum_out=attst[:, ib * 4 + 1:ib * 4 + 2]),
                           waits=[ex, last_rc[ib]])
                rc = p.op("vector", lambda E, ib=ib: E.reciprocal(
                    out=attst[:, ib * 4 + 2:ib * 4 + 3],
                    in_=attst[:, ib * 4 + 1:ib * 4 + 2]),
                    waits=[stt, last_dg[ib]])
                last_rc[ib] = rc
                dg = p.op("scalar", lambda E, ib=ib, ib4=ib4: E.mul(
                    out=diagv[:, ib4, :], in_=ident[:],
                    mul=attst[:, ib * 4 + 2:ib * 4 + 3]),
                    waits=[rc, trp4[ib4]])
                last_dg[ib] = dg
                dg_tok[t] = dg

            def emit_tail(t):
                hh, i = divmod(t, NT)
                head = 4 * g + hh
                ib = t % 2
                ib4 = t % 4
                pb = pbuf[:, ib4 * 640:ib4 * 640 + 640]
                trp = None
                for j in range(5):
                    trp = p.op("tensor", lambda E, pb=pb, j=j, ib4=ib4: E.matmul(
                        psP[:, j * 128:(j + 1) * 128],
                        lhsT=pb[:, j * 128:(j + 1) * 128],
                        rhs=diagv[:, ib4, :],
                        start=True, stop=True),
                        waits=([dg_tok.pop(t)] + prev_ptev if j == 0 else []))
                trp4[ib4] = trp
                pts = ptsb[:, ib * 640:ib * 640 + 640]
                ptev = p.op("vector", lambda E, pts=pts: E.tensor_copy(
                    out=pts, in_=psP[:, 0:640]), waits=[trp, last_mv[ib]])
                prev_ptev[:] = [ptev]
                mv = None
                for j in range(5):
                    mv = p.op("tensor", lambda E, pts=pts, i=i, j=j: E.matmul(
                        psY[:, (i % 4) * 128:(i % 4) * 128 + 128],
                        lhsT=vtm[:, (i + j) * 128:(i + j) * 128 + 128],
                        rhs=pts[:, j * 128:(j + 1) * 128],
                        start=(i % 4 == 0 and j == 0),
                        stop=(i % 4 == 3 and j == 4)),
                        waits=([ptev, last_ye[(i // 4) % 2]]
                               if j == 0 else []))
                last_mv[ib] = mv
                if i % 4 == 3:
                    q4 = i // 4
                    yv_ = yevb[:, (q4 % 2) * 512:(q4 % 2) * 512 + 512]
                    ye = p.op("vector", lambda E, yv_=yv_: E.tensor_copy(
                        out=yv_, in_=psY[:]), waits=[mv, last_ydma[q4 % 2]])
                    last_ye[q4 % 2] = ye
                    last_ydma[q4 % 2] = p.dma(
                        "scalar", "q_out", dscY[head][:, q4 * 512:(q4 + 1) * 512],
                        yv_, waits=[ye])

            for t in range(4 * NT):
                if t >= 2:
                    emit_tail(t - 2)
                att_head(t)
            emit_tail(4 * NT - 2)
            emit_tail(4 * NT - 1)
            p.barrier()

        if phases == "A2":
            return _finish(nc, p, blk)

        wpTv = wpT[:].rearrange("p (h f) -> p h f", f=D)
        dwpTv = dwpT.rearrange("(h p) f -> p h f", p=128)
        wl = None
        for c4 in range(4):
            wl = p.dma("gpsimd", "q_wt", wpTv[:, c4 * 4:(c4 + 1) * 4, :],
                       dwpTv[:, c4 * 4:(c4 + 1) * 4, :])
        wptok = wl
        scYv = dscY.rearrange("h p t -> p h t")
        a3yldv = a3yld[:].rearrange("p (b h c) -> p b h c", b=2, h=HQ)
        xmv = dscXM.rearrange("(i p) d -> i p d", p=128)
        ln2Tv = ln2T[:].rearrange("p (kc t) -> p kc t", t=T)
        lastxr = [None, None]
        prev_add = None
        prev_xmdma = None
        prev_le2 = None
        yld_dma = [None, None]
        evs_prev = [None] * 4
        for i in range(NT):
            ib2 = (i // 2) % 2
            if i % 2 == 0:
                yld_dma[ib2] = p.dma(
                    "sync", "q_in", a3yldv[:, ib2, :, :],
                    scYv[:, :, i * 128:i * 128 + 256],
                    waits=[prev_add] if i >= 4 else [])
            xrt = p.dma("sync", "q_in", a3xr[:, (i % 2) * D:(i % 2) * D + D],
                        xv[i], waits=[lastxr[i % 2]])
            chunk_mm = []
            for oc in range(4):
                pst = psS if oc < 2 else psP
                cols = (oc % 2) * 512
                mm = None
                for h2 in range(HQ):
                    mm = p.op("tensor", lambda E, h2=h2, oc=oc, i=i, pst=pst, cols=cols:
                              E.matmul(pst[:, cols:cols + 512],
                                       lhsT=a3yldv[:, (i // 2) % 2, h2,
                                                   (i % 2) * 128:(i % 2) * 128 + 128],
                                       rhs=wpTv[:, h2, oc * 512:(oc + 1) * 512],
                                       start=(h2 == 0), stop=(h2 == HQ - 1)),
                              waits=([yld_dma[(i // 2) % 2], wptok, evs_prev[oc]]
                                     if h2 == 0 else []))
                chunk_mm.append(mm)
            evs = []
            for oc in range(4):
                pst = psS if oc < 2 else psP
                cols = (oc % 2) * 512
                evs.append(p.op("vector", lambda E, oc=oc, pst=pst, cols=cols:
                                E.tensor_copy(out=a3y2[:, oc * 512:(oc + 1) * 512],
                                              in_=pst[:, cols:cols + 512]),
                                waits=[chunk_mm[oc]]))
                evs_prev[oc] = evs[oc]
            ap_ = emit_ln(a3st, (i % 2) * 30, a3y2[:], evs, a3y2[:])
            xr = a3xr[:, (i % 2) * D:(i % 2) * D + D]
            add = p.op("vector", lambda E, xr=xr: E.tensor_add(
                out=a3xm[:], in0=xr, in1=a3y2[:]),
                waits=[ap_, xrt, prev_xmdma])
            prev_add = add
            lastxr[i % 2] = add
            xmdma = p.dma("scalar", "q_out", xmv[i], a3xm[:], waits=[add])
            prev_xmdma = xmdma
            ap2 = emit_ln(a3st, 60 + (i % 2) * 30, a3xm[:], [add], a3ln2[:],
                          extra_ap_waits=[prev_le2])
            le1 = prev_le2
            for r in range(2):
                tr = None
                for j in range(8):
                    tr = p.op("tensor", lambda E, r=r, j=j: E.matmul(
                        psQ[:, j * 128:(j + 1) * 128],
                        lhsT=a3ln2[:, (r * 8 + j) * 128:(r * 8 + j + 1) * 128],
                        rhs=ident[:], start=True, stop=True),
                        waits=([ap2, le1] if j == 0 else []))
                le1 = p.op("vector", lambda E, i=i, r=r: E.tensor_copy(
                    out=ln2Tv[:, r * 8:r * 8 + 8, i * 128:(i + 1) * 128],
                    in_=psQ[:].rearrange("p (kc c) -> p kc c", c=128)), waits=[tr])
            prev_le2 = le1
        p.barrier()
        if debug:
            p.dma("sync", "q_out", dln2, ln2T[:])
            p.barrier()
        if phases == "A3":
            return _finish(nc, p, blk)

        w1Tv = dw1T.rearrange("(kc p) f -> p kc f", p=128)
        wT1v = wT1[:].rearrange("p (b kc c) -> p b kc c", b=2, c=128)
        h1v = h1[:].rearrange("p (m t) -> p m t", t=T)
        wtk = [None, None]
        rel_tok = [None, None]
        prev_rel_chunk = [None] * 4
        for m in range(32):
            mb = m % 2
            wtk[mb] = p.dma("gpsimd", "q_wt", wT1v[:, mb, :, :],
                            w1Tv[:, :, m * 128:(m + 1) * 128],
                            waits=[rel_tok[mb]])
            rel = None
            for chn in range(4):
                pst = psS if chn < 2 else psP
                cols = (chn % 2) * 512
                mm = None
                for kcc in range(KC):
                    mm = p.op("tensor", lambda E, kcc=kcc, chn=chn, pst=pst, cols=cols,
                              mb=mb: E.matmul(
                                  pst[:, cols:cols + 512],
                                  lhsT=wT1v[:, mb, kcc, :],
                                  rhs=ln2Tv[:, kcc, chn * 512:(chn + 1) * 512],
                                  start=(kcc == 0), stop=(kcc == KC - 1)),
                              waits=([wtk[mb], prev_rel_chunk[chn]] if kcc == 0 else []))
                rel = p.op("scalar", lambda E, m=m, chn=chn, pst=pst, cols=cols:
                           E.activation(out=h1v[:, m, chn * 512:(chn + 1) * 512],
                                        in_=pst[:, cols:cols + 512],
                                        func=AF.Relu, bias=b1c[:, m:m + 1]),
                           waits=[mm, tok_id])
                prev_rel_chunk[chn] = rel
            rel_tok[mb] = rel
        p.barrier()
        if debug:
            p.dma("sync", "q_out", dh1d, h1[:])
            p.barrier()
        if phases == "B1":
            return _finish(nc, p, blk)

        w2Tv = dw2T.rearrange("(kc p) f -> p kc f", p=128)
        wT2v = wT2[:].rearrange("p (b kc c) -> p b kc c", b=2, c=128)
        wtk = [None, None]
        rel_tok = [None, None]
        h2dma = [None, None]
        prev_rel_chunk = [None] * 4
        for m in range(32):
            mb = m % 2
            p.dma("gpsimd", "q_wt", wT2v[:, mb, 0:16, :],
                  w2Tv[:, 0:16, m * 128:(m + 1) * 128], waits=[rel_tok[mb]])
            wtk[mb] = p.dma("gpsimd", "q_wt", wT2v[:, mb, 16:32, :],
                            w2Tv[:, 16:32, m * 128:(m + 1) * 128])
            rel = None
            for chn in range(4):
                pst = psS if chn < 2 else psP
                cols = (chn % 2) * 512
                mm = None
                for kcc in range(32):
                    mm = p.op("tensor", lambda E, kcc=kcc, chn=chn, pst=pst, cols=cols,
                              mb=mb: E.matmul(
                                  pst[:, cols:cols + 512],
                                  lhsT=wT2v[:, mb, kcc, :],
                                  rhs=h1v[:, kcc, chn * 512:(chn + 1) * 512],
                                  start=(kcc == 0), stop=(kcc == 31)),
                              waits=([wtk[mb], prev_rel_chunk[chn]] if kcc == 0 else []))
                rel = p.op("scalar", lambda E, chn=chn, pst=pst, cols=cols, mb=mb, m=m:
                           E.activation(out=h2b[:, mb * D + chn * 512:mb * D + (chn + 1) * 512],
                                        in_=pst[:, cols:cols + 512],
                                        func=AF.Relu, bias=b2c[:, m:m + 1]),
                           waits=([mm, h2dma[mb]] if chn == 0 else [mm]))
                prev_rel_chunk[chn] = rel
            rel_tok[mb] = rel
            h2dma[mb] = p.dma("scalar", "q_out", dscH2[m], h2b[:, mb * D:mb * D + D],
                              waits=[rel])
        p.barrier()
        if phases == "B2":
            return _finish(nc, p, blk)

        w3Tv = dw3T.rearrange("(kc p) f -> p kc f", p=128)
        w3sv = w3Ts[:].rearrange("p (kc f) -> p kc f", f=T)
        w3tok_half = None
        w3tok = None
        for c8 in range(4):
            w3tok = p.dma("gpsimd", "q_wt", w3sv[:, c8 * 8:(c8 + 1) * 8, :],
                          w3Tv[:, c8 * 8:(c8 + 1) * 8, :])
            if c8 == 1:
                w3tok_half = w3tok
        b3ld = p.dma("sync", "q_in", b3r[:], db3r)
        btoks = []
        for oc in range(4):
            mmb = p.op("tensor", lambda E, oc=oc: E.matmul(
                psQ[:, (oc % 2) * 512:(oc % 2) * 512 + 512], lhsT=onesr[:],
                rhs=b3r[:, oc * 512:(oc + 1) * 512], start=True, stop=True),
                waits=[b3ld, tok_id, btoks[oc - 2] if oc >= 2 else None])
            btoks.append(p.op("vector", lambda E, oc=oc: E.tensor_copy(
                out=b3bc[:, oc * 512:(oc + 1) * 512],
                in_=psQ[:, (oc % 2) * 512:(oc % 2) * 512 + 512]), waits=[mmb]))
        b3tok = btoks[-1]
        scH2v = dscH2.rearrange("m p t -> p m t")
        b3h2v = b3h2[:].rearrange("p (b m c) -> p b m c", b=2, m=32)
        h2ld = [None, None]
        lastout = None
        prev_outdma = None
        pair_mm = [None, None]
        evs3_prev = [None] * 4
        for i in range(NT):
            ib2 = (i // 2) % 2
            if i % 2 == 0:
                p.dma("sync", "q_in", b3h2v[:, ib2, 0:16, :],
                      scH2v[:, 0:16, i * 128:i * 128 + 256],
                      waits=[pair_mm[ib2]])
                h2ld[ib2] = p.dma("sync", "q_in", b3h2v[:, ib2, 16:32, :],
                                  scH2v[:, 16:32, i * 128:i * 128 + 256])
            xmt = p.dma("sync", "q_in", b3xm[:], xmv[i], waits=[lastout])
            evs = []
            mm = None
            for oc in range(4):
                pst = psS if oc < 2 else psP
                cols = (oc % 2) * 512
                for kcc in range(32):
                    w = []
                    if kcc == 0:
                        w = [h2ld[ib2], w3tok_half, evs3_prev[oc]]
                    elif kcc == 16:
                        w = [w3tok]
                    mm = p.op("tensor", lambda E, kcc=kcc, oc=oc, i=i, pst=pst,
                              cols=cols, ib2=ib2: E.matmul(
                                  pst[:, cols:cols + 512],
                                  lhsT=b3h2v[:, ib2, kcc,
                                             (i % 2) * 128:(i % 2) * 128 + 128],
                                  rhs=w3sv[:, kcc, oc * 512:(oc + 1) * 512],
                                  start=(kcc == 0), stop=(kcc == 31)),
                              waits=w)
                evs.append(p.op("vector", lambda E, oc=oc, pst=pst, cols=cols:
                                E.tensor_add(out=b3y3[:, oc * 512:(oc + 1) * 512],
                                             in0=pst[:, cols:cols + 512],
                                             in1=b3bc[:, oc * 512:(oc + 1) * 512]),
                                waits=[mm, b3tok]))
                evs3_prev[oc] = evs[oc]
            pair_mm[ib2] = mm
            ap_ = emit_ln(b3st, (i % 2) * 30, b3y3[:], evs, b3y3[:])
            lastout = p.op("vector", lambda E: E.tensor_add(
                out=b3ob[:], in0=b3xm[:], in1=b3y3[:]),
                waits=[ap_, xmt, prev_outdma])
            prev_outdma = p.dma("scalar", "q_out",
                                dy.rearrange("(i p) d -> i p d", p=128)[i], b3ob[:],
                                waits=[lastout])
        p.barrier()
        return _finish(nc, p, blk)


def _finish(nc, p, blk):
    p.barrier()
    for ename in ENGINES:
        stream = p.stream[ename]

        def runner(E, stream=stream):
            for fn in stream:
                fn(E)
        getattr(blk, ename)(runner)
    return nc


def host_inputs(full_inputs):
    bf = ml_dtypes.bfloat16
    x = np.ascontiguousarray(full_inputs["L0_attn_x"], dtype=np.float32)
    wqkvT = np.ascontiguousarray(np.asarray(full_inputs["L0_attn_w_qkv"], np.float32).T)
    wpT = np.ascontiguousarray(np.asarray(full_inputs["L0_attn_w_proj"], np.float32).T)
    w1T = np.ascontiguousarray(np.asarray(full_inputs["L0_ffn_W1"], np.float32).T)
    w2T = np.ascontiguousarray(np.asarray(full_inputs["L0_ffn_W2"], np.float32).T)
    w3T = np.ascontiguousarray(np.asarray(full_inputs["L0_ffn_W3"], np.float32).T)
    b1c = np.ascontiguousarray(np.asarray(full_inputs["L0_ffn_b1"], np.float32).reshape(32, 128).T)
    b2c = np.ascontiguousarray(np.asarray(full_inputs["L0_ffn_b2"], np.float32).reshape(32, 128).T)
    b3r = np.ascontiguousarray(np.asarray(full_inputs["L0_ffn_b3"], np.float32).reshape(1, D))
    identt = np.eye(128, dtype=bf)
    ones_row = np.ones((1, 128), dtype=np.float32)

    inv = 1.0 / 10000.0 ** (np.arange(0, DH, 2, dtype=np.float64) / DH)
    rope = {}
    maskh = {}
    for h in range(2):
        start = h * T
        pos = start - 512 + np.arange(TK, dtype=np.float64)
        fr = pos[:, None] * inv[None, :]
        cos = np.repeat(np.cos(fr), 2, axis=-1)
        sin = np.repeat(np.sin(fr), 2, axis=-1)
        sgn = np.ones(DH)
        sgn[0::2] = -1.0
        ssgn = sin * sgn[None, :]
        rope[h] = (
            np.ascontiguousarray(cos.reshape(NTK, 128, DH).transpose(1, 0, 2)
                                 .reshape(128, NTK * DH).astype(np.float32)),
            np.ascontiguousarray(ssgn.reshape(NTK, 128, DH).transpose(1, 0, 2)
                                 .reshape(128, NTK * DH).astype(np.float32)))
        r = np.arange(128)[:, None]
        c = np.arange(640)[None, :]
        mk = np.zeros((128, 5 * 640), dtype=bf)
        for m in range(5):
            valid = (c > r) & (c <= r + 512)
            if h == 0 and m < 4:
                valid = valid & (m * 128 + c >= 512)
            mk[:, m * 640:(m + 1) * 640] = valid.astype(bf)
        maskh[h] = mk

    in_maps = []
    for core in range(8):
        b, h = core // 2, core % 2
        x_own = np.ascontiguousarray(x[b, h * T:(h + 1) * T])
        x_ctx = (np.ascontiguousarray(x[b, T - 512:T]) if h == 1
                 else np.zeros((512, D), np.float32))
        in_maps.append({
            "x_own": x_own, "x_ctx": x_ctx,
            "wqkvT": wqkvT, "wpT": wpT, "w1T": w1T, "w2T": w2T, "w3T": w3T,
            "b1c": b1c, "b2c": b2c, "b3r": b3r,
            "masks": maskh[h], "ropeC": rope[h][0], "ropeS": rope[h][1],
            "ident": identt, "ones_row": ones_row,
        })
    return in_maps


_CACHE = {}


def _get(debug=False, phases="full"):
    key = (debug, phases)
    if key not in _CACHE:
        _CACHE[key] = build(debug, phases)
    return _CACHE[key]


def kernel(**inputs):
    nc = _get(False, "full")
    in_maps = host_inputs(inputs)
    res = run_bass_kernel_spmd(nc, in_maps, core_ids=list(range(8)))
    out = np.empty((4, 4096, D), dtype=np.float32)
    for core in range(8):
        b, h = core // 2, core % 2
        out[b, h * T:(h + 1) * T] = res.results[core]["y"]
    return out


# version 08
# speedup vs baseline: 1.0588x; 1.0420x over previous
import numpy as np
import ml_dtypes
from contextlib import ExitStack

import concourse.bass as bass
import concourse.mybir as mybir
from concourse.bass_utils import run_bass_kernel_spmd

F32 = mybir.dt.float32
BF16 = mybir.dt.bfloat16
AF = mybir.ActivationFunctionType
ALU = mybir.AluOpType
AX = mybir.AxisListType

T = 2048
TK = 2560
D = 2048
DH = 128
HQ = 16
G = 4
RMS_EPS = 1.1920929e-07
LN_EPS = 1e-5
NT = 16
NTK = 20
KC = 16
QBIAS = float(-0.5 * np.log(128.0))
EXPBIAS = -16.0

ENGINES = ("tensor", "vector", "scalar", "gpsimd", "sync")


class Prog:
    def __init__(self):
        self.stream = {e: [] for e in ENGINES}
        self.sem = {}
        self.count = {}
        self.lastwait = {}

    def add_sem(self, name, handle):
        self.sem[name] = handle
        self.count[name] = 0

    def wait(self, eng, token):
        if token is None:
            return
        name, val = token
        if val <= 0:
            return
        if self.lastwait.get((eng, name), 0) >= val:
            return
        self.lastwait[(eng, name)] = val
        h = self.sem[name]
        self.stream[eng].append(lambda E, h=h, v=val: E.wait_ge(h, v))

    def op(self, eng, fn, waits=(), inc="default", amt=1):
        for t in waits:
            self.wait(eng, t)
        if inc == "default":
            inc = eng if eng != "sync" else None
        tok = None
        if inc is not None:
            self.count[inc] += amt
            tok = (inc, self.count[inc])
            h = self.sem[inc]
            self.stream[eng].append(
                lambda E, fn=fn, h=h, a=amt: fn(E).then_inc(h, a))
        else:
            self.stream[eng].append(lambda E, fn=fn: fn(E))
        return tok

    def dma(self, eng, qname, out, in_, waits=()):
        return self.op(eng, lambda E: E.dma_start(out=out, in_=in_),
                       waits=waits, inc=qname, amt=16)

    def barrier(self):
        snapshot = [(n, c) for n, c in self.count.items() if c > 0]
        for e in ENGINES:
            for n, c in snapshot:
                self.wait(e, (n, c))

    def token(self, name):
        return (name, self.count[name])


class Arena:

    def __init__(self, nc, size_bytes):
        self.nc = nc
        self.size = size_bytes
        self.pend = []

    def alloc(self, name, shape, dtype, off):
        per_part = int(np.prod(shape[1:])) * mybir.dt.size(dtype)
        assert off % 32 == 0, (name, off)
        assert off + per_part <= self.size, (name, off, per_part)
        hnd = self.nc.virtual_sbuf_tensor(name, shape, dtype)
        self.pend.append((hnd, off))
        return hnd

    def materialize(self, ctx):
        handle = ctx.enter_context(
            self.nc.sbuf_tensor("arena", [128, self.size], mybir.dt.uint8))
        base = self.nc.lookup_mloc(handle).addr
        for hnd, off in self.pend:
            self.nc._tensor(hnd.name, hnd.shape, hnd.dtype, type="SB")
            mloc = self.nc.lookup_mloc(hnd)
            mloc.allocated = True
            mloc.addr = base + off


def _ev(ap_):
    return ap_.rearrange("p (a two) -> p a two", two=2)[:, :, 0]


def _od(ap_):
    return ap_.rearrange("p (a two) -> p a two", two=2)[:, :, 1]


ARENA = 206144
A0_SERIAL = False


def build(debug=False, phases="full"):
    nc = bass.Bass("TRN2", target_bir_lowering=False)

    dx = nc.dram_tensor("x_own", [T, D], F32, kind="ExternalInput").ap()
    dxc = nc.dram_tensor("x_ctx", [512, D], F32, kind="ExternalInput").ap()
    dwqkvT = nc.dram_tensor("wqkvT", [D, 3072], F32, kind="ExternalInput").ap()
    dwpT = nc.dram_tensor("wpT", [D, D], F32, kind="ExternalInput").ap()
    dw1T = nc.dram_tensor("w1T", [D, 4096], F32, kind="ExternalInput").ap()
    dw2T = nc.dram_tensor("w2T", [4096, 4096], F32, kind="ExternalInput").ap()
    dw3T = nc.dram_tensor("w3T", [4096, D], F32, kind="ExternalInput").ap()
    db1c = nc.dram_tensor("b1c", [128, 32], F32, kind="ExternalInput").ap()
    db2c = nc.dram_tensor("b2c", [128, 32], F32, kind="ExternalInput").ap()
    db3r = nc.dram_tensor("b3r", [1, D], F32, kind="ExternalInput").ap()
    dmask = nc.dram_tensor("masks", [128, 5 * 640], BF16, kind="ExternalInput").ap()
    dropeC = nc.dram_tensor("ropeC", [128, NTK * DH], F32, kind="ExternalInput").ap()
    dropeS = nc.dram_tensor("ropeS", [128, NTK * DH], F32, kind="ExternalInput").ap()
    dident = nc.dram_tensor("ident", [128, 128], BF16, kind="ExternalInput").ap()
    dones = nc.dram_tensor("ones_row", [1, 128], F32, kind="ExternalInput").ap()

    dy = nc.dram_tensor("y", [T, D], F32, kind="ExternalOutput").ap()
    sck = "ExternalOutput" if debug else "Internal"
    dscY = nc.dram_tensor("scY", [HQ, 128, T], BF16, kind=sck).ap()
    dscXM = nc.dram_tensor("scXM", [T, D], F32, kind=sck).ap()
    dscH2 = nc.dram_tensor("scH2", [32, 128, T], BF16, kind=sck).ap()
    if debug:
        drr = nc.dram_tensor("dbg_rr", [NT, 128, 32], F32, kind="ExternalOutput").ap()
        dsqb = nc.dram_tensor("dbg_sqb", [NT, 128, 512], F32, kind="ExternalOutput").ap()
        dln1 = nc.dram_tensor("dbg_ln1T", [128, KC * TK], BF16, kind="ExternalOutput").ap()
        dqn = nc.dram_tensor("dbg_qn", [G, 128, 4 * T], BF16, kind="ExternalOutput").ap()
        dkn = nc.dram_tensor("dbg_kn", [G, 128, TK], BF16, kind="ExternalOutput").ap()
        dvt = nc.dram_tensor("dbg_vt", [G, 128, TK], BF16, kind="ExternalOutput").ap()
        dln2 = nc.dram_tensor("dbg_ln2T", [128, KC * T], BF16, kind="ExternalOutput").ap()
        dh1d = nc.dram_tensor("dbg_h1", [128, 32 * T], BF16, kind="ExternalOutput").ap()

    p = Prog()
    ar = Arena(nc, ARENA)

    ident = ar.alloc("identt", [128, 128], BF16, 0)
    onesr = ar.alloc("onesrr", [1, 128], F32, 256)
    b1c = ar.alloc("b1cc", [128, 32], F32, 768)
    b2c = ar.alloc("b2cc", [128, 32], F32, 896)
    constv = ar.alloc("constv", [128, 8], F32, 1024)
    P0 = 1088
    ln1T = ar.alloc("ln1T", [128, KC * TK], BF16, P0)
    A2R = P0 + 81920
    cur = A2R

    def nxt(sz):
        nonlocal cur
        off = cur
        cur += sz
        assert cur <= ARENA
        return off
    ropeC = ar.alloc("ropeCs", [128, NTK * DH], F32, nxt(10240))
    ropeS = ar.alloc("ropeSs", [128, NTK * DH], F32, nxt(10240))
    masks = ar.alloc("maskss", [128, 5 * 640], BF16, nxt(6400))
    A0ALIAS = cur
    wTq = ar.alloc("wTq", [128, KC * 512], BF16, nxt(16384))
    wTkv = ar.alloc("wTkv", [128, KC * 256], BF16, nxt(8192))
    stg = ar.alloc("stg", [128, 1024], F32, nxt(4096))
    sqb = ar.alloc("sqb", [128, 1024], F32, nxt(4096))
    rr = ar.alloc("rr", [128, 32], F32, nxt(128))
    ropet = ar.alloc("ropet", [128, 256], F32, nxt(1024))
    ntm = ar.alloc("ntm", [128, 512], BF16, nxt(1024))
    qnf = ar.alloc("qnf", [128, 4 * T], BF16, nxt(16384))
    knf = ar.alloc("knf", [128, TK], BF16, nxt(5120))
    vtm = ar.alloc("vtm", [128, TK], BF16, nxt(5120))
    pbuf = ar.alloc("pbuf", [128, 4 * 640], BF16, nxt(5120))
    onescol = ar.alloc("onescol", [128, 1], BF16, nxt(32))
    recipr = ar.alloc("recipr", [1, 512], F32, nxt(2048))
    yevb = ar.alloc("yevb", [128, 2 * 512], BF16, nxt(2048))
    attst = ar.alloc("attst", [128, 8], F32, nxt(32))
    assert cur <= ARENA
    a0x = ar.alloc("a0x", [128, 2 * D], F32, A0ALIAS)
    a0ln = ar.alloc("a0ln", [128, D], BF16, A0ALIAS + 16384)
    a0st = ar.alloc("a0st", [128, 64], F32, A0ALIAS + 20480)
    wpT = ar.alloc("wpTs", [128, KC * D], BF16, P0)
    A3B = P0 + 65536
    a3yld = ar.alloc("a3yld", [128, 2 * HQ * 256], BF16, A3B)
    a3y2 = ar.alloc("a3y2", [128, D], F32, A3B + 16384)
    a3xr = ar.alloc("a3xr", [128, 2 * D], F32, A3B + 24576)
    a3xm = ar.alloc("a3xm", [128, D], F32, A3B + 40960)
    a3ln2 = ar.alloc("a3ln2", [128, D], BF16, A3B + 49152)
    a3st = ar.alloc("a3st", [128, 128], F32, A3B + 53248)
    LN2T = P0 + 131072
    ln2T = ar.alloc("ln2T", [128, KC * T], BF16, LN2T)
    h1 = ar.alloc("h1s", [128, 32 * T], BF16, P0)
    B1B = LN2T + 65536
    wT1 = ar.alloc("wT1", [128, 2 * KC * 128], BF16, B1B)
    wT2 = ar.alloc("wT2", [128, 2 * 32 * 128], BF16, LN2T)
    h2b = ar.alloc("h2b", [128, 2 * D], BF16, LN2T + 16384)
    w3Ts = ar.alloc("w3Ts", [128, 32 * T], BF16, P0)
    b3h2 = ar.alloc("b3h2", [128, 2 * 32 * 256], BF16, LN2T)
    b3y3 = ar.alloc("b3y3", [128, D], F32, LN2T + 32768)
    b3xm = ar.alloc("b3xm", [128, D], F32, LN2T + 40960)
    b3bc = ar.alloc("b3bc", [128, D], F32, LN2T + 49152)
    b3st = ar.alloc("b3st", [128, 64], F32, LN2T + 57344)
    b3r = ar.alloc("b3rr", [1, D], F32, LN2T + 57600)
    b3ob = ar.alloc("b3ob", [128, D], F32, LN2T + 65792)

    ctx = ExitStack()
    with ctx:
        for s in ("tensor", "vector", "scalar", "gpsimd", "q_in", "q_wt", "q_out"):
            p.add_sem(s, ctx.enter_context(nc.semaphore(s)))
        ar.materialize(ctx)
        psQ = ctx.enter_context(nc.psum_tensor("psQ", [128, 1024], F32))
        psS = ctx.enter_context(nc.psum_tensor("psS", [128, 1024], F32))
        psP = ctx.enter_context(nc.psum_tensor("psP", [128, 1024], F32))
        psY = ctx.enter_context(nc.psum_tensor("psY", [128, 512], F32))
        blk = ctx.enter_context(nc.Block())

        def emit_ln(stbuf, c0, src, waits, out_ap, out_src=None, extra_ap_waits=()):
            stt = None
            for chn in range(4):
                stt = p.op("vector", lambda E, chn=chn: E.bn_stats(
                    out=stbuf[:, c0 + chn * 6:c0 + chn * 6 + 6],
                    in_=src[:, chn * 512:(chn + 1) * 512]),
                    waits=(list(waits) if chn == 0 else []))
            agg = p.op("vector", lambda E: E.bn_aggr(
                out=stbuf[:, c0 + 24:c0 + 26], in_=stbuf[:, c0:c0 + 24]), waits=[stt])
            l_ = p.op("scalar", lambda E: E.activation(
                out=stbuf[:, c0 + 26:c0 + 27], in_=stbuf[:, c0 + 25:c0 + 26],
                func=AF.Ln, bias=LN_EPS), waits=[agg])
            e_ = p.op("scalar", lambda E: E.activation(
                out=stbuf[:, c0 + 27:c0 + 28], in_=stbuf[:, c0 + 26:c0 + 27],
                func=AF.Exp, scale=-0.5), waits=[l_])
            nm_ = p.op("vector", lambda E: E.scalar_tensor_tensor(
                out=stbuf[:, c0 + 28:c0 + 29], in0=stbuf[:, c0 + 24:c0 + 25],
                scalar=-1.0, in1=stbuf[:, c0 + 27:c0 + 28],
                op0=ALU.mult, op1=ALU.mult), waits=[e_])
            src2 = out_src if out_src is not None else src
            ap_ = p.op("vector", lambda E: E.tensor_scalar(
                out_ap, src2, stbuf[:, c0 + 27:c0 + 28], stbuf[:, c0 + 28:c0 + 29],
                ALU.mult, ALU.add), waits=[nm_] + list(extra_ap_waits))
            return ap_

        CONSTS = (LN_EPS, RMS_EPS, QBIAS, -0.5, 1.0 / 128.0, EXPBIAS)
        cst = None
        for k, v in enumerate(CONSTS):
            nc.const_aps.register(F32, v, constv[:, k:k + 1])
            cst = p.op("vector", lambda E, k=k, v=v: E.memset(constv[:, k:k + 1], v))
        cst = p.op("vector", lambda E: E.memset(onescol[:], 1.0))
        for e in ENGINES:
            p.wait(e, cst)

        p.dma("sync", "q_in", ident[:], dident)
        p.dma("sync", "q_in", onesr[:], dones)
        p.dma("sync", "q_in", b1c[:], db1c)
        p.dma("sync", "q_in", b2c[:], db2c)
        tok_id = p.token("q_in")
        p.dma("sync", "q_in", ropeC[:], dropeC)
        p.dma("sync", "q_in", ropeS[:], dropeS)
        p.dma("sync", "q_in", masks[:], dmask)
        tok_consts = p.token("q_in")

        ln1Tv = ln1T[:].rearrange("p (kc t) -> p kc t", t=TK)
        xv = dx.rearrange("(i p) d -> i p d", p=128)
        xcv = dxc.rearrange("(i p) d -> i p d", p=128)

        ldtok = [None] * NTK
        evtok = [None] * NTK

        def a0_load(jk):
            src = xcv[jk] if jk < 4 else xv[jk - 4]
            ldtok[jk] = p.dma("sync", "q_in",
                              a0x[:, (jk % 2) * D:(jk % 2) * D + D], src,
                              waits=[evtok[jk - 2]] if jk >= 2 else [])
        a0_load(0)
        a0_load(1)
        prev_tr2 = None
        for jk in range(NTK):
            xt = a0x[:, (jk % 2) * D:(jk % 2) * D + D]
            c0 = (jk % 2) * 30
            aptok = emit_ln(a0st, c0, xt, [ldtok[jk]], a0ln[:],
                            extra_ap_waits=[prev_tr2])
            tr1 = tr2 = None
            for j in range(KC):
                pst = psS if j < 8 else psP
                tr = p.op("tensor", lambda E, j=j, pst=pst: E.matmul(
                    pst[:, (j % 8) * 128:(j % 8) * 128 + 128],
                    lhsT=a0ln[:, j * 128:(j + 1) * 128], rhs=ident[:],
                    start=True, stop=True),
                    waits=([aptok, tok_id, evtok[jk - 1] if jk >= 1 else None]
                           if j == 0 else []))
                if j == 7:
                    tr1 = tr
                if j == 15:
                    tr2 = tr
            prev_tr2 = tr2
            e1 = p.op("vector", lambda E, jk=jk: E.tensor_copy(
                out=ln1Tv[:, 0:8, jk * 128:(jk + 1) * 128],
                in_=psS[:].rearrange("p (kc c) -> p kc c", c=128)), waits=[tr1])
            evtok[jk] = p.op("vector", lambda E, jk=jk: E.tensor_copy(
                out=ln1Tv[:, 8:16, jk * 128:(jk + 1) * 128],
                in_=psP[:].rearrange("p (kc c) -> p kc c", c=128)), waits=[tr2, e1])
            if jk + 2 < NTK:
                a0_load(jk + 2)
            if A0_SERIAL:
                p.barrier()
        p.barrier()
        if debug:
            p.dma("sync", "q_out", dln1, ln1T[:])
            p.barrier()
        if phases == "A0":
            return _finish(nc, p, blk)

        wTqv = wTq[:].rearrange("p (kc f) -> p kc f", f=512)
        wTkvv = wTkv[:].rearrange("p (kc f) -> p kc f", f=256)
        qnfv = qnf[:].rearrange("p (hh t) -> p hh t", t=T)
        maskv = masks[:].rearrange("p (i c) -> p i c", c=640)
        wqkvTv = dwqkvT.rearrange("(kc p) f -> p kc f", p=128)

        for g in range(G):
            wtoks = [
                p.dma("gpsimd", "q_wt", wTqv[:, 0:8, :],
                      wqkvTv[:, 0:8, 512 * g:512 * g + 512]),
                p.dma("gpsimd", "q_wt", wTqv[:, 8:16, :],
                      wqkvTv[:, 8:16, 512 * g:512 * g + 512]),
                p.dma("gpsimd", "q_wt", wTkvv[:, :, 0:128],
                      wqkvTv[:, :, 2048 + 128 * g:2048 + 128 * g + 128]),
                p.dma("gpsimd", "q_wt", wTkvv[:, :, 128:256],
                      wqkvTv[:, :, 2560 + 128 * g:2560 + 128 * g + 128]),
            ]
            wtok = wtoks[-1]

            kn_done = [None] * NTK
            qn_done = [None] * NT
            stg_free = [None, None]
            sq_free = [None, None]
            prev_cpk = None
            prev_cpq = None
            prev_kev = None
            prev_qev = None
            prev_trk = None
            prev_trq = None
            for jk in range(NTK):
                half = jk % 2
                stgk = stg[:, half * 512:half * 512 + 256]
                mmk = None
                for kcc in range(KC):
                    mmk = p.op("tensor", lambda E, kcc=kcc, jk=jk: E.matmul(
                        psQ[:, 512:768],
                        lhsT=ln1Tv[:, kcc, jk * 128:(jk + 1) * 128],
                        rhs=wTkvv[:, kcc, :],
                        start=(kcc == 0), stop=(kcc == KC - 1)),
                        waits=([wtok, prev_cpk] if kcc == 0 else []))
                cpk = p.op("vector", lambda E, stgk=stgk: E.tensor_copy(
                    out=stgk, in_=psQ[:, 512:768]),
                    waits=[mmk, stg_free[half], sq_free[half]])
                prev_cpk = cpk
                sqk = p.op("scalar", lambda E, stgk=stgk, half=half: E.activation(
                    out=sqb[:, 512 + half * 128:512 + half * 128 + 128],
                    in_=stgk[:, 0:128], func=AF.Square), waits=[cpk])
                red = p.op("vector", lambda E, half=half: E.reduce_sum(
                    out=rr[:, half:half + 1],
                    in_=sqb[:, 512 + half * 128:512 + half * 128 + 128],
                    axis=AX.X), waits=[sqk])
                ln_ = p.op("scalar", lambda E, half=half: E.activation(
                    out=rr[:, 2 + half:3 + half], in_=rr[:, half:half + 1],
                    func=AF.Ln, scale=1.0 / 128.0, bias=RMS_EPS), waits=[red])
                exr = p.op("scalar", lambda E, half=half: E.activation(
                    out=rr[:, 4 + half:5 + half], in_=rr[:, 2 + half:3 + half],
                    func=AF.Exp, scale=-0.5), waits=[ln_])
                rK = rr[:, 4 + half:5 + half]
                cs = ropeC[:, jk * 128:(jk + 1) * 128]
                ss = ropeS[:, jk * 128:(jk + 1) * 128]
                t1 = ropet[:, 0:128]
                t2 = ropet[:, 128:256]
                ktm = ntm[:, 0:128]
                a_ = p.op("vector", lambda E, stgk=stgk, t1=t1, cs=cs, rK=rK:
                          E.scalar_tensor_tensor(out=t1, in0=stgk[:, 0:128], scalar=rK,
                                                 in1=cs, op0=ALU.mult, op1=ALU.mult),
                          waits=[exr, tok_consts])
                b_ = p.op("vector", lambda E, stgk=stgk, t2=t2, ss=ss, rK=rK:
                          E.scalar_tensor_tensor(out=_ev(t2), in0=_od(stgk[:, 0:128]),
                                                 scalar=rK, in1=_ev(ss),
                                                 op0=ALU.mult, op1=ALU.mult), waits=[exr])
                c_ = p.op("vector", lambda E, stgk=stgk, t2=t2, ss=ss, rK=rK:
                          E.scalar_tensor_tensor(out=_od(t2), in0=_ev(stgk[:, 0:128]),
                                                 scalar=rK, in1=_od(ss),
                                                 op0=ALU.mult, op1=ALU.mult), waits=[exr])
                d_ = p.op("vector", lambda E, t1=t1, t2=t2, ktm=ktm: E.tensor_add(
                    out=ktm, in0=t1, in1=t2), waits=[a_, b_, c_, prev_trk])
                trk = p.op("tensor", lambda E, ktm=ktm: E.matmul(
                    psP[:, 0:128], lhsT=ktm, rhs=ident[:], start=True, stop=True),
                    waits=[d_, prev_kev])
                prev_trk = trk
                kev = p.op("vector", lambda E, jk=jk: E.tensor_copy(
                    out=knf[:, jk * 128:(jk + 1) * 128], in_=psP[:, 0:128]), waits=[trk])
                prev_kev = kev
                vcp = p.op("vector", lambda E, stgk=stgk, jk=jk: E.tensor_copy(
                    out=vtm[:, jk * 128:(jk + 1) * 128], in_=stgk[:, 128:256]),
                    waits=[cpk])
                kn_done[jk] = vcp
                if jk < 4:
                    stg_free[half] = vcp
                    sq_free[half] = sqk

                if jk >= 4:
                    i = jk - 4
                    qstg = stg[:, half * 512:half * 512 + 512]
                    mmq = None
                    for kcc in range(KC):
                        mmq = p.op("tensor", lambda E, kcc=kcc, jk=jk: E.matmul(
                            psQ[:, 0:512],
                            lhsT=ln1Tv[:, kcc, jk * 128:(jk + 1) * 128],
                            rhs=wTqv[:, kcc, :],
                            start=(kcc == 0), stop=(kcc == KC - 1)),
                            waits=([wtok, prev_cpq] if kcc == 0 else []))
                    cpq = p.op("vector", lambda E, qstg=qstg: E.tensor_copy(
                        out=qstg, in_=psQ[:, 0:512]), waits=[mmq, sqk])
                    prev_cpq = cpq
                    sqq = p.op("scalar", lambda E, qstg=qstg: E.activation(
                        out=sqb[:, 0:512], in_=qstg, func=AF.Square), waits=[cpq])
                    redq = p.op("vector", lambda E: E.reduce_sum(
                        out=rr[:, 8:12], in_=sqb[:, 0:512].rearrange(
                            "p (h c) -> p h c", c=128), axis=AX.X), waits=[sqq])
                    lnq = p.op("scalar", lambda E: E.activation(
                        out=rr[:, 12:16], in_=rr[:, 8:12], func=AF.Ln,
                        scale=1.0 / 128.0, bias=RMS_EPS), waits=[redq])
                    exq = p.op("scalar", lambda E: E.activation(
                        out=rr[:, 16:20], in_=rr[:, 12:16], func=AF.Exp,
                        scale=-0.5, bias=QBIAS), waits=[lnq])
                    if debug and g == 0:
                        p.dma("sync", "q_out", drr[i], rr[:], waits=[exq])
                        p.dma("sync", "q_out", dsqb[i], sqb[:, 0:512], waits=[exq])
                    qev = None
                    for hh in range(4):
                        rQ = rr[:, 16 + hh:17 + hh]
                        qs = qstg[:, hh * 128:hh * 128 + 128]
                        qt1 = ropet[:, 0:128]
                        qt2 = ropet[:, 128:256]
                        qtm = ntm[:, 128:256]
                        a2_ = p.op("vector", lambda E, qs=qs, qt1=qt1, cs=cs, rQ=rQ:
                                   E.scalar_tensor_tensor(out=qt1, in0=qs, scalar=rQ,
                                                          in1=cs, op0=ALU.mult,
                                                          op1=ALU.mult),
                                   waits=[exq, d_])
                        b2_ = p.op("vector", lambda E, qs=qs, qt2=qt2, ss=ss, rQ=rQ:
                                   E.scalar_tensor_tensor(out=_ev(qt2), in0=_od(qs),
                                                          scalar=rQ, in1=_ev(ss),
                                                          op0=ALU.mult, op1=ALU.mult),
                                   waits=[exq])
                        c2_ = p.op("vector", lambda E, qs=qs, qt2=qt2, ss=ss, rQ=rQ:
                                   E.scalar_tensor_tensor(out=_od(qt2), in0=_ev(qs),
                                                          scalar=rQ, in1=_od(ss),
                                                          op0=ALU.mult, op1=ALU.mult),
                                   waits=[exq])
                        d2_ = p.op("vector", lambda E, qt1=qt1, qt2=qt2, qtm=qtm:
                                   E.tensor_add(out=qtm, in0=qt1, in1=qt2),
                                   waits=[a2_, b2_, c2_, prev_trq])
                        trq = p.op("tensor", lambda E, qtm=qtm: E.matmul(
                            psP[:, 128:256], lhsT=qtm, rhs=ident[:], start=True,
                            stop=True), waits=[d2_, prev_qev])
                        prev_trq = trq
                        qev = p.op("vector", lambda E, hh=hh, i=i: E.tensor_copy(
                            out=qnfv[:, hh, i * 128:(i + 1) * 128], in_=psP[:, 128:256]),
                            waits=[trq])
                        prev_qev = qev
                    qn_done[i] = qev
                    stg_free[half] = qev
                    sq_free[half] = sqq
            if debug:
                p.barrier()
                p.dma("sync", "q_out", dqn[g], qnf[:])
                p.dma("sync", "q_out", dkn[g], knf[:])
                p.dma("sync", "q_out", dvt[g], vtm[:])
                p.barrier()

            last_exp = [None, None]
            pb_free = {}
            mask_tok = {}
            last_ye = [None, None]
            last_ydma = [None, None]
            last_rc = None
            last_bc = None
            last_yn = None
            last_dn = None

            def att_head(t):
                hh, i = divmod(t, NT)
                ib = t % 2
                ib4 = t % 4
                pS = psS if ib == 0 else psQ
                st = None
                for j in range(5):
                    st = p.op("tensor", lambda E, hh=hh, i=i, j=j, pS=pS: E.matmul(
                        pS[:, j * 128:(j + 1) * 128],
                        lhsT=knf[:, (i + j) * 128:(i + j + 1) * 128],
                        rhs=qnfv[:, hh, i * 128:(i + 1) * 128],
                        start=True, stop=True),
                        waits=([qn_done[i], kn_done[min(i + 4, NTK - 1)],
                                last_exp[ib]] if j == 0 else []))
                pb = pbuf[:, ib4 * 640:ib4 * 640 + 640]
                ex = p.op("scalar", lambda E, pb=pb, pS=pS: E.activation(
                    out=pb, in_=pS[:, 0:640], func=AF.Exp, bias=EXPBIAS),
                    waits=[st, pb_free.pop(t - 4, None)])
                last_exp[ib] = ex
                if i < 4:
                    mk = p.op("vector", lambda E, pb=pb, i=i: E.tensor_tensor(
                        out=pb, in0=pb, in1=maskv[:, i, :], op=ALU.mult),
                        waits=[ex])
                else:
                    p.op("vector", lambda E, pb=pb: E.tensor_tensor(
                        out=pb[:, 0:128], in0=pb[:, 0:128],
                        in1=maskv[:, 4, 0:128], op=ALU.mult), waits=[ex])
                    mk = p.op("vector", lambda E, pb=pb: E.tensor_tensor(
                        out=pb[:, 512:640], in0=pb[:, 512:640],
                        in1=maskv[:, 4, 512:640], op=ALU.mult))
                mask_tok[t] = mk

            def emit_tail(t):
                nonlocal last_rc, last_bc, last_yn, last_dn
                hh, i = divmod(t, NT)
                head = 4 * g + hh
                ib4 = t % 4
                pb = pbuf[:, ib4 * 640:ib4 * 640 + 640]
                mk = mask_tok.pop(t)
                dn = None
                for j in range(5):
                    dn = p.op("tensor", lambda E, pb=pb, j=j, i=i: E.matmul(
                        psP[0:1, (i % 4) * 128:(i % 4) * 128 + 128],
                        lhsT=onescol[:], rhs=pb[:, j * 128:(j + 1) * 128],
                        start=(j == 0), stop=(j == 4)),
                        waits=([mk] + ([last_rc] if i % 4 == 0 else [])
                               if j == 0 else []))
                last_dn = dn
                mv = None
                for j in range(5):
                    mv = p.op("tensor", lambda E, pb=pb, i=i, j=j: E.matmul(
                        psY[:, (i % 4) * 128:(i % 4) * 128 + 128],
                        lhsT=vtm[:, (i + j) * 128:(i + j) * 128 + 128],
                        rhs=pb[:, j * 128:(j + 1) * 128],
                        start=(i % 4 == 0 and j == 0),
                        stop=(i % 4 == 3 and j == 4)),
                        waits=([mk] + ([last_ye[(i // 4) % 2]]
                                       if i % 4 == 0 else [])
                               if j == 0 else []))
                pb_free[t] = mv
                if i % 4 == 3:
                    q4 = i // 4
                    gpar = q4 % 2
                    yv_ = yevb[:, gpar * 512:gpar * 512 + 512]
                    ye = p.op("vector", lambda E, yv_=yv_: E.tensor_copy(
                        out=yv_, in_=psY[:]), waits=[mv, last_ydma[gpar]])
                    last_ye[gpar] = ye
                    rc = p.op("vector", lambda E: E.reciprocal(
                        out=recipr[:], in_=psP[0:1, 0:512]),
                        waits=[dn, last_bc])
                    last_rc = rc
                    bc = p.op("tensor", lambda E: E.matmul(
                        psP[:, 512:1024], lhsT=onesr[:], rhs=recipr[:],
                        start=True, stop=True), waits=[rc, last_yn])
                    last_bc = bc
                    yn = p.op("vector", lambda E, yv_=yv_: E.tensor_tensor(
                        out=yv_, in0=yv_, in1=psP[:, 512:1024], op=ALU.mult),
                        waits=[ye, bc])
                    last_yn = yn
                    last_ydma[gpar] = p.dma(
                        "scalar", "q_out", dscY[head][:, q4 * 512:(q4 + 1) * 512],
                        yv_, waits=[yn])

            for t in range(4 * NT):
                if t >= 2:
                    emit_tail(t - 2)
                att_head(t)
            emit_tail(4 * NT - 2)
            emit_tail(4 * NT - 1)
            p.barrier()

        if phases == "A2":
            return _finish(nc, p, blk)

        wpTv = wpT[:].rearrange("p (h f) -> p h f", f=D)
        dwpTv = dwpT.rearrange("(h p) f -> p h f", p=128)
        wl = None
        for c4 in range(4):
            wl = p.dma("gpsimd", "q_wt", wpTv[:, c4 * 4:(c4 + 1) * 4, :],
                       dwpTv[:, c4 * 4:(c4 + 1) * 4, :])
        wptok = wl
        scYv = dscY.rearrange("h p t -> p h t")
        a3yldv = a3yld[:].rearrange("p (b h c) -> p b h c", b=2, h=HQ)
        xmv = dscXM.rearrange("(i p) d -> i p d", p=128)
        ln2Tv = ln2T[:].rearrange("p (kc t) -> p kc t", t=T)
        lastxr = [None, None]
        prev_add = None
        prev_xmdma = None
        prev_le2 = None
        yld_dma = [None, None]
        evs_prev = [None] * 4
        for i in range(NT):
            ib2 = (i // 2) % 2
            if i % 2 == 0:
                yld_dma[ib2] = p.dma(
                    "sync", "q_in", a3yldv[:, ib2, :, :],
                    scYv[:, :, i * 128:i * 128 + 256],
                    waits=[prev_add] if i >= 4 else [])
            xrt = p.dma("sync", "q_in", a3xr[:, (i % 2) * D:(i % 2) * D + D],
                        xv[i], waits=[lastxr[i % 2]])
            chunk_mm = []
            for oc in range(4):
                pst = psS if oc < 2 else psP
                cols = (oc % 2) * 512
                mm = None
                for h2 in range(HQ):
                    mm = p.op("tensor", lambda E, h2=h2, oc=oc, i=i, pst=pst, cols=cols:
                              E.matmul(pst[:, cols:cols + 512],
                                       lhsT=a3yldv[:, (i // 2) % 2, h2,
                                                   (i % 2) * 128:(i % 2) * 128 + 128],
                                       rhs=wpTv[:, h2, oc * 512:(oc + 1) * 512],
                                       start=(h2 == 0), stop=(h2 == HQ - 1)),
                              waits=([yld_dma[(i // 2) % 2], wptok, evs_prev[oc]]
                                     if h2 == 0 else []))
                chunk_mm.append(mm)
            evs = []
            for oc in range(4):
                pst = psS if oc < 2 else psP
                cols = (oc % 2) * 512
                evs.append(p.op("vector", lambda E, oc=oc, pst=pst, cols=cols:
                                E.tensor_copy(out=a3y2[:, oc * 512:(oc + 1) * 512],
                                              in_=pst[:, cols:cols + 512]),
                                waits=[chunk_mm[oc]]))
                evs_prev[oc] = evs[oc]
            ap_ = emit_ln(a3st, (i % 2) * 30, a3y2[:], evs, a3y2[:])
            xr = a3xr[:, (i % 2) * D:(i % 2) * D + D]
            add = p.op("vector", lambda E, xr=xr: E.tensor_add(
                out=a3xm[:], in0=xr, in1=a3y2[:]),
                waits=[ap_, xrt, prev_xmdma])
            prev_add = add
            lastxr[i % 2] = add
            xmdma = p.dma("scalar", "q_out", xmv[i], a3xm[:], waits=[add])
            prev_xmdma = xmdma
            ap2 = emit_ln(a3st, 60 + (i % 2) * 30, a3xm[:], [add], a3ln2[:],
                          extra_ap_waits=[prev_le2])
            le1 = prev_le2
            for r in range(2):
                tr = None
                for j in range(8):
                    tr = p.op("tensor", lambda E, r=r, j=j: E.matmul(
                        psQ[:, j * 128:(j + 1) * 128],
                        lhsT=a3ln2[:, (r * 8 + j) * 128:(r * 8 + j + 1) * 128],
                        rhs=ident[:], start=True, stop=True),
                        waits=([ap2, le1] if j == 0 else []))
                le1 = p.op("vector", lambda E, i=i, r=r: E.tensor_copy(
                    out=ln2Tv[:, r * 8:r * 8 + 8, i * 128:(i + 1) * 128],
                    in_=psQ[:].rearrange("p (kc c) -> p kc c", c=128)), waits=[tr])
            prev_le2 = le1
        p.barrier()
        if debug:
            p.dma("sync", "q_out", dln2, ln2T[:])
            p.barrier()
        if phases == "A3":
            return _finish(nc, p, blk)

        w1Tv = dw1T.rearrange("(kc p) f -> p kc f", p=128)
        wT1v = wT1[:].rearrange("p (b kc c) -> p b kc c", b=2, c=128)
        h1v = h1[:].rearrange("p (m t) -> p m t", t=T)
        wtk = [None, None]
        rel_tok = [None, None]
        prev_rel_chunk = [None] * 4
        for m in range(32):
            mb = m % 2
            wtk[mb] = p.dma("gpsimd", "q_wt", wT1v[:, mb, :, :],
                            w1Tv[:, :, m * 128:(m + 1) * 128],
                            waits=[rel_tok[mb]])
            rel = None
            for chn in range(4):
                pst = psS if chn < 2 else psP
                cols = (chn % 2) * 512
                mm = None
                for kcc in range(KC):
                    mm = p.op("tensor", lambda E, kcc=kcc, chn=chn, pst=pst, cols=cols,
                              mb=mb: E.matmul(
                                  pst[:, cols:cols + 512],
                                  lhsT=wT1v[:, mb, kcc, :],
                                  rhs=ln2Tv[:, kcc, chn * 512:(chn + 1) * 512],
                                  start=(kcc == 0), stop=(kcc == KC - 1)),
                              waits=([wtk[mb], prev_rel_chunk[chn]] if kcc == 0 else []))
                rel = p.op("scalar", lambda E, m=m, chn=chn, pst=pst, cols=cols:
                           E.activation(out=h1v[:, m, chn * 512:(chn + 1) * 512],
                                        in_=pst[:, cols:cols + 512],
                                        func=AF.Relu, bias=b1c[:, m:m + 1]),
                           waits=[mm, tok_id])
                prev_rel_chunk[chn] = rel
            rel_tok[mb] = rel
        p.barrier()
        if debug:
            p.dma("sync", "q_out", dh1d, h1[:])
            p.barrier()
        if phases == "B1":
            return _finish(nc, p, blk)

        w2Tv = dw2T.rearrange("(kc p) f -> p kc f", p=128)
        wT2v = wT2[:].rearrange("p (b kc c) -> p b kc c", b=2, c=128)
        wtk = [None, None]
        rel_tok = [None, None]
        h2dma = [None, None]
        prev_rel_chunk = [None] * 4
        for m in range(32):
            mb = m % 2
            p.dma("gpsimd", "q_wt", wT2v[:, mb, 0:16, :],
                  w2Tv[:, 0:16, m * 128:(m + 1) * 128], waits=[rel_tok[mb]])
            wtk[mb] = p.dma("gpsimd", "q_wt", wT2v[:, mb, 16:32, :],
                            w2Tv[:, 16:32, m * 128:(m + 1) * 128])
            rel = None
            for chn in range(4):
                pst = psS if chn < 2 else psP
                cols = (chn % 2) * 512
                mm = None
                for kcc in range(32):
                    mm = p.op("tensor", lambda E, kcc=kcc, chn=chn, pst=pst, cols=cols,
                              mb=mb: E.matmul(
                                  pst[:, cols:cols + 512],
                                  lhsT=wT2v[:, mb, kcc, :],
                                  rhs=h1v[:, kcc, chn * 512:(chn + 1) * 512],
                                  start=(kcc == 0), stop=(kcc == 31)),
                              waits=([wtk[mb], prev_rel_chunk[chn]] if kcc == 0 else []))
                rel = p.op("scalar", lambda E, chn=chn, pst=pst, cols=cols, mb=mb, m=m:
                           E.activation(out=h2b[:, mb * D + chn * 512:mb * D + (chn + 1) * 512],
                                        in_=pst[:, cols:cols + 512],
                                        func=AF.Relu, bias=b2c[:, m:m + 1]),
                           waits=([mm, h2dma[mb]] if chn == 0 else [mm]))
                prev_rel_chunk[chn] = rel
            rel_tok[mb] = rel
            h2dma[mb] = p.dma("scalar", "q_out", dscH2[m], h2b[:, mb * D:mb * D + D],
                              waits=[rel])
        p.barrier()
        if phases == "B2":
            return _finish(nc, p, blk)

        w3Tv = dw3T.rearrange("(kc p) f -> p kc f", p=128)
        w3sv = w3Ts[:].rearrange("p (kc f) -> p kc f", f=T)
        w3tok_half = None
        w3tok = None
        for c8 in range(4):
            w3tok = p.dma("gpsimd", "q_wt", w3sv[:, c8 * 8:(c8 + 1) * 8, :],
                          w3Tv[:, c8 * 8:(c8 + 1) * 8, :])
            if c8 == 1:
                w3tok_half = w3tok
        b3ld = p.dma("sync", "q_in", b3r[:], db3r)
        btoks = []
        for oc in range(4):
            mmb = p.op("tensor", lambda E, oc=oc: E.matmul(
                psQ[:, (oc % 2) * 512:(oc % 2) * 512 + 512], lhsT=onesr[:],
                rhs=b3r[:, oc * 512:(oc + 1) * 512], start=True, stop=True),
                waits=[b3ld, tok_id, btoks[oc - 2] if oc >= 2 else None])
            btoks.append(p.op("vector", lambda E, oc=oc: E.tensor_copy(
                out=b3bc[:, oc * 512:(oc + 1) * 512],
                in_=psQ[:, (oc % 2) * 512:(oc % 2) * 512 + 512]), waits=[mmb]))
        b3tok = btoks[-1]
        scH2v = dscH2.rearrange("m p t -> p m t")
        b3h2v = b3h2[:].rearrange("p (b m c) -> p b m c", b=2, m=32)
        h2ld = [None, None]
        lastout = None
        prev_outdma = None
        pair_mm = [None, None]
        evs3_prev = [None] * 4
        for i in range(NT):
            ib2 = (i // 2) % 2
            if i % 2 == 0:
                p.dma("sync", "q_in", b3h2v[:, ib2, 0:16, :],
                      scH2v[:, 0:16, i * 128:i * 128 + 256],
                      waits=[pair_mm[ib2]])
                h2ld[ib2] = p.dma("sync", "q_in", b3h2v[:, ib2, 16:32, :],
                                  scH2v[:, 16:32, i * 128:i * 128 + 256])
            xmt = p.dma("sync", "q_in", b3xm[:], xmv[i], waits=[lastout])
            evs = []
            mm = None
            for oc in range(4):
                pst = psS if oc < 2 else psP
                cols = (oc % 2) * 512
                for kcc in range(32):
                    w = []
                    if kcc == 0:
                        w = [h2ld[ib2], w3tok_half, evs3_prev[oc]]
                    elif kcc == 16:
                        w = [w3tok]
                    mm = p.op("tensor", lambda E, kcc=kcc, oc=oc, i=i, pst=pst,
                              cols=cols, ib2=ib2: E.matmul(
                                  pst[:, cols:cols + 512],
                                  lhsT=b3h2v[:, ib2, kcc,
                                             (i % 2) * 128:(i % 2) * 128 + 128],
                                  rhs=w3sv[:, kcc, oc * 512:(oc + 1) * 512],
                                  start=(kcc == 0), stop=(kcc == 31)),
                              waits=w)
                evs.append(p.op("vector", lambda E, oc=oc, pst=pst, cols=cols:
                                E.tensor_add(out=b3y3[:, oc * 512:(oc + 1) * 512],
                                             in0=pst[:, cols:cols + 512],
                                             in1=b3bc[:, oc * 512:(oc + 1) * 512]),
                                waits=[mm, b3tok]))
                evs3_prev[oc] = evs[oc]
            pair_mm[ib2] = mm
            ap_ = emit_ln(b3st, (i % 2) * 30, b3y3[:], evs, b3y3[:])
            lastout = p.op("vector", lambda E: E.tensor_add(
                out=b3ob[:], in0=b3xm[:], in1=b3y3[:]),
                waits=[ap_, xmt, prev_outdma])
            prev_outdma = p.dma("scalar", "q_out",
                                dy.rearrange("(i p) d -> i p d", p=128)[i], b3ob[:],
                                waits=[lastout])
        p.barrier()
        return _finish(nc, p, blk)


def _finish(nc, p, blk):
    p.barrier()
    for ename in ENGINES:
        stream = p.stream[ename]

        def runner(E, stream=stream):
            for fn in stream:
                fn(E)
        getattr(blk, ename)(runner)
    return nc


def host_inputs(full_inputs):
    bf = ml_dtypes.bfloat16
    x = np.ascontiguousarray(full_inputs["L0_attn_x"], dtype=np.float32)
    wqkvT = np.ascontiguousarray(np.asarray(full_inputs["L0_attn_w_qkv"], np.float32).T)
    wpT = np.ascontiguousarray(np.asarray(full_inputs["L0_attn_w_proj"], np.float32).T)
    w1T = np.ascontiguousarray(np.asarray(full_inputs["L0_ffn_W1"], np.float32).T)
    w2T = np.ascontiguousarray(np.asarray(full_inputs["L0_ffn_W2"], np.float32).T)
    w3T = np.ascontiguousarray(np.asarray(full_inputs["L0_ffn_W3"], np.float32).T)
    b1c = np.ascontiguousarray(np.asarray(full_inputs["L0_ffn_b1"], np.float32).reshape(32, 128).T)
    b2c = np.ascontiguousarray(np.asarray(full_inputs["L0_ffn_b2"], np.float32).reshape(32, 128).T)
    b3r = np.ascontiguousarray(np.asarray(full_inputs["L0_ffn_b3"], np.float32).reshape(1, D))
    identt = np.eye(128, dtype=bf)
    ones_row = np.ones((1, 128), dtype=np.float32)

    inv = 1.0 / 10000.0 ** (np.arange(0, DH, 2, dtype=np.float64) / DH)
    rope = {}
    maskh = {}
    for h in range(2):
        start = h * T
        pos = start - 512 + np.arange(TK, dtype=np.float64)
        fr = pos[:, None] * inv[None, :]
        cos = np.repeat(np.cos(fr), 2, axis=-1)
        sin = np.repeat(np.sin(fr), 2, axis=-1)
        sgn = np.ones(DH)
        sgn[0::2] = -1.0
        ssgn = sin * sgn[None, :]
        rope[h] = (
            np.ascontiguousarray(cos.reshape(NTK, 128, DH).transpose(1, 0, 2)
                                 .reshape(128, NTK * DH).astype(np.float32)),
            np.ascontiguousarray(ssgn.reshape(NTK, 128, DH).transpose(1, 0, 2)
                                 .reshape(128, NTK * DH).astype(np.float32)))
        r = np.arange(128)[None, :]
        kc = np.arange(128)[:, None]
        mk = np.zeros((128, 5 * 640), dtype=bf)
        for m in range(5):
            for j in range(5):
                c = j * 128 + kc
                valid = (c > r) & (c <= r + 512)
                if h == 0 and m < 4:
                    valid = valid & (m * 128 + c >= 512)
                mk[:, m * 640 + j * 128:m * 640 + (j + 1) * 128] = valid.astype(bf)
        maskh[h] = mk

    in_maps = []
    for core in range(8):
        b, h = core // 2, core % 2
        x_own = np.ascontiguousarray(x[b, h * T:(h + 1) * T])
        x_ctx = (np.ascontiguousarray(x[b, T - 512:T]) if h == 1
                 else np.zeros((512, D), np.float32))
        in_maps.append({
            "x_own": x_own, "x_ctx": x_ctx,
            "wqkvT": wqkvT, "wpT": wpT, "w1T": w1T, "w2T": w2T, "w3T": w3T,
            "b1c": b1c, "b2c": b2c, "b3r": b3r,
            "masks": maskh[h], "ropeC": rope[h][0], "ropeS": rope[h][1],
            "ident": identt, "ones_row": ones_row,
        })
    return in_maps


_CACHE = {}


def _get(debug=False, phases="full"):
    key = (debug, phases)
    if key not in _CACHE:
        _CACHE[key] = build(debug, phases)
    return _CACHE[key]


def kernel(**inputs):
    nc = _get(False, "full")
    in_maps = host_inputs(inputs)
    res = run_bass_kernel_spmd(nc, in_maps, core_ids=list(range(8)))
    out = np.empty((4, 4096, D), dtype=np.float32)
    for core in range(8):
        b, h = core // 2, core % 2
        out[b, h * T:(h + 1) * T] = res.results[core]["y"]
    return out


# version 09
# speedup vs baseline: 1.1041x; 1.0428x over previous
import numpy as np
import ml_dtypes
from contextlib import ExitStack

import concourse.bass as bass
import concourse.mybir as mybir
from concourse.bass_utils import run_bass_kernel_spmd

F32 = mybir.dt.float32
BF16 = mybir.dt.bfloat16
AF = mybir.ActivationFunctionType
ALU = mybir.AluOpType
AX = mybir.AxisListType

T = 2048
TK = 2560
D = 2048
DH = 128
HQ = 16
G = 4
RMS_EPS = 1.1920929e-07
LN_EPS = 1e-5
NT = 16
NTK = 20
KC = 16
QBIAS = float(-0.5 * np.log(128.0))
EXPBIAS = -16.0

ENGINES = ("tensor", "vector", "scalar", "gpsimd", "sync")


class Prog:
    def __init__(self):
        self.stream = {e: [] for e in ENGINES}
        self.sem = {}
        self.count = {}
        self.lastwait = {}

    def add_sem(self, name, handle):
        self.sem[name] = handle
        self.count[name] = 0

    def wait(self, eng, token):
        if token is None:
            return
        name, val = token
        if val <= 0:
            return
        if self.lastwait.get((eng, name), 0) >= val:
            return
        self.lastwait[(eng, name)] = val
        h = self.sem[name]
        self.stream[eng].append(lambda E, h=h, v=val: E.wait_ge(h, v))

    def op(self, eng, fn, waits=(), inc="default", amt=1):
        for t in waits:
            self.wait(eng, t)
        if inc == "default":
            inc = eng if eng != "sync" else None
        tok = None
        if inc is not None:
            self.count[inc] += amt
            tok = (inc, self.count[inc])
            h = self.sem[inc]
            self.stream[eng].append(
                lambda E, fn=fn, h=h, a=amt: fn(E).then_inc(h, a))
        else:
            self.stream[eng].append(lambda E, fn=fn: fn(E))
        return tok

    def dma(self, eng, qname, out, in_, waits=()):
        return self.op(eng, lambda E: E.dma_start(out=out, in_=in_),
                       waits=waits, inc=qname, amt=16)

    def barrier(self):
        snapshot = [(n, c) for n, c in self.count.items() if c > 0]
        for e in ENGINES:
            for n, c in snapshot:
                self.wait(e, (n, c))

    def token(self, name):
        return (name, self.count[name])


class Arena:

    def __init__(self, nc, size_bytes):
        self.nc = nc
        self.size = size_bytes
        self.pend = []

    def alloc(self, name, shape, dtype, off):
        per_part = int(np.prod(shape[1:])) * mybir.dt.size(dtype)
        assert off % 32 == 0, (name, off)
        assert off + per_part <= self.size, (name, off, per_part)
        hnd = self.nc.virtual_sbuf_tensor(name, shape, dtype)
        self.pend.append((hnd, off))
        return hnd

    def materialize(self, ctx):
        handle = ctx.enter_context(
            self.nc.sbuf_tensor("arena", [128, self.size], mybir.dt.uint8))
        base = self.nc.lookup_mloc(handle).addr
        for hnd, off in self.pend:
            self.nc._tensor(hnd.name, hnd.shape, hnd.dtype, type="SB")
            mloc = self.nc.lookup_mloc(hnd)
            mloc.allocated = True
            mloc.addr = base + off


def _ev(ap_):
    return ap_.rearrange("p (a two) -> p a two", two=2)[:, :, 0]


def _od(ap_):
    return ap_.rearrange("p (a two) -> p a two", two=2)[:, :, 1]


ARENA = 206144
A0_SERIAL = False


def build(debug=False, phases="full"):
    nc = bass.Bass("TRN2", target_bir_lowering=False)

    dx = nc.dram_tensor("x_own", [T, D], F32, kind="ExternalInput").ap()
    dxc = nc.dram_tensor("x_ctx", [512, D], F32, kind="ExternalInput").ap()
    dwqkvT = nc.dram_tensor("wqkvT", [D, 3072], F32, kind="ExternalInput").ap()
    dwpT = nc.dram_tensor("wpT", [D, D], F32, kind="ExternalInput").ap()
    dw1T = nc.dram_tensor("w1T", [D, 4096], F32, kind="ExternalInput").ap()
    dw2T = nc.dram_tensor("w2T", [4096, 4096], F32, kind="ExternalInput").ap()
    dw3T = nc.dram_tensor("w3T", [4096, D], F32, kind="ExternalInput").ap()
    db1c = nc.dram_tensor("b1c", [128, 32], F32, kind="ExternalInput").ap()
    db2c = nc.dram_tensor("b2c", [128, 32], F32, kind="ExternalInput").ap()
    db3r = nc.dram_tensor("b3r", [1, D], F32, kind="ExternalInput").ap()
    dmask = nc.dram_tensor("masks", [128, 5 * 640], BF16, kind="ExternalInput").ap()
    dropeC = nc.dram_tensor("ropeC", [128, NTK * DH], F32, kind="ExternalInput").ap()
    dropeS = nc.dram_tensor("ropeS", [128, NTK * DH], F32, kind="ExternalInput").ap()
    dident = nc.dram_tensor("ident", [128, 128], BF16, kind="ExternalInput").ap()
    dones = nc.dram_tensor("ones_row", [1, 128], F32, kind="ExternalInput").ap()

    dy = nc.dram_tensor("y", [T, D], F32, kind="ExternalOutput").ap()
    sck = "ExternalOutput" if debug else "Internal"
    dscY = nc.dram_tensor("scY", [HQ, 128, T], BF16, kind=sck).ap()
    dscXM = nc.dram_tensor("scXM", [T, D], F32, kind=sck).ap()
    dscH2 = nc.dram_tensor("scH2", [32, 128, T], BF16, kind=sck).ap()
    if debug:
        drr = nc.dram_tensor("dbg_rr", [NT, 128, 32], F32, kind="ExternalOutput").ap()
        dsqb = nc.dram_tensor("dbg_sqb", [NT, 128, 512], F32, kind="ExternalOutput").ap()
        dln1 = nc.dram_tensor("dbg_ln1T", [128, KC * TK], BF16, kind="ExternalOutput").ap()
        dqn = nc.dram_tensor("dbg_qn", [G, 128, 4 * T], BF16, kind="ExternalOutput").ap()
        dkn = nc.dram_tensor("dbg_kn", [G, 128, TK], BF16, kind="ExternalOutput").ap()
        dvt = nc.dram_tensor("dbg_vt", [G, 128, TK], BF16, kind="ExternalOutput").ap()
        dln2 = nc.dram_tensor("dbg_ln2T", [128, KC * T], BF16, kind="ExternalOutput").ap()
        dh1d = nc.dram_tensor("dbg_h1", [128, 32 * T], BF16, kind="ExternalOutput").ap()

    p = Prog()
    ar = Arena(nc, ARENA)

    ident = ar.alloc("identt", [128, 128], BF16, 0)
    onesr = ar.alloc("onesrr", [1, 128], F32, 256)
    b1c = ar.alloc("b1cc", [128, 32], F32, 768)
    b2c = ar.alloc("b2cc", [128, 32], F32, 896)
    constv = ar.alloc("constv", [128, 8], F32, 1024)
    P0 = 1088
    ln1T = ar.alloc("ln1T", [128, KC * TK], BF16, P0)
    A2R = P0 + 81920
    cur = A2R

    def nxt(sz):
        nonlocal cur
        off = cur
        cur += sz
        assert cur <= ARENA
        return off
    ropeC = ar.alloc("ropeCs", [128, NTK * DH], F32, nxt(10240))
    ropeS = ar.alloc("ropeSs", [128, NTK * DH], F32, nxt(10240))
    masks = ar.alloc("maskss", [128, 5 * 640], BF16, nxt(6400))
    A0ALIAS = cur
    wTq = ar.alloc("wTq", [128, KC * 512], BF16, nxt(16384))
    wTkv = ar.alloc("wTkv", [128, KC * 256], BF16, nxt(8192))
    stg = ar.alloc("stg", [128, 1024], F32, nxt(4096))
    sqb = ar.alloc("sqb", [128, 1024], F32, nxt(4096))
    rr = ar.alloc("rr", [128, 32], F32, nxt(128))
    ropet = ar.alloc("ropet", [128, 256], F32, nxt(1024))
    ntm = ar.alloc("ntm", [128, 512], BF16, nxt(1024))
    qnf = ar.alloc("qnf", [128, 4 * T], BF16, nxt(16384))
    knf = ar.alloc("knf", [128, TK], BF16, nxt(5120))
    vtm = ar.alloc("vtm", [128, TK], BF16, nxt(5120))
    pbuf = ar.alloc("pbuf", [128, 4 * 640], BF16, nxt(5120))
    onescol = ar.alloc("onescol", [128, 1], BF16, nxt(32))
    onesb = ar.alloc("onesb", [1, 128], BF16, nxt(256))
    recipr = ar.alloc("recipr", [1, 512], BF16, nxt(1024))
    yevb = ar.alloc("yevb", [128, 2 * 512], BF16, nxt(2048))
    attst = ar.alloc("attst", [128, 8], F32, nxt(32))
    assert cur <= ARENA
    a0x = ar.alloc("a0x", [128, 2 * D], F32, A0ALIAS)
    a0ln = ar.alloc("a0ln", [128, D], BF16, A0ALIAS + 16384)
    a0st = ar.alloc("a0st", [128, 64], F32, A0ALIAS + 20480)
    wpT = ar.alloc("wpTs", [128, KC * D], BF16, P0)
    A3B = P0 + 65536
    a3yld = ar.alloc("a3yld", [128, 2 * HQ * 256], BF16, A3B)
    a3y2 = ar.alloc("a3y2", [128, D], F32, A3B + 16384)
    a3xr = ar.alloc("a3xr", [128, 2 * D], F32, A3B + 24576)
    a3xm = ar.alloc("a3xm", [128, D], F32, A3B + 40960)
    a3ln2 = ar.alloc("a3ln2", [128, D], BF16, A3B + 49152)
    a3st = ar.alloc("a3st", [128, 128], F32, A3B + 53248)
    LN2T = P0 + 131072
    ln2T = ar.alloc("ln2T", [128, KC * T], BF16, LN2T)
    h1 = ar.alloc("h1s", [128, 32 * T], BF16, P0)
    B1B = LN2T + 65536
    wT1 = ar.alloc("wT1", [128, 2 * KC * 128], BF16, B1B)
    wT2 = ar.alloc("wT2", [128, 2 * 32 * 128], BF16, LN2T)
    h2b = ar.alloc("h2b", [128, 2 * D], BF16, LN2T + 16384)
    w3Ts = ar.alloc("w3Ts", [128, 32 * T], BF16, P0)
    b3h2 = ar.alloc("b3h2", [128, 2 * 32 * 256], BF16, LN2T)
    b3y3 = ar.alloc("b3y3", [128, D], F32, LN2T + 32768)
    b3xm = ar.alloc("b3xm", [128, D], F32, LN2T + 40960)
    b3bc = ar.alloc("b3bc", [128, D], F32, LN2T + 49152)
    b3st = ar.alloc("b3st", [128, 64], F32, LN2T + 57344)
    b3r = ar.alloc("b3rr", [1, D], F32, LN2T + 57600)
    b3ob = ar.alloc("b3ob", [128, D], F32, LN2T + 65792)

    ctx = ExitStack()
    with ctx:
        for s in ("tensor", "vector", "scalar", "gpsimd", "q_in", "q_wt", "q_out"):
            p.add_sem(s, ctx.enter_context(nc.semaphore(s)))
        ar.materialize(ctx)
        psQ = ctx.enter_context(nc.psum_tensor("psQ", [128, 1024], F32))
        psS = ctx.enter_context(nc.psum_tensor("psS", [128, 1024], F32))
        psP = ctx.enter_context(nc.psum_tensor("psP", [128, 1024], F32))
        psY = ctx.enter_context(nc.psum_tensor("psY", [128, 512], F32))
        blk = ctx.enter_context(nc.Block())

        def emit_ln(stbuf, c0, src, waits, out_ap, out_src=None, extra_ap_waits=()):
            stt = None
            for chn in range(4):
                stt = p.op("vector", lambda E, chn=chn: E.bn_stats(
                    out=stbuf[:, c0 + chn * 6:c0 + chn * 6 + 6],
                    in_=src[:, chn * 512:(chn + 1) * 512]),
                    waits=(list(waits) if chn == 0 else []))
            agg = p.op("vector", lambda E: E.bn_aggr(
                out=stbuf[:, c0 + 24:c0 + 26], in_=stbuf[:, c0:c0 + 24]), waits=[stt])
            l_ = p.op("scalar", lambda E: E.activation(
                out=stbuf[:, c0 + 26:c0 + 27], in_=stbuf[:, c0 + 25:c0 + 26],
                func=AF.Ln, bias=LN_EPS), waits=[agg])
            e_ = p.op("scalar", lambda E: E.activation(
                out=stbuf[:, c0 + 27:c0 + 28], in_=stbuf[:, c0 + 26:c0 + 27],
                func=AF.Exp, scale=-0.5), waits=[l_])
            nm_ = p.op("vector", lambda E: E.scalar_tensor_tensor(
                out=stbuf[:, c0 + 28:c0 + 29], in0=stbuf[:, c0 + 24:c0 + 25],
                scalar=-1.0, in1=stbuf[:, c0 + 27:c0 + 28],
                op0=ALU.mult, op1=ALU.mult), waits=[e_])
            src2 = out_src if out_src is not None else src
            ap_ = p.op("vector", lambda E: E.tensor_scalar(
                out_ap, src2, stbuf[:, c0 + 27:c0 + 28], stbuf[:, c0 + 28:c0 + 29],
                ALU.mult, ALU.add), waits=[nm_] + list(extra_ap_waits))
            return ap_

        CONSTS = (LN_EPS, RMS_EPS, QBIAS, -0.5, 1.0 / 128.0, EXPBIAS)
        cst = None
        for k, v in enumerate(CONSTS):
            nc.const_aps.register(F32, v, constv[:, k:k + 1])
            cst = p.op("vector", lambda E, k=k, v=v: E.memset(constv[:, k:k + 1], v))
        cst = p.op("vector", lambda E: E.memset(onescol[:], 1.0))
        cst = p.op("vector", lambda E: E.memset(onesb[:], 1.0))
        for e in ENGINES:
            p.wait(e, cst)

        p.dma("sync", "q_in", ident[:], dident)
        p.dma("sync", "q_in", onesr[:], dones)
        p.dma("sync", "q_in", b1c[:], db1c)
        p.dma("sync", "q_in", b2c[:], db2c)
        tok_id = p.token("q_in")
        p.dma("sync", "q_in", ropeC[:], dropeC)
        p.dma("sync", "q_in", ropeS[:], dropeS)
        p.dma("sync", "q_in", masks[:], dmask)
        tok_consts = p.token("q_in")

        ln1Tv = ln1T[:].rearrange("p (kc t) -> p kc t", t=TK)
        xv = dx.rearrange("(i p) d -> i p d", p=128)
        xcv = dxc.rearrange("(i p) d -> i p d", p=128)

        ldtok = [None] * NTK
        evtok = [None] * NTK

        def a0_load(jk):
            src = xcv[jk] if jk < 4 else xv[jk - 4]
            eng, q = (("sync", "q_in") if jk % 2 == 0 else ("scalar", "q_out"))
            ldtok[jk] = p.dma(eng, q,
                              a0x[:, (jk % 2) * D:(jk % 2) * D + D], src,
                              waits=[evtok[jk - 2]] if jk >= 2 else [])
        a0_load(0)
        a0_load(1)
        prev_tr2 = None
        for jk in range(NTK):
            xt = a0x[:, (jk % 2) * D:(jk % 2) * D + D]
            c0 = (jk % 2) * 30
            aptok = emit_ln(a0st, c0, xt, [ldtok[jk]], a0ln[:],
                            extra_ap_waits=[prev_tr2])
            tr1 = tr2 = None
            for j in range(KC):
                pst = psS if j < 8 else psP
                tr = p.op("tensor", lambda E, j=j, pst=pst: E.matmul(
                    pst[:, (j % 8) * 128:(j % 8) * 128 + 128],
                    lhsT=a0ln[:, j * 128:(j + 1) * 128], rhs=ident[:],
                    start=True, stop=True),
                    waits=([aptok, tok_id, evtok[jk - 1] if jk >= 1 else None]
                           if j == 0 else []))
                if j == 7:
                    tr1 = tr
                if j == 15:
                    tr2 = tr
            prev_tr2 = tr2
            e1 = p.op("vector", lambda E, jk=jk: E.tensor_copy(
                out=ln1Tv[:, 0:8, jk * 128:(jk + 1) * 128],
                in_=psS[:].rearrange("p (kc c) -> p kc c", c=128)), waits=[tr1])
            evtok[jk] = p.op("vector", lambda E, jk=jk: E.tensor_copy(
                out=ln1Tv[:, 8:16, jk * 128:(jk + 1) * 128],
                in_=psP[:].rearrange("p (kc c) -> p kc c", c=128)), waits=[tr2, e1])
            if jk + 2 < NTK:
                a0_load(jk + 2)
            if A0_SERIAL:
                p.barrier()
        p.barrier()
        if debug:
            p.dma("sync", "q_out", dln1, ln1T[:])
            p.barrier()
        if phases == "A0":
            return _finish(nc, p, blk)

        wTqv = wTq[:].rearrange("p (kc f) -> p kc f", f=512)
        wTkvv = wTkv[:].rearrange("p (kc f) -> p kc f", f=256)
        qnfv = qnf[:].rearrange("p (hh t) -> p hh t", t=T)
        maskv = masks[:].rearrange("p (i c) -> p i c", c=640)
        wqkvTv = dwqkvT.rearrange("(kc p) f -> p kc f", p=128)

        for g in range(G):
            wtoks = [
                p.dma("gpsimd", "q_wt", wTqv[:, 0:8, :],
                      wqkvTv[:, 0:8, 512 * g:512 * g + 512]),
                p.dma("gpsimd", "q_wt", wTqv[:, 8:16, :],
                      wqkvTv[:, 8:16, 512 * g:512 * g + 512]),
                p.dma("gpsimd", "q_wt", wTkvv[:, :, 0:128],
                      wqkvTv[:, :, 2048 + 128 * g:2048 + 128 * g + 128]),
                p.dma("gpsimd", "q_wt", wTkvv[:, :, 128:256],
                      wqkvTv[:, :, 2560 + 128 * g:2560 + 128 * g + 128]),
            ]
            wtok = wtoks[-1]

            kn_done = [None] * NTK
            qn_done = [None] * NT
            stg_free = [None, None]
            sq_free = [None, None]
            prev_cpk = None
            prev_cpq = None
            prev_kev = None
            prev_qev = None
            prev_trk = None
            prev_trq = None
            for jk in range(NTK):
                half = jk % 2
                stgk = stg[:, half * 512:half * 512 + 256]
                mmk = None
                for kcc in range(KC):
                    mmk = p.op("tensor", lambda E, kcc=kcc, jk=jk: E.matmul(
                        psQ[:, 512:768],
                        lhsT=ln1Tv[:, kcc, jk * 128:(jk + 1) * 128],
                        rhs=wTkvv[:, kcc, :],
                        start=(kcc == 0), stop=(kcc == KC - 1)),
                        waits=([wtok, prev_cpk] if kcc == 0 else []))
                cpk = p.op("vector", lambda E, stgk=stgk: E.tensor_copy(
                    out=stgk, in_=psQ[:, 512:768]),
                    waits=[mmk, stg_free[half], sq_free[half]])
                prev_cpk = cpk
                sqk = p.op("scalar", lambda E, stgk=stgk, half=half: E.activation(
                    out=sqb[:, 512 + half * 128:512 + half * 128 + 128],
                    in_=stgk[:, 0:128], func=AF.Square), waits=[cpk])
                red = p.op("vector", lambda E, half=half: E.reduce_sum(
                    out=rr[:, half:half + 1],
                    in_=sqb[:, 512 + half * 128:512 + half * 128 + 128],
                    axis=AX.X), waits=[sqk])
                ln_ = p.op("scalar", lambda E, half=half: E.activation(
                    out=rr[:, 2 + half:3 + half], in_=rr[:, half:half + 1],
                    func=AF.Ln, scale=1.0 / 128.0, bias=RMS_EPS), waits=[red])
                exr = p.op("scalar", lambda E, half=half: E.activation(
                    out=rr[:, 4 + half:5 + half], in_=rr[:, 2 + half:3 + half],
                    func=AF.Exp, scale=-0.5), waits=[ln_])
                rK = rr[:, 4 + half:5 + half]
                cs = ropeC[:, jk * 128:(jk + 1) * 128]
                ss = ropeS[:, jk * 128:(jk + 1) * 128]
                t1 = ropet[:, 0:128]
                t2 = ropet[:, 128:256]
                ktm = ntm[:, 0:128]
                a_ = p.op("vector", lambda E, stgk=stgk, t1=t1, cs=cs, rK=rK:
                          E.scalar_tensor_tensor(out=t1, in0=stgk[:, 0:128], scalar=rK,
                                                 in1=cs, op0=ALU.mult, op1=ALU.mult),
                          waits=[exr, tok_consts])
                b_ = p.op("vector", lambda E, stgk=stgk, t2=t2, ss=ss, rK=rK:
                          E.scalar_tensor_tensor(out=_ev(t2), in0=_od(stgk[:, 0:128]),
                                                 scalar=rK, in1=_ev(ss),
                                                 op0=ALU.mult, op1=ALU.mult), waits=[exr])
                c_ = p.op("vector", lambda E, stgk=stgk, t2=t2, ss=ss, rK=rK:
                          E.scalar_tensor_tensor(out=_od(t2), in0=_ev(stgk[:, 0:128]),
                                                 scalar=rK, in1=_od(ss),
                                                 op0=ALU.mult, op1=ALU.mult), waits=[exr])
                d_ = p.op("vector", lambda E, t1=t1, t2=t2, ktm=ktm: E.tensor_add(
                    out=ktm, in0=t1, in1=t2), waits=[a_, b_, c_, prev_trk])
                trk = p.op("tensor", lambda E, ktm=ktm: E.matmul(
                    psP[:, 0:128], lhsT=ktm, rhs=ident[:], start=True, stop=True),
                    waits=[d_, prev_kev])
                prev_trk = trk
                kev = p.op("vector", lambda E, jk=jk: E.tensor_copy(
                    out=knf[:, jk * 128:(jk + 1) * 128], in_=psP[:, 0:128]), waits=[trk])
                prev_kev = kev
                vcp = p.op("vector", lambda E, stgk=stgk, jk=jk: E.tensor_copy(
                    out=vtm[:, jk * 128:(jk + 1) * 128], in_=stgk[:, 128:256]),
                    waits=[cpk])
                kn_done[jk] = vcp
                if jk < 4:
                    stg_free[half] = vcp
                    sq_free[half] = sqk

                if jk >= 4:
                    i = jk - 4
                    qstg = stg[:, half * 512:half * 512 + 512]
                    mmq = None
                    for kcc in range(KC):
                        mmq = p.op("tensor", lambda E, kcc=kcc, jk=jk: E.matmul(
                            psQ[:, 0:512],
                            lhsT=ln1Tv[:, kcc, jk * 128:(jk + 1) * 128],
                            rhs=wTqv[:, kcc, :],
                            start=(kcc == 0), stop=(kcc == KC - 1)),
                            waits=([wtok, prev_cpq] if kcc == 0 else []))
                    cpq = p.op("vector", lambda E, qstg=qstg: E.tensor_copy(
                        out=qstg, in_=psQ[:, 0:512]), waits=[mmq, sqk])
                    prev_cpq = cpq
                    sqq = p.op("scalar", lambda E, qstg=qstg: E.activation(
                        out=sqb[:, 0:512], in_=qstg, func=AF.Square), waits=[cpq])
                    redq = p.op("vector", lambda E: E.reduce_sum(
                        out=rr[:, 8:12], in_=sqb[:, 0:512].rearrange(
                            "p (h c) -> p h c", c=128), axis=AX.X), waits=[sqq])
                    lnq = p.op("scalar", lambda E: E.activation(
                        out=rr[:, 12:16], in_=rr[:, 8:12], func=AF.Ln,
                        scale=1.0 / 128.0, bias=RMS_EPS), waits=[redq])
                    exq = p.op("scalar", lambda E: E.activation(
                        out=rr[:, 16:20], in_=rr[:, 12:16], func=AF.Exp,
                        scale=-0.5, bias=QBIAS), waits=[lnq])
                    if debug and g == 0:
                        p.dma("sync", "q_out", drr[i], rr[:], waits=[exq])
                        p.dma("sync", "q_out", dsqb[i], sqb[:, 0:512], waits=[exq])
                    qev = None
                    for hh in range(4):
                        rQ = rr[:, 16 + hh:17 + hh]
                        qs = qstg[:, hh * 128:hh * 128 + 128]
                        qt1 = ropet[:, 0:128]
                        qt2 = ropet[:, 128:256]
                        qtm = ntm[:, 128:256]
                        a2_ = p.op("vector", lambda E, qs=qs, qt1=qt1, cs=cs, rQ=rQ:
                                   E.scalar_tensor_tensor(out=qt1, in0=qs, scalar=rQ,
                                                          in1=cs, op0=ALU.mult,
                                                          op1=ALU.mult),
                                   waits=[exq, d_])
                        b2_ = p.op("vector", lambda E, qs=qs, qt2=qt2, ss=ss, rQ=rQ:
                                   E.scalar_tensor_tensor(out=_ev(qt2), in0=_od(qs),
                                                          scalar=rQ, in1=_ev(ss),
                                                          op0=ALU.mult, op1=ALU.mult),
                                   waits=[exq])
                        c2_ = p.op("vector", lambda E, qs=qs, qt2=qt2, ss=ss, rQ=rQ:
                                   E.scalar_tensor_tensor(out=_od(qt2), in0=_ev(qs),
                                                          scalar=rQ, in1=_od(ss),
                                                          op0=ALU.mult, op1=ALU.mult),
                                   waits=[exq])
                        d2_ = p.op("vector", lambda E, qt1=qt1, qt2=qt2, qtm=qtm:
                                   E.tensor_add(out=qtm, in0=qt1, in1=qt2),
                                   waits=[a2_, b2_, c2_, prev_trq])
                        trq = p.op("tensor", lambda E, qtm=qtm: E.matmul(
                            psP[:, 128:256], lhsT=qtm, rhs=ident[:], start=True,
                            stop=True), waits=[d2_, prev_qev])
                        prev_trq = trq
                        qev = p.op("vector", lambda E, hh=hh, i=i: E.tensor_copy(
                            out=qnfv[:, hh, i * 128:(i + 1) * 128], in_=psP[:, 128:256]),
                            waits=[trq])
                        prev_qev = qev
                    qn_done[i] = qev
                    stg_free[half] = qev
                    sq_free[half] = sqq
            if debug:
                p.barrier()
                p.dma("sync", "q_out", dqn[g], qnf[:])
                p.dma("sync", "q_out", dkn[g], knf[:])
                p.dma("sync", "q_out", dvt[g], vtm[:])
                p.barrier()

            last_exp = [None, None]
            pb_free = {}
            mask_tok = {}
            last_ye = [None, None]
            last_ydma = [None, None]
            last_rc = None
            last_bc = None
            last_yn = None
            last_dn = None

            def att_head(t):
                hh, i = divmod(t, NT)
                ib = t % 2
                ib4 = t % 4
                pS = psS if ib == 0 else psQ
                st = None
                for j in range(5):
                    st = p.op("tensor", lambda E, hh=hh, i=i, j=j, pS=pS: E.matmul(
                        pS[:, j * 128:(j + 1) * 128],
                        lhsT=knf[:, (i + j) * 128:(i + j + 1) * 128],
                        rhs=qnfv[:, hh, i * 128:(i + 1) * 128],
                        start=True, stop=True),
                        waits=([qn_done[i], kn_done[min(i + 4, NTK - 1)],
                                last_exp[ib]] if j == 0 else []))
                pb = pbuf[:, ib4 * 640:ib4 * 640 + 640]
                ex = p.op("scalar", lambda E, pb=pb, pS=pS: E.activation(
                    out=pb, in_=pS[:, 0:640], func=AF.Exp, bias=EXPBIAS),
                    waits=[st, pb_free.pop(t - 4, None)])
                last_exp[ib] = ex
                if i < 4:
                    mk = p.op("vector", lambda E, pb=pb, i=i: E.tensor_tensor(
                        out=pb, in0=pb, in1=maskv[:, i, :], op=ALU.mult),
                        waits=[ex])
                else:
                    p.op("vector", lambda E, pb=pb: E.tensor_tensor(
                        out=pb[:, 0:128], in0=pb[:, 0:128],
                        in1=maskv[:, 4, 0:128], op=ALU.mult), waits=[ex])
                    mk = p.op("vector", lambda E, pb=pb: E.tensor_tensor(
                        out=pb[:, 512:640], in0=pb[:, 512:640],
                        in1=maskv[:, 4, 512:640], op=ALU.mult))
                mask_tok[t] = mk

            def emit_tail(t):
                nonlocal last_rc, last_bc, last_yn, last_dn
                hh, i = divmod(t, NT)
                head = 4 * g + hh
                ib4 = t % 4
                pb = pbuf[:, ib4 * 640:ib4 * 640 + 640]
                mk = mask_tok.pop(t)
                dn = None
                for j in range(5):
                    dn = p.op("tensor", lambda E, pb=pb, j=j, i=i: E.matmul(
                        psP[0:1, (i % 4) * 128:(i % 4) * 128 + 128],
                        lhsT=onescol[:], rhs=pb[:, j * 128:(j + 1) * 128],
                        start=(j == 0), stop=(j == 4)),
                        waits=([mk] + ([last_rc] if i % 4 == 0 else [])
                               if j == 0 else []))
                last_dn = dn
                mv = None
                for j in range(5):
                    mv = p.op("tensor", lambda E, pb=pb, i=i, j=j: E.matmul(
                        psY[:, (i % 4) * 128:(i % 4) * 128 + 128],
                        lhsT=vtm[:, (i + j) * 128:(i + j) * 128 + 128],
                        rhs=pb[:, j * 128:(j + 1) * 128],
                        start=(i % 4 == 0 and j == 0),
                        stop=(i % 4 == 3 and j == 4)),
                        waits=([mk] + ([last_ye[(i // 4) % 2]]
                                       if i % 4 == 0 else [])
                               if j == 0 else []))
                pb_free[t] = mv
                if i % 4 == 3:
                    q4 = i // 4
                    gpar = q4 % 2
                    yv_ = yevb[:, gpar * 512:gpar * 512 + 512]
                    ye = p.op("vector", lambda E, yv_=yv_: E.tensor_copy(
                        out=yv_, in_=psY[:]), waits=[mv, last_ydma[gpar]])
                    last_ye[gpar] = ye
                    def _rc(E):
                        with nc.allow_low_precision(reason="bf16 recip row"):
                            return E.reciprocal(out=recipr[:],
                                                in_=psP[0:1, 0:512])
                    rc = p.op("vector", _rc, waits=[dn, last_bc])
                    last_rc = rc
                    bc = p.op("tensor", lambda E: E.matmul(
                        psP[:, 512:1024], lhsT=onesb[:], rhs=recipr[:],
                        start=True, stop=True), waits=[rc, last_yn])
                    last_bc = bc
                    yn = p.op("vector", lambda E, yv_=yv_: E.tensor_tensor(
                        out=yv_, in0=yv_, in1=psP[:, 512:1024], op=ALU.mult),
                        waits=[ye, bc])
                    last_yn = yn
                    last_ydma[gpar] = p.dma(
                        "scalar", "q_out", dscY[head][:, q4 * 512:(q4 + 1) * 512],
                        yv_, waits=[yn])

            for t in range(4 * NT):
                if t >= 3:
                    emit_tail(t - 3)
                att_head(t)
            for t in range(4 * NT - 3, 4 * NT):
                emit_tail(t)
            p.barrier()

        if phases == "A2":
            return _finish(nc, p, blk)

        wpTv = wpT[:].rearrange("p (h f) -> p h f", f=D)
        dwpTv = dwpT.rearrange("(h p) f -> p h f", p=128)
        wl = None
        for c4 in range(4):
            wl = p.dma("gpsimd", "q_wt", wpTv[:, c4 * 4:(c4 + 1) * 4, :],
                       dwpTv[:, c4 * 4:(c4 + 1) * 4, :])
        wptok = wl
        scYv = dscY.rearrange("h p t -> p h t")
        a3yldv = a3yld[:].rearrange("p (b h c) -> p b h c", b=2, h=HQ)
        xmv = dscXM.rearrange("(i p) d -> i p d", p=128)
        ln2Tv = ln2T[:].rearrange("p (kc t) -> p kc t", t=T)
        lastxr = [None, None]
        prev_add = None
        prev_xmdma = None
        prev_le2 = None
        yld_dma = [None, None]
        evs_prev = [None] * 4
        for i in range(NT):
            ib2 = (i // 2) % 2
            if i % 2 == 0:
                yld_dma[ib2] = p.dma(
                    "sync", "q_in", a3yldv[:, ib2, :, :],
                    scYv[:, :, i * 128:i * 128 + 256],
                    waits=[prev_add] if i >= 4 else [])
            xrt = p.dma("sync", "q_in", a3xr[:, (i % 2) * D:(i % 2) * D + D],
                        xv[i], waits=[lastxr[i % 2]])
            chunk_mm = []
            for oc in range(4):
                pst = psS if oc < 2 else psP
                cols = (oc % 2) * 512
                mm = None
                for h2 in range(HQ):
                    mm = p.op("tensor", lambda E, h2=h2, oc=oc, i=i, pst=pst, cols=cols:
                              E.matmul(pst[:, cols:cols + 512],
                                       lhsT=a3yldv[:, (i // 2) % 2, h2,
                                                   (i % 2) * 128:(i % 2) * 128 + 128],
                                       rhs=wpTv[:, h2, oc * 512:(oc + 1) * 512],
                                       start=(h2 == 0), stop=(h2 == HQ - 1)),
                              waits=([yld_dma[(i // 2) % 2], wptok, evs_prev[oc]]
                                     if h2 == 0 else []))
                chunk_mm.append(mm)
            evs = []
            for oc in range(4):
                pst = psS if oc < 2 else psP
                cols = (oc % 2) * 512
                evs.append(p.op("vector", lambda E, oc=oc, pst=pst, cols=cols:
                                E.tensor_copy(out=a3y2[:, oc * 512:(oc + 1) * 512],
                                              in_=pst[:, cols:cols + 512]),
                                waits=[chunk_mm[oc]]))
                evs_prev[oc] = evs[oc]
            ap_ = emit_ln(a3st, (i % 2) * 30, a3y2[:], evs, a3y2[:])
            xr = a3xr[:, (i % 2) * D:(i % 2) * D + D]
            add = p.op("vector", lambda E, xr=xr: E.tensor_add(
                out=a3xm[:], in0=xr, in1=a3y2[:]),
                waits=[ap_, xrt, prev_xmdma])
            prev_add = add
            lastxr[i % 2] = add
            xmdma = p.dma("scalar", "q_out", xmv[i], a3xm[:], waits=[add])
            prev_xmdma = xmdma
            ap2 = emit_ln(a3st, 60 + (i % 2) * 30, a3xm[:], [add], a3ln2[:],
                          extra_ap_waits=[prev_le2])
            le1 = prev_le2
            for r in range(2):
                tr = None
                for j in range(8):
                    tr = p.op("tensor", lambda E, r=r, j=j: E.matmul(
                        psQ[:, j * 128:(j + 1) * 128],
                        lhsT=a3ln2[:, (r * 8 + j) * 128:(r * 8 + j + 1) * 128],
                        rhs=ident[:], start=True, stop=True),
                        waits=([ap2, le1] if j == 0 else []))
                le1 = p.op("vector", lambda E, i=i, r=r: E.tensor_copy(
                    out=ln2Tv[:, r * 8:r * 8 + 8, i * 128:(i + 1) * 128],
                    in_=psQ[:].rearrange("p (kc c) -> p kc c", c=128)), waits=[tr])
            prev_le2 = le1
        p.barrier()
        if debug:
            p.dma("sync", "q_out", dln2, ln2T[:])
            p.barrier()
        if phases == "A3":
            return _finish(nc, p, blk)

        w1Tv = dw1T.rearrange("(kc p) f -> p kc f", p=128)
        wT1v = wT1[:].rearrange("p (b kc c) -> p b kc c", b=2, c=128)
        h1v = h1[:].rearrange("p (m t) -> p m t", t=T)
        wtk = [None, None]
        rel_tok = [None, None]
        prev_rel_chunk = [None] * 4
        for m in range(32):
            mb = m % 2
            wtk[mb] = p.dma("gpsimd", "q_wt", wT1v[:, mb, :, :],
                            w1Tv[:, :, m * 128:(m + 1) * 128],
                            waits=[rel_tok[mb]])
            rel = None
            for chn in range(4):
                pst = psS if chn < 2 else psP
                cols = (chn % 2) * 512
                mm = None
                for kcc in range(KC):
                    mm = p.op("tensor", lambda E, kcc=kcc, chn=chn, pst=pst, cols=cols,
                              mb=mb: E.matmul(
                                  pst[:, cols:cols + 512],
                                  lhsT=wT1v[:, mb, kcc, :],
                                  rhs=ln2Tv[:, kcc, chn * 512:(chn + 1) * 512],
                                  start=(kcc == 0), stop=(kcc == KC - 1)),
                              waits=([wtk[mb], prev_rel_chunk[chn]] if kcc == 0 else []))
                rel = p.op("scalar", lambda E, m=m, chn=chn, pst=pst, cols=cols:
                           E.activation(out=h1v[:, m, chn * 512:(chn + 1) * 512],
                                        in_=pst[:, cols:cols + 512],
                                        func=AF.Relu, bias=b1c[:, m:m + 1]),
                           waits=[mm, tok_id])
                prev_rel_chunk[chn] = rel
            rel_tok[mb] = rel
        p.barrier()
        if debug:
            p.dma("sync", "q_out", dh1d, h1[:])
            p.barrier()
        if phases == "B1":
            return _finish(nc, p, blk)

        w2Tv = dw2T.rearrange("(kc p) f -> p kc f", p=128)
        wT2v = wT2[:].rearrange("p (b kc c) -> p b kc c", b=2, c=128)
        wtk = [None, None]
        rel_tok = [None, None]
        h2dma = [None, None]
        prev_rel_chunk = [None] * 4
        for m in range(32):
            mb = m % 2
            p.dma("gpsimd", "q_wt", wT2v[:, mb, 0:16, :],
                  w2Tv[:, 0:16, m * 128:(m + 1) * 128], waits=[rel_tok[mb]])
            wtk[mb] = p.dma("gpsimd", "q_wt", wT2v[:, mb, 16:32, :],
                            w2Tv[:, 16:32, m * 128:(m + 1) * 128])
            rel = None
            for chn in range(4):
                pst = psS if chn < 2 else psP
                cols = (chn % 2) * 512
                mm = None
                for kcc in range(32):
                    mm = p.op("tensor", lambda E, kcc=kcc, chn=chn, pst=pst, cols=cols,
                              mb=mb: E.matmul(
                                  pst[:, cols:cols + 512],
                                  lhsT=wT2v[:, mb, kcc, :],
                                  rhs=h1v[:, kcc, chn * 512:(chn + 1) * 512],
                                  start=(kcc == 0), stop=(kcc == 31)),
                              waits=([wtk[mb], prev_rel_chunk[chn]] if kcc == 0 else []))
                rel = p.op("scalar", lambda E, chn=chn, pst=pst, cols=cols, mb=mb, m=m:
                           E.activation(out=h2b[:, mb * D + chn * 512:mb * D + (chn + 1) * 512],
                                        in_=pst[:, cols:cols + 512],
                                        func=AF.Relu, bias=b2c[:, m:m + 1]),
                           waits=([mm, h2dma[mb]] if chn == 0 else [mm]))
                prev_rel_chunk[chn] = rel
            rel_tok[mb] = rel
            h2dma[mb] = p.dma("scalar", "q_out", dscH2[m], h2b[:, mb * D:mb * D + D],
                              waits=[rel])
        p.barrier()
        if phases == "B2":
            return _finish(nc, p, blk)

        w3Tv = dw3T.rearrange("(kc p) f -> p kc f", p=128)
        w3sv = w3Ts[:].rearrange("p (kc f) -> p kc f", f=T)
        w3tok_half = None
        w3tok = None
        for c8 in range(4):
            w3tok = p.dma("gpsimd", "q_wt", w3sv[:, c8 * 8:(c8 + 1) * 8, :],
                          w3Tv[:, c8 * 8:(c8 + 1) * 8, :])
            if c8 == 1:
                w3tok_half = w3tok
        b3ld = p.dma("sync", "q_in", b3r[:], db3r)
        btoks = []
        for oc in range(4):
            mmb = p.op("tensor", lambda E, oc=oc: E.matmul(
                psQ[:, (oc % 2) * 512:(oc % 2) * 512 + 512], lhsT=onesr[:],
                rhs=b3r[:, oc * 512:(oc + 1) * 512], start=True, stop=True),
                waits=[b3ld, tok_id, btoks[oc - 2] if oc >= 2 else None])
            btoks.append(p.op("vector", lambda E, oc=oc: E.tensor_copy(
                out=b3bc[:, oc * 512:(oc + 1) * 512],
                in_=psQ[:, (oc % 2) * 512:(oc % 2) * 512 + 512]), waits=[mmb]))
        b3tok = btoks[-1]
        scH2v = dscH2.rearrange("m p t -> p m t")
        b3h2v = b3h2[:].rearrange("p (b m c) -> p b m c", b=2, m=32)
        h2ld = [None, None]
        lastout = None
        prev_outdma = None
        pair_mm = [None, None]
        evs3_prev = [None] * 4
        for i in range(NT):
            ib2 = (i // 2) % 2
            if i % 2 == 0:
                p.dma("sync", "q_in", b3h2v[:, ib2, 0:16, :],
                      scH2v[:, 0:16, i * 128:i * 128 + 256],
                      waits=[pair_mm[ib2]])
                h2ld[ib2] = p.dma("sync", "q_in", b3h2v[:, ib2, 16:32, :],
                                  scH2v[:, 16:32, i * 128:i * 128 + 256])
            xmt = p.dma("sync", "q_in", b3xm[:], xmv[i], waits=[lastout])
            evs = []
            mm = None
            for oc in range(4):
                pst = psS if oc < 2 else psP
                cols = (oc % 2) * 512
                for kcc in range(32):
                    w = []
                    if kcc == 0:
                        w = [h2ld[ib2], w3tok_half, evs3_prev[oc]]
                    elif kcc == 16:
                        w = [w3tok]
                    mm = p.op("tensor", lambda E, kcc=kcc, oc=oc, i=i, pst=pst,
                              cols=cols, ib2=ib2: E.matmul(
                                  pst[:, cols:cols + 512],
                                  lhsT=b3h2v[:, ib2, kcc,
                                             (i % 2) * 128:(i % 2) * 128 + 128],
                                  rhs=w3sv[:, kcc, oc * 512:(oc + 1) * 512],
                                  start=(kcc == 0), stop=(kcc == 31)),
                              waits=w)
                evs.append(p.op("vector", lambda E, oc=oc, pst=pst, cols=cols:
                                E.tensor_add(out=b3y3[:, oc * 512:(oc + 1) * 512],
                                             in0=pst[:, cols:cols + 512],
                                             in1=b3bc[:, oc * 512:(oc + 1) * 512]),
                                waits=[mm, b3tok]))
                evs3_prev[oc] = evs[oc]
            pair_mm[ib2] = mm
            ap_ = emit_ln(b3st, (i % 2) * 30, b3y3[:], evs, b3y3[:])
            lastout = p.op("vector", lambda E: E.tensor_add(
                out=b3ob[:], in0=b3xm[:], in1=b3y3[:]),
                waits=[ap_, xmt, prev_outdma])
            prev_outdma = p.dma("scalar", "q_out",
                                dy.rearrange("(i p) d -> i p d", p=128)[i], b3ob[:],
                                waits=[lastout])
        p.barrier()
        return _finish(nc, p, blk)


def _finish(nc, p, blk):
    p.barrier()
    for ename in ENGINES:
        stream = p.stream[ename]

        def runner(E, stream=stream):
            for fn in stream:
                fn(E)
        getattr(blk, ename)(runner)
    return nc


def host_inputs(full_inputs):
    bf = ml_dtypes.bfloat16
    x = np.ascontiguousarray(full_inputs["L0_attn_x"], dtype=np.float32)
    wqkvT = np.ascontiguousarray(np.asarray(full_inputs["L0_attn_w_qkv"], np.float32).T)
    wpT = np.ascontiguousarray(np.asarray(full_inputs["L0_attn_w_proj"], np.float32).T)
    w1T = np.ascontiguousarray(np.asarray(full_inputs["L0_ffn_W1"], np.float32).T)
    w2T = np.ascontiguousarray(np.asarray(full_inputs["L0_ffn_W2"], np.float32).T)
    w3T = np.ascontiguousarray(np.asarray(full_inputs["L0_ffn_W3"], np.float32).T)
    b1c = np.ascontiguousarray(np.asarray(full_inputs["L0_ffn_b1"], np.float32).reshape(32, 128).T)
    b2c = np.ascontiguousarray(np.asarray(full_inputs["L0_ffn_b2"], np.float32).reshape(32, 128).T)
    b3r = np.ascontiguousarray(np.asarray(full_inputs["L0_ffn_b3"], np.float32).reshape(1, D))
    identt = np.eye(128, dtype=bf)
    ones_row = np.ones((1, 128), dtype=np.float32)

    inv = 1.0 / 10000.0 ** (np.arange(0, DH, 2, dtype=np.float64) / DH)
    rope = {}
    maskh = {}
    for h in range(2):
        start = h * T
        pos = start - 512 + np.arange(TK, dtype=np.float64)
        fr = pos[:, None] * inv[None, :]
        cos = np.repeat(np.cos(fr), 2, axis=-1)
        sin = np.repeat(np.sin(fr), 2, axis=-1)
        sgn = np.ones(DH)
        sgn[0::2] = -1.0
        ssgn = sin * sgn[None, :]
        rope[h] = (
            np.ascontiguousarray(cos.reshape(NTK, 128, DH).transpose(1, 0, 2)
                                 .reshape(128, NTK * DH).astype(np.float32)),
            np.ascontiguousarray(ssgn.reshape(NTK, 128, DH).transpose(1, 0, 2)
                                 .reshape(128, NTK * DH).astype(np.float32)))
        r = np.arange(128)[None, :]
        kc = np.arange(128)[:, None]
        mk = np.zeros((128, 5 * 640), dtype=bf)
        for m in range(5):
            for j in range(5):
                c = j * 128 + kc
                valid = (c > r) & (c <= r + 512)
                if h == 0 and m < 4:
                    valid = valid & (m * 128 + c >= 512)
                mk[:, m * 640 + j * 128:m * 640 + (j + 1) * 128] = valid.astype(bf)
        maskh[h] = mk

    in_maps = []
    for core in range(8):
        b, h = core // 2, core % 2
        x_own = np.ascontiguousarray(x[b, h * T:(h + 1) * T])
        x_ctx = (np.ascontiguousarray(x[b, T - 512:T]) if h == 1
                 else np.zeros((512, D), np.float32))
        in_maps.append({
            "x_own": x_own, "x_ctx": x_ctx,
            "wqkvT": wqkvT, "wpT": wpT, "w1T": w1T, "w2T": w2T, "w3T": w3T,
            "b1c": b1c, "b2c": b2c, "b3r": b3r,
            "masks": maskh[h], "ropeC": rope[h][0], "ropeS": rope[h][1],
            "ident": identt, "ones_row": ones_row,
        })
    return in_maps


_CACHE = {}


def _get(debug=False, phases="full"):
    key = (debug, phases)
    if key not in _CACHE:
        _CACHE[key] = build(debug, phases)
    return _CACHE[key]


def kernel(**inputs):
    nc = _get(False, "full")
    in_maps = host_inputs(inputs)
    res = run_bass_kernel_spmd(nc, in_maps, core_ids=list(range(8)))
    out = np.empty((4, 4096, D), dtype=np.float32)
    for core in range(8):
        b, h = core // 2, core % 2
        out[b, h * T:(h + 1) * T] = res.results[core]["y"]
    return out


# version 13
# speedup vs baseline: 1.2390x; 1.1222x over previous
import numpy as np
import ml_dtypes
from contextlib import ExitStack

import concourse.bass as bass
import concourse.mybir as mybir
from concourse.bass_utils import run_bass_kernel_spmd

F32 = mybir.dt.float32
BF16 = mybir.dt.bfloat16
AF = mybir.ActivationFunctionType
ALU = mybir.AluOpType
AX = mybir.AxisListType

T = 2048
TK = 2560
D = 2048
DH = 128
HQ = 16
G = 4
RMS_EPS = 1.1920929e-07
LN_EPS = 1e-5
NT = 16
NTK = 20
KC = 16
QBIAS = float(-0.5 * np.log(128.0))
EXPBIAS = -16.0

ENGINES = ("tensor", "vector", "scalar", "gpsimd", "sync")


class Prog:
    def __init__(self):
        self.stream = {e: [] for e in ENGINES}
        self.sem = {}
        self.count = {}
        self.lastwait = {}

    def add_sem(self, name, handle):
        self.sem[name] = handle
        self.count[name] = 0

    def wait(self, eng, token):
        if token is None:
            return
        name, val = token
        if val <= 0:
            return
        if self.lastwait.get((eng, name), 0) >= val:
            return
        self.lastwait[(eng, name)] = val
        h = self.sem[name]
        self.stream[eng].append(lambda E, h=h, v=val: E.wait_ge(h, v))

    def op(self, eng, fn, waits=(), inc="default", amt=1):
        for t in waits:
            self.wait(eng, t)
        if inc == "default":
            inc = eng if eng != "sync" else None
        tok = None
        if inc is not None:
            self.count[inc] += amt
            tok = (inc, self.count[inc])
            h = self.sem[inc]
            self.stream[eng].append(
                lambda E, fn=fn, h=h, a=amt: fn(E).then_inc(h, a))
        else:
            self.stream[eng].append(lambda E, fn=fn: fn(E))
        return tok

    def dma(self, eng, qname, out, in_, waits=()):
        return self.op(eng, lambda E: E.dma_start(out=out, in_=in_),
                       waits=waits, inc=qname, amt=16)

    def barrier(self):
        snapshot = [(n, c) for n, c in self.count.items() if c > 0]
        for e in ENGINES:
            for n, c in snapshot:
                self.wait(e, (n, c))

    def token(self, name):
        return (name, self.count[name])


class Arena:

    def __init__(self, nc, size_bytes):
        self.nc = nc
        self.size = size_bytes
        self.pend = []

    def alloc(self, name, shape, dtype, off):
        per_part = int(np.prod(shape[1:])) * mybir.dt.size(dtype)
        assert off % 32 == 0, (name, off)
        assert off + per_part <= self.size, (name, off, per_part)
        hnd = self.nc.virtual_sbuf_tensor(name, shape, dtype)
        self.pend.append((hnd, off))
        return hnd

    def materialize(self, ctx):
        handle = ctx.enter_context(
            self.nc.sbuf_tensor("arena", [128, self.size], mybir.dt.uint8))
        base = self.nc.lookup_mloc(handle).addr
        for hnd, off in self.pend:
            self.nc._tensor(hnd.name, hnd.shape, hnd.dtype, type="SB")
            mloc = self.nc.lookup_mloc(hnd)
            mloc.allocated = True
            mloc.addr = base + off


def _ev(ap_):
    return ap_.rearrange("p (a two) -> p a two", two=2)[:, :, 0]


def _od(ap_):
    return ap_.rearrange("p (a two) -> p a two", two=2)[:, :, 1]


ARENA = 206144
A0_SERIAL = False


def build(debug=False, phases="full"):
    nc = bass.Bass("TRN2", target_bir_lowering=False)

    dx = nc.dram_tensor("x_own", [T, D], F32, kind="ExternalInput").ap()
    dxc = nc.dram_tensor("x_ctx", [512, D], F32, kind="ExternalInput").ap()
    dwqkvT = nc.dram_tensor("wqkvT", [D, 3072], F32, kind="ExternalInput").ap()
    dwpT = nc.dram_tensor("wpT", [D, D], F32, kind="ExternalInput").ap()
    dw1T = nc.dram_tensor("w1T", [D, 4096], F32, kind="ExternalInput").ap()
    dw2T = nc.dram_tensor("w2T", [4096, 4096], F32, kind="ExternalInput").ap()
    dw3T = nc.dram_tensor("w3T", [4096, D], F32, kind="ExternalInput").ap()
    db1c = nc.dram_tensor("b1c", [128, 32], F32, kind="ExternalInput").ap()
    db2c = nc.dram_tensor("b2c", [128, 32], F32, kind="ExternalInput").ap()
    db3r = nc.dram_tensor("b3r", [1, D], F32, kind="ExternalInput").ap()
    dmask = nc.dram_tensor("masks", [128, 5 * 640], BF16, kind="ExternalInput").ap()
    dropeC = nc.dram_tensor("ropeC", [128, NTK * DH], F32, kind="ExternalInput").ap()
    dropeS = nc.dram_tensor("ropeS", [128, NTK * DH], F32, kind="ExternalInput").ap()
    dident = nc.dram_tensor("ident", [128, 128], BF16, kind="ExternalInput").ap()
    dones = nc.dram_tensor("ones_row", [1, 128], F32, kind="ExternalInput").ap()

    dy = nc.dram_tensor("y", [T, D], F32, kind="ExternalOutput").ap()
    sck = "ExternalOutput" if debug else "Internal"
    dscY = nc.dram_tensor("scY", [HQ, 128, T], BF16, kind=sck).ap()
    dscXM = nc.dram_tensor("scXM", [T, D], F32, kind=sck).ap()
    dscH2 = nc.dram_tensor("scH2", [32, 128, T], BF16, kind=sck).ap()
    if debug:
        drr = nc.dram_tensor("dbg_rr", [NT, 128, 32], F32, kind="ExternalOutput").ap()
        dsqb = nc.dram_tensor("dbg_sqb", [NT, 128, 512], F32, kind="ExternalOutput").ap()
        dln1 = nc.dram_tensor("dbg_ln1T", [128, KC * TK], BF16, kind="ExternalOutput").ap()
        dqn = nc.dram_tensor("dbg_qn", [G, 128, 4 * T], BF16, kind="ExternalOutput").ap()
        dkn = nc.dram_tensor("dbg_kn", [G, 128, TK], BF16, kind="ExternalOutput").ap()
        dvt = nc.dram_tensor("dbg_vt", [G, 128, TK], BF16, kind="ExternalOutput").ap()
        dln2 = nc.dram_tensor("dbg_ln2T", [128, KC * T], BF16, kind="ExternalOutput").ap()
        dh1d = nc.dram_tensor("dbg_h1", [128, 32 * T], BF16, kind="ExternalOutput").ap()

    p = Prog()
    ar = Arena(nc, ARENA)

    ident = ar.alloc("identt", [128, 128], BF16, 0)
    onesr = ar.alloc("onesrr", [1, 128], F32, 256)
    b1c = ar.alloc("b1cc", [128, 32], F32, 768)
    b2c = ar.alloc("b2cc", [128, 32], F32, 896)
    constv = ar.alloc("constv", [128, 8], F32, 1024)
    P0 = 1088
    ln1T = ar.alloc("ln1T", [128, KC * TK], BF16, P0)
    A2R = P0 + 81920
    cur = A2R

    def nxt(sz):
        nonlocal cur
        off = cur
        cur += sz
        assert cur <= ARENA
        return off
    ropeC = ar.alloc("ropeCs", [128, NTK * DH], F32, nxt(10240))
    ropeS = ar.alloc("ropeSs", [128, NTK * DH], F32, nxt(10240))
    masks = ar.alloc("maskss", [128, 5 * 640], BF16, nxt(6400))
    A0ALIAS = cur
    wTq = ar.alloc("wTq", [128, KC * 512], BF16, nxt(16384))
    wTkv = ar.alloc("wTkv", [128, KC * 256], BF16, nxt(8192))
    stg = ar.alloc("stg", [128, 1024], F32, nxt(4096))
    sqb = ar.alloc("sqb", [128, 1024], F32, nxt(4096))
    rr = ar.alloc("rr", [128, 32], F32, nxt(128))
    ropet = ar.alloc("ropet", [128, 256], F32, nxt(1024))
    ntm = ar.alloc("ntm", [128, 512], BF16, nxt(1024))
    qnf = ar.alloc("qnf", [128, 4 * T], BF16, nxt(16384))
    knf = ar.alloc("knf", [128, TK], BF16, nxt(5120))
    vtm = ar.alloc("vtm", [128, TK], BF16, nxt(5120))
    pbuf = ar.alloc("pbuf", [128, 4 * 640], BF16, nxt(5120))
    onescol = ar.alloc("onescol", [128, 1], BF16, nxt(32))
    onesb = ar.alloc("onesb", [1, 128], BF16, nxt(256))
    recipr = ar.alloc("recipr", [1, 512], BF16, nxt(1024))
    t1w = ar.alloc("t1w", [128, 512], F32, nxt(2048))
    t2w = ar.alloc("t2w", [128, 512], F32, nxt(2048))
    qstgB = ar.alloc("qstgB", [128, 512], F32, nxt(2048))
    ksq = ar.alloc("ksq", [128, 128], F32, nxt(512))
    kmix = ar.alloc("kmix", [128, 128], F32, nxt(512))
    qtmb = ar.alloc("qtmb", [128, 512], BF16, nxt(1024))
    yevb = ar.alloc("yevb", [128, 2 * 512], BF16, nxt(2048))
    attst = ar.alloc("attst", [128, 8], F32, nxt(32))
    assert cur <= ARENA
    a0x = ar.alloc("a0x", [128, 2 * D], F32, A0ALIAS)
    a0ln = ar.alloc("a0ln", [128, D], BF16, A0ALIAS + 16384)
    a0st = ar.alloc("a0st", [128, 64], F32, A0ALIAS + 20480)
    wpT = ar.alloc("wpTs", [128, KC * D], BF16, P0)
    A3B = P0 + 65536
    a3yld = ar.alloc("a3yld", [128, 2 * HQ * 256], BF16, A3B)
    a3y2 = ar.alloc("a3y2", [128, D], F32, A3B + 16384)
    a3xr = ar.alloc("a3xr", [128, 2 * D], F32, A3B + 24576)
    a3xm = ar.alloc("a3xm", [128, D], F32, A3B + 40960)
    a3ln2 = ar.alloc("a3ln2", [128, D], BF16, A3B + 49152)
    a3st = ar.alloc("a3st", [128, 128], F32, A3B + 53248)
    LN2T = P0 + 131072
    ln2T = ar.alloc("ln2T", [128, KC * T], BF16, LN2T)
    h1 = ar.alloc("h1s", [128, 32 * T], BF16, P0)
    B1B = LN2T + 65536
    wT1 = ar.alloc("wT1", [128, 2 * KC * 128], BF16, B1B)
    wT2 = ar.alloc("wT2", [128, 2 * 32 * 128], BF16, LN2T)
    h2b = ar.alloc("h2b", [128, 2 * D], BF16, LN2T + 16384)
    w3Ts = ar.alloc("w3Ts", [128, 32 * T], BF16, P0)
    b3h2 = ar.alloc("b3h2", [128, 2 * 32 * 256], BF16, LN2T)
    b3y3 = ar.alloc("b3y3", [128, D], F32, LN2T + 32768)
    b3xm = ar.alloc("b3xm", [128, D], F32, LN2T + 40960)
    b3bc = ar.alloc("b3bc", [128, D], F32, LN2T + 49152)
    b3st = ar.alloc("b3st", [128, 64], F32, LN2T + 57344)
    b3r = ar.alloc("b3rr", [1, D], F32, LN2T + 57600)
    b3ob = ar.alloc("b3ob", [128, D], F32, LN2T + 65792)

    ctx = ExitStack()
    with ctx:
        for s in ("tensor", "vector", "scalar", "gpsimd", "q_in", "q_wt", "q_out"):
            p.add_sem(s, ctx.enter_context(nc.semaphore(s)))
        ar.materialize(ctx)
        psQ = ctx.enter_context(nc.psum_tensor("psQ", [128, 1024], F32))
        psS = ctx.enter_context(nc.psum_tensor("psS", [128, 1024], F32))
        psP = ctx.enter_context(nc.psum_tensor("psP", [128, 1024], F32))
        psY = ctx.enter_context(nc.psum_tensor("psY", [128, 512], F32))
        blk = ctx.enter_context(nc.Block())

        def emit_ln(stbuf, c0, src, waits, out_ap, out_src=None, extra_ap_waits=()):
            stt = None
            for chn in range(4):
                stt = p.op("vector", lambda E, chn=chn: E.bn_stats(
                    out=stbuf[:, c0 + chn * 6:c0 + chn * 6 + 6],
                    in_=src[:, chn * 512:(chn + 1) * 512]),
                    waits=(list(waits) if chn == 0 else []))
            agg = p.op("vector", lambda E: E.bn_aggr(
                out=stbuf[:, c0 + 24:c0 + 26], in_=stbuf[:, c0:c0 + 24]), waits=[stt])
            l_ = p.op("scalar", lambda E: E.activation(
                out=stbuf[:, c0 + 26:c0 + 27], in_=stbuf[:, c0 + 25:c0 + 26],
                func=AF.Ln, bias=LN_EPS), waits=[agg])
            e_ = p.op("scalar", lambda E: E.activation(
                out=stbuf[:, c0 + 27:c0 + 28], in_=stbuf[:, c0 + 26:c0 + 27],
                func=AF.Exp, scale=-0.5), waits=[l_])
            nm_ = p.op("vector", lambda E: E.scalar_tensor_tensor(
                out=stbuf[:, c0 + 28:c0 + 29], in0=stbuf[:, c0 + 24:c0 + 25],
                scalar=-1.0, in1=stbuf[:, c0 + 27:c0 + 28],
                op0=ALU.mult, op1=ALU.mult), waits=[e_])
            src2 = out_src if out_src is not None else src
            ap_ = p.op("vector", lambda E: E.tensor_scalar(
                out_ap, src2, stbuf[:, c0 + 27:c0 + 28], stbuf[:, c0 + 28:c0 + 29],
                ALU.mult, ALU.add), waits=[nm_] + list(extra_ap_waits))
            return ap_

        CONSTS = (LN_EPS, RMS_EPS, QBIAS, -0.5, 1.0 / 128.0, EXPBIAS)
        cst = None
        for k, v in enumerate(CONSTS):
            nc.const_aps.register(F32, v, constv[:, k:k + 1])
            cst = p.op("vector", lambda E, k=k, v=v: E.memset(constv[:, k:k + 1], v))
        cst = p.op("vector", lambda E: E.memset(onescol[:], 1.0))
        cst = p.op("vector", lambda E: E.memset(onesb[:], 1.0))
        for e in ENGINES:
            p.wait(e, cst)

        p.dma("sync", "q_in", ident[:], dident)
        p.dma("sync", "q_in", onesr[:], dones)
        p.dma("sync", "q_in", b1c[:], db1c)
        p.dma("sync", "q_in", b2c[:], db2c)
        tok_id = p.token("q_in")
        p.dma("sync", "q_in", ropeC[:], dropeC)
        p.dma("sync", "q_in", ropeS[:], dropeS)
        p.dma("sync", "q_in", masks[:], dmask)
        tok_consts = p.token("q_in")

        ln1Tv = ln1T[:].rearrange("p (kc t) -> p kc t", t=TK)
        xv = dx.rearrange("(i p) d -> i p d", p=128)
        xcv = dxc.rearrange("(i p) d -> i p d", p=128)

        ldtok = [None] * NTK
        evtok = [None] * NTK

        def a0_load(jk):
            src = xcv[jk] if jk < 4 else xv[jk - 4]
            eng, q = (("sync", "q_in") if jk % 2 == 0 else ("scalar", "q_out"))
            ldtok[jk] = p.dma(eng, q,
                              a0x[:, (jk % 2) * D:(jk % 2) * D + D], src,
                              waits=[evtok[jk - 2]] if jk >= 2 else [])
        a0_load(0)
        a0_load(1)
        prev_tr2 = None
        for jk in range(NTK):
            xt = a0x[:, (jk % 2) * D:(jk % 2) * D + D]
            c0 = (jk % 2) * 30
            aptok = emit_ln(a0st, c0, xt, [ldtok[jk]], a0ln[:],
                            extra_ap_waits=[prev_tr2])
            tr1 = tr2 = None
            for j in range(KC):
                pst = psS if j < 8 else psP
                tr = p.op("tensor", lambda E, j=j, pst=pst: E.matmul(
                    pst[:, (j % 8) * 128:(j % 8) * 128 + 128],
                    lhsT=a0ln[:, j * 128:(j + 1) * 128], rhs=ident[:],
                    start=True, stop=True),
                    waits=([aptok, tok_id, evtok[jk - 1] if jk >= 1 else None]
                           if j == 0 else []))
                if j == 7:
                    tr1 = tr
                if j == 15:
                    tr2 = tr
            prev_tr2 = tr2
            e1 = p.op("vector", lambda E, jk=jk: E.tensor_copy(
                out=ln1Tv[:, 0:8, jk * 128:(jk + 1) * 128],
                in_=psS[:].rearrange("p (kc c) -> p kc c", c=128)), waits=[tr1])
            evtok[jk] = p.op("vector", lambda E, jk=jk: E.tensor_copy(
                out=ln1Tv[:, 8:16, jk * 128:(jk + 1) * 128],
                in_=psP[:].rearrange("p (kc c) -> p kc c", c=128)), waits=[tr2, e1])
            if jk + 2 < NTK:
                a0_load(jk + 2)
            if A0_SERIAL:
                p.barrier()
        p.barrier()
        if debug:
            p.dma("sync", "q_out", dln1, ln1T[:])
            p.barrier()
        if phases == "A0":
            return _finish(nc, p, blk)

        wTqv = wTq[:].rearrange("p (kc f) -> p kc f", f=512)
        wTkvv = wTkv[:].rearrange("p (kc f) -> p kc f", f=256)
        qnfv = qnf[:].rearrange("p (hh t) -> p hh t", t=T)
        maskv = masks[:].rearrange("p (i c) -> p i c", c=640)
        wqkvTv = dwqkvT.rearrange("(kc p) f -> p kc f", p=128)

        for g in range(G):
            wtoks = [
                p.dma("gpsimd", "q_wt", wTqv[:, 0:8, :],
                      wqkvTv[:, 0:8, 512 * g:512 * g + 512]),
                p.dma("gpsimd", "q_wt", wTqv[:, 8:16, :],
                      wqkvTv[:, 8:16, 512 * g:512 * g + 512]),
                p.dma("gpsimd", "q_wt", wTkvv[:, :, 0:128],
                      wqkvTv[:, :, 2048 + 128 * g:2048 + 128 * g + 128]),
                p.dma("gpsimd", "q_wt", wTkvv[:, :, 128:256],
                      wqkvTv[:, :, 2560 + 128 * g:2560 + 128 * g + 128]),
            ]
            wtok = wtoks[-1]

            kn_done = [None] * NTK
            qn_done = [None] * NT
            ktok = [dict() for _ in range(NTK)]
            qtok = [dict() for _ in range(NT)]
            psSv = [psS[:, 0:256], psS[:, 512:768]]
            psQv = [psQ[:, 0:512], psQ[:, 512:1024]]
            kstv = [stg[:, 0:128], stg[:, 128:256]]
            qstgv = [stg[:, 512:1024], qstgB[:]]
            ktmv = [ntm[:, 0:128], ntm[:, 128:256]]
            t1k = ropet[:, 0:128]
            t2k = ropet[:, 128:256]

            def K_head(jk):
                kp = jk % 2
                pv = psSv[kp]
                prev = ktok[jk - 2] if jk >= 2 else {}
                pr1 = ktok[jk - 1] if jk >= 1 else {}
                tk = ktok[jk]
                mmk = None
                for kcc in range(KC):
                    mmk = p.op("tensor", lambda E, kcc=kcc, jk=jk, pv=pv: E.matmul(
                        pv, lhsT=ln1Tv[:, kcc, jk * 128:(jk + 1) * 128],
                        rhs=wTkvv[:, kcc, :],
                        start=(kcc == 0), stop=(kcc == KC - 1)),
                        waits=([wtok, prev.get("cpk"), prev.get("sqk"),
                                prev.get("vcp")] if kcc == 0 else []))
                cpk = p.op("vector", lambda E, pv=pv, kp=kp: E.tensor_copy(
                    out=kstv[kp], in_=pv[:, 0:128]),
                    waits=[mmk, prev.get("ck")])
                sqk = p.op("scalar", lambda E, kp=kp: E.activation(
                    out=ksq[:], in_=kstv[kp], func=AF.Square),
                    waits=[cpk, pr1.get("redk")])
                redk = p.op("vector", lambda E, kp=kp: E.reduce_sum(
                    out=rr[:, kp:kp + 1], in_=ksq[:], axis=AX.X),
                    waits=[sqk, prev.get("lnk")])
                lnk = p.op("scalar", lambda E, kp=kp: E.activation(
                    out=rr[:, 2 + kp:3 + kp], in_=rr[:, kp:kp + 1],
                    func=AF.Ln, scale=1.0 / 128.0, bias=RMS_EPS),
                    waits=[redk, prev.get("exk")])
                exk = p.op("scalar", lambda E, kp=kp: E.activation(
                    out=rr[:, 4 + kp:5 + kp], in_=rr[:, 2 + kp:3 + kp],
                    func=AF.Exp, scale=-0.5), waits=[lnk, prev.get("mulk")])
                cs = ropeC[:, jk * 128:(jk + 1) * 128]
                ss = ropeS[:, jk * 128:(jk + 1) * 128]
                a_k = p.op("vector", lambda E, kp=kp, cs=cs: E.tensor_mul(
                    out=t1k, in0=kstv[kp], in1=cs),
                    waits=[cpk, pr1.get("mulk"), tok_consts])
                b_k = p.op("vector", lambda E, kp=kp, ss=ss:
                           E.scalar_tensor_tensor(
                               out=_ev(t2k), in0=_od(kstv[kp]), scalar=1.0,
                               in1=_ev(ss), op0=ALU.mult, op1=ALU.mult),
                           waits=[])
                c_k = p.op("vector", lambda E, kp=kp, ss=ss:
                           E.scalar_tensor_tensor(
                               out=_od(t2k), in0=_ev(kstv[kp]), scalar=1.0,
                               in1=_od(ss), op0=ALU.mult, op1=ALU.mult),
                           waits=[])
                d_k = p.op("vector", lambda E: E.tensor_add(
                    out=kmix[:], in0=t1k, in1=t2k),
                    waits=[a_k, b_k, c_k, pr1.get("mulk")])
                mulk = p.op("scalar", lambda E, kp=kp: E.mul(
                    out=ktmv[kp], in_=kmix[:], mul=rr[:, 4 + kp:5 + kp]),
                    waits=[d_k, exk, prev.get("trk")])
                vcp = p.op("vector", lambda E, pv=pv, jk=jk: E.tensor_copy(
                    out=vtm[:, jk * 128:(jk + 1) * 128], in_=pv[:, 128:256]),
                    waits=[mmk])
                tk.update(cpk=cpk, sqk=sqk, redk=redk, lnk=lnk, exk=exk,
                          mulk=mulk, vcp=vcp, ck=c_k)

            def K_tail(jk):
                kp = jk % 2
                trk = p.op("tensor", lambda E, kp=kp: E.matmul(
                    psP[:, 0:128], lhsT=ktmv[kp], rhs=ident[:],
                    start=True, stop=True),
                    waits=[ktok[jk]["mulk"],
                           kn_done[jk - 1] if jk >= 1 else None])
                kev = p.op("vector", lambda E, jk=jk: E.tensor_copy(
                    out=knf[:, jk * 128:(jk + 1) * 128], in_=psP[:, 0:128]),
                    waits=[trk])
                ktok[jk]["trk"] = trk
                kn_done[jk] = kev

            def Q_head(i):
                jkq = i + 4
                qp = i % 2
                qb = 8 + qp * 12
                pv = psQv[qp]
                prev = qtok[i - 2] if i >= 2 else {}
                pr1 = qtok[i - 1] if i >= 1 else {}
                tq = qtok[i]
                mmq = None
                for kcc in range(KC):
                    mmq = p.op("tensor", lambda E, kcc=kcc, jkq=jkq, pv=pv:
                               E.matmul(
                                   pv,
                                   lhsT=ln1Tv[:, kcc, jkq * 128:(jkq + 1) * 128],
                                   rhs=wTqv[:, kcc, :],
                                   start=(kcc == 0), stop=(kcc == KC - 1)),
                               waits=([wtok, prev.get("cpq"), prev.get("sqq")]
                                      if kcc == 0 else []))
                cpq = p.op("vector", lambda E, pv=pv, qp=qp: E.tensor_copy(
                    out=qstgv[qp], in_=pv), waits=[mmq, prev.get("cq")])
                sqq = p.op("scalar", lambda E, qp=qp: E.activation(
                    out=sqb[:, 0:512], in_=qstgv[qp], func=AF.Square),
                    waits=[cpq, pr1.get("redq")])
                redq = p.op("vector", lambda E, qb=qb: E.reduce_sum(
                    out=rr[:, qb:qb + 4], in_=sqb[:, 0:512].rearrange(
                        "p (h c) -> p h c", c=128), axis=AX.X),
                    waits=[sqq, prev.get("lnq")])
                lnq = p.op("scalar", lambda E, qb=qb: E.activation(
                    out=rr[:, qb + 4:qb + 8], in_=rr[:, qb:qb + 4],
                    func=AF.Ln, scale=1.0 / 128.0, bias=RMS_EPS),
                    waits=[redq, prev.get("exq")])
                exq = p.op("scalar", lambda E, qb=qb: E.activation(
                    out=rr[:, qb + 8:qb + 12], in_=rr[:, qb + 4:qb + 8],
                    func=AF.Exp, scale=-0.5, bias=QBIAS),
                    waits=[lnq, prev.get("mul3")])
                cs = ropeC[:, jkq * 128:(jkq + 1) * 128]
                ss = ropeS[:, jkq * 128:(jkq + 1) * 128]
                abc = None
                for hh in range(4):
                    qs = qstgv[qp][:, hh * 128:hh * 128 + 128]
                    tc1 = t1w[:, hh * 128:hh * 128 + 128]
                    tc2 = t2w[:, hh * 128:hh * 128 + 128]
                    p.op("vector", lambda E, qs=qs, tc1=tc1, cs=cs: E.tensor_mul(
                        out=tc1, in0=qs, in1=cs),
                        waits=([cpq, pr1.get("dw")] if hh == 0 else []))
                    p.op("vector", lambda E, qs=qs, tc2=tc2, ss=ss:
                         E.scalar_tensor_tensor(
                             out=_ev(tc2), in0=_od(qs), scalar=1.0,
                             in1=_ev(ss), op0=ALU.mult, op1=ALU.mult),
                         waits=[])
                    abc = p.op("vector", lambda E, qs=qs, tc2=tc2, ss=ss:
                               E.scalar_tensor_tensor(
                                   out=_od(tc2), in0=_ev(qs), scalar=1.0,
                                   in1=_od(ss), op0=ALU.mult, op1=ALU.mult),
                               waits=[])
                dw = p.op("vector", lambda E: E.tensor_add(
                    out=sqb[:, 512:1024], in0=t1w[:], in1=t2w[:]),
                    waits=[abc, redq, pr1.get("mul3")])
                mul3 = None
                for hh in range(4):
                    mul3 = p.op("scalar", lambda E, hh=hh, qb=qb: E.mul(
                        out=qtmb[:, hh * 128:hh * 128 + 128],
                        in_=sqb[:, 512 + hh * 128:512 + hh * 128 + 128],
                        mul=rr[:, qb + 8 + hh:qb + 9 + hh]),
                        waits=([dw, exq, pr1.get("trq3")] if hh == 0 else []))
                tq.update(cpq=cpq, sqq=sqq, redq=redq, lnq=lnq, exq=exq,
                          dw=dw, mul3=mul3, cq=abc)

            def Q_tail(i):
                trq = None
                for hh in range(4):
                    trq = p.op("tensor", lambda E, hh=hh: E.matmul(
                        psP[:, 512 + hh * 128:512 + hh * 128 + 128],
                        lhsT=qtmb[:, hh * 128:hh * 128 + 128], rhs=ident[:],
                        start=True, stop=True),
                        waits=([qtok[i]["mul3"],
                                qn_done[i - 1] if i >= 1 else None]
                               if hh == 0 else []))
                qev = None
                for hh in range(4):
                    qev = p.op("vector", lambda E, i=i, hh=hh: E.tensor_copy(
                        out=qnfv[:, hh, i * 128:(i + 1) * 128],
                        in_=psP[:, 512 + hh * 128:512 + hh * 128 + 128]),
                        waits=([trq] if hh == 0 else []))
                qtok[i]["trq3"] = trq
                qn_done[i] = qev

            kq_only = phases.startswith("QKV1")
            do_q = phases != "QKV1K"
            for jk in range(NTK):
                if jk >= 1:
                    K_tail(jk - 1)
                if do_q and jk >= 5:
                    Q_tail(jk - 5)
                K_head(jk)
                if do_q and jk >= 4:
                    Q_head(jk - 4)
            K_tail(NTK - 1)
            if do_q:
                Q_tail(NT - 1)
            if kq_only:
                return _finish(nc, p, blk)
            if debug:
                p.barrier()
                p.dma("sync", "q_out", dqn[g], qnf[:])
                p.dma("sync", "q_out", dkn[g], knf[:])
                p.dma("sync", "q_out", dvt[g], vtm[:])
                p.barrier()

            last_exp = [None, None]
            pb_free = {}
            mask_tok = {}
            last_ye = [None, None]
            last_ydma = [None, None]
            last_rc = None
            last_bc = None
            last_yn = None
            last_dn = None

            att_ps_war = [
                [ktok[NTK - 2]["cpk"], ktok[NTK - 2]["sqk"], ktok[NTK - 2]["vcp"],
                 ktok[NTK - 1]["cpk"], ktok[NTK - 1]["sqk"], ktok[NTK - 1]["vcp"]],
                [qtok[NT - 2]["cpq"], qtok[NT - 2]["sqq"],
                 qtok[NT - 1]["cpq"], qtok[NT - 1]["sqq"]],
            ]
            first_den = [kn_done[NTK - 1]]
            first_bc = [qn_done[NT - 1]]

            def att_head(t):
                hh, i = divmod(t, NT)
                ib = t % 2
                ib4 = t % 4
                pS = psS if ib == 0 else psQ
                st = None
                for j in range(5):
                    st = p.op("tensor", lambda E, hh=hh, i=i, j=j, pS=pS: E.matmul(
                        pS[:, j * 128:(j + 1) * 128],
                        lhsT=knf[:, (i + j) * 128:(i + j + 1) * 128],
                        rhs=qnfv[:, hh, i * 128:(i + 1) * 128],
                        start=True, stop=True),
                        waits=([qn_done[i], kn_done[min(i + 4, NTK - 1)],
                                last_exp[ib]]
                               + (att_ps_war[ib] if t < 2 else [])
                               if j == 0 else []))
                pb = pbuf[:, ib4 * 640:ib4 * 640 + 640]
                ex = p.op("scalar", lambda E, pb=pb, pS=pS: E.activation(
                    out=pb, in_=pS[:, 0:640], func=AF.Exp, bias=EXPBIAS),
                    waits=[st, pb_free.pop(t - 4, None)])
                last_exp[ib] = ex
                if i < 4:
                    mk = p.op("vector", lambda E, pb=pb, i=i: E.tensor_tensor(
                        out=pb, in0=pb, in1=maskv[:, i, :], op=ALU.mult),
                        waits=[ex])
                else:
                    p.op("vector", lambda E, pb=pb: E.tensor_tensor(
                        out=pb[:, 0:128], in0=pb[:, 0:128],
                        in1=maskv[:, 4, 0:128], op=ALU.mult), waits=[ex])
                    mk = p.op("vector", lambda E, pb=pb: E.tensor_tensor(
                        out=pb[:, 512:640], in0=pb[:, 512:640],
                        in1=maskv[:, 4, 512:640], op=ALU.mult))
                mask_tok[t] = mk

            def emit_tail(t):
                nonlocal last_rc, last_bc, last_yn, last_dn
                hh, i = divmod(t, NT)
                head = 4 * g + hh
                ib4 = t % 4
                pb = pbuf[:, ib4 * 640:ib4 * 640 + 640]
                mk = mask_tok.pop(t)
                dn = None
                for j in range(5):
                    dn = p.op("tensor", lambda E, pb=pb, j=j, i=i: E.matmul(
                        psP[0:1, (i % 4) * 128:(i % 4) * 128 + 128],
                        lhsT=onescol[:], rhs=pb[:, j * 128:(j + 1) * 128],
                        start=(j == 0), stop=(j == 4)),
                        waits=([mk] + ([last_rc] if i % 4 == 0 else [])
                               + (first_den if t == 0 else [])
                               if j == 0 else []))
                last_dn = dn
                mv = None
                for j in range(5):
                    mv = p.op("tensor", lambda E, pb=pb, i=i, j=j: E.matmul(
                        psY[:, (i % 4) * 128:(i % 4) * 128 + 128],
                        lhsT=vtm[:, (i + j) * 128:(i + j) * 128 + 128],
                        rhs=pb[:, j * 128:(j + 1) * 128],
                        start=(i % 4 == 0 and j == 0),
                        stop=(i % 4 == 3 and j == 4)),
                        waits=([mk] + ([last_ye[(i // 4) % 2]]
                                       if i % 4 == 0 else [])
                               if j == 0 else []))
                pb_free[t] = mv
                if i % 4 == 3:
                    q4 = i // 4
                    gpar = q4 % 2
                    yv_ = yevb[:, gpar * 512:gpar * 512 + 512]
                    ye = p.op("vector", lambda E, yv_=yv_: E.tensor_copy(
                        out=yv_, in_=psY[:]), waits=[mv, last_ydma[gpar]])
                    last_ye[gpar] = ye
                    def _rc(E):
                        with nc.allow_low_precision(reason="bf16 recip row"):
                            return E.reciprocal(out=recipr[:],
                                                in_=psP[0:1, 0:512])
                    rc = p.op("vector", _rc, waits=[dn, last_bc])
                    last_rc = rc
                    bc = p.op("tensor", lambda E: E.matmul(
                        psP[:, 512:1024], lhsT=onesb[:], rhs=recipr[:],
                        start=True, stop=True),
                        waits=[rc, last_yn] + (first_bc if t < 4 else []))
                    last_bc = bc
                    yn = p.op("vector", lambda E, yv_=yv_: E.tensor_tensor(
                        out=yv_, in0=yv_, in1=psP[:, 512:1024], op=ALU.mult),
                        waits=[ye, bc])
                    last_yn = yn
                    last_ydma[gpar] = p.dma(
                        "scalar", "q_out", dscY[head][:, q4 * 512:(q4 + 1) * 512],
                        yv_, waits=[yn])

            for t in range(4 * NT):
                if t >= 3:
                    emit_tail(t - 3)
                att_head(t)
            for t in range(4 * NT - 3, 4 * NT):
                emit_tail(t)
            p.barrier()

        if phases == "A2":
            return _finish(nc, p, blk)

        wpTv = wpT[:].rearrange("p (h f) -> p h f", f=D)
        dwpTv = dwpT.rearrange("(h p) f -> p h f", p=128)
        wl = None
        for c4 in range(4):
            wl = p.dma("gpsimd", "q_wt", wpTv[:, c4 * 4:(c4 + 1) * 4, :],
                       dwpTv[:, c4 * 4:(c4 + 1) * 4, :])
        wptok = wl
        scYv = dscY.rearrange("h p t -> p h t")
        a3yldv = a3yld[:].rearrange("p (b h c) -> p b h c", b=2, h=HQ)
        xmv = dscXM.rearrange("(i p) d -> i p d", p=128)
        ln2Tv = ln2T[:].rearrange("p (kc t) -> p kc t", t=T)
        lastxr = [None, None]
        prev_add = None
        prev_xmdma = None
        prev_le2 = None
        yld_dma = [None, None]
        evs_prev = [None] * 4
        for i in range(NT):
            ib2 = (i // 2) % 2
            if i % 2 == 0:
                yld_dma[ib2] = p.dma(
                    "sync", "q_in", a3yldv[:, ib2, :, :],
                    scYv[:, :, i * 128:i * 128 + 256],
                    waits=[prev_add] if i >= 4 else [])
            xrt = p.dma("sync", "q_in", a3xr[:, (i % 2) * D:(i % 2) * D + D],
                        xv[i], waits=[lastxr[i % 2]])
            chunk_mm = []
            for oc in range(4):
                pst = psS if oc < 2 else psP
                cols = (oc % 2) * 512
                mm = None
                for h2 in range(HQ):
                    mm = p.op("tensor", lambda E, h2=h2, oc=oc, i=i, pst=pst, cols=cols:
                              E.matmul(pst[:, cols:cols + 512],
                                       lhsT=a3yldv[:, (i // 2) % 2, h2,
                                                   (i % 2) * 128:(i % 2) * 128 + 128],
                                       rhs=wpTv[:, h2, oc * 512:(oc + 1) * 512],
                                       start=(h2 == 0), stop=(h2 == HQ - 1)),
                              waits=([yld_dma[(i // 2) % 2], wptok, evs_prev[oc]]
                                     if h2 == 0 else []))
                chunk_mm.append(mm)
            evs = []
            for oc in range(4):
                pst = psS if oc < 2 else psP
                cols = (oc % 2) * 512
                evs.append(p.op("vector", lambda E, oc=oc, pst=pst, cols=cols:
                                E.tensor_copy(out=a3y2[:, oc * 512:(oc + 1) * 512],
                                              in_=pst[:, cols:cols + 512]),
                                waits=[chunk_mm[oc]]))
                evs_prev[oc] = evs[oc]
            ap_ = emit_ln(a3st, (i % 2) * 30, a3y2[:], evs, a3y2[:])
            xr = a3xr[:, (i % 2) * D:(i % 2) * D + D]
            add = p.op("vector", lambda E, xr=xr: E.tensor_add(
                out=a3xm[:], in0=xr, in1=a3y2[:]),
                waits=[ap_, xrt, prev_xmdma])
            prev_add = add
            lastxr[i % 2] = add
            xmdma = p.dma("scalar", "q_out", xmv[i], a3xm[:], waits=[add])
            prev_xmdma = xmdma
            ap2 = emit_ln(a3st, 60 + (i % 2) * 30, a3xm[:], [add], a3ln2[:],
                          extra_ap_waits=[prev_le2])
            le1 = prev_le2
            for r in range(2):
                tr = None
                for j in range(8):
                    tr = p.op("tensor", lambda E, r=r, j=j: E.matmul(
                        psQ[:, j * 128:(j + 1) * 128],
                        lhsT=a3ln2[:, (r * 8 + j) * 128:(r * 8 + j + 1) * 128],
                        rhs=ident[:], start=True, stop=True),
                        waits=([ap2, le1] if j == 0 else []))
                le1 = p.op("vector", lambda E, i=i, r=r: E.tensor_copy(
                    out=ln2Tv[:, r * 8:r * 8 + 8, i * 128:(i + 1) * 128],
                    in_=psQ[:].rearrange("p (kc c) -> p kc c", c=128)), waits=[tr])
            prev_le2 = le1
        p.barrier()
        if debug:
            p.dma("sync", "q_out", dln2, ln2T[:])
            p.barrier()
        if phases == "A3":
            return _finish(nc, p, blk)

        w1Tv = dw1T.rearrange("(kc p) f -> p kc f", p=128)
        wT1v = wT1[:].rearrange("p (b kc c) -> p b kc c", b=2, c=128)
        h1v = h1[:].rearrange("p (m t) -> p m t", t=T)
        wtk = [None, None]
        rel_tok = [None, None]
        prev_rel_chunk = [None] * 4
        for m in range(32):
            mb = m % 2
            wtk[mb] = p.dma("gpsimd", "q_wt", wT1v[:, mb, :, :],
                            w1Tv[:, :, m * 128:(m + 1) * 128],
                            waits=[rel_tok[mb]])
            rel = None
            for chn in range(4):
                pst = psS if chn < 2 else psP
                cols = (chn % 2) * 512
                mm = None
                for kcc in range(KC):
                    mm = p.op("tensor", lambda E, kcc=kcc, chn=chn, pst=pst, cols=cols,
                              mb=mb: E.matmul(
                                  pst[:, cols:cols + 512],
                                  lhsT=wT1v[:, mb, kcc, :],
                                  rhs=ln2Tv[:, kcc, chn * 512:(chn + 1) * 512],
                                  start=(kcc == 0), stop=(kcc == KC - 1)),
                              waits=([wtk[mb], prev_rel_chunk[chn]] if kcc == 0 else []))
                rel = p.op("scalar", lambda E, m=m, chn=chn, pst=pst, cols=cols:
                           E.activation(out=h1v[:, m, chn * 512:(chn + 1) * 512],
                                        in_=pst[:, cols:cols + 512],
                                        func=AF.Relu, bias=b1c[:, m:m + 1]),
                           waits=[mm, tok_id])
                prev_rel_chunk[chn] = rel
            rel_tok[mb] = rel
        p.barrier()
        if debug:
            p.dma("sync", "q_out", dh1d, h1[:])
            p.barrier()
        if phases == "B1":
            return _finish(nc, p, blk)

        w2Tv = dw2T.rearrange("(kc p) f -> p kc f", p=128)
        wT2v = wT2[:].rearrange("p (b kc c) -> p b kc c", b=2, c=128)
        wtk = [None, None]
        rel_tok = [None, None]
        h2dma = [None, None]
        prev_rel_chunk = [None] * 4
        for m in range(32):
            mb = m % 2
            p.dma("gpsimd", "q_wt", wT2v[:, mb, 0:16, :],
                  w2Tv[:, 0:16, m * 128:(m + 1) * 128], waits=[rel_tok[mb]])
            wtk[mb] = p.dma("gpsimd", "q_wt", wT2v[:, mb, 16:32, :],
                            w2Tv[:, 16:32, m * 128:(m + 1) * 128])
            rel = None
            for chn in range(4):
                pst = psS if chn < 2 else psP
                cols = (chn % 2) * 512
                mm = None
                for kcc in range(32):
                    mm = p.op("tensor", lambda E, kcc=kcc, chn=chn, pst=pst, cols=cols,
                              mb=mb: E.matmul(
                                  pst[:, cols:cols + 512],
                                  lhsT=wT2v[:, mb, kcc, :],
                                  rhs=h1v[:, kcc, chn * 512:(chn + 1) * 512],
                                  start=(kcc == 0), stop=(kcc == 31)),
                              waits=([wtk[mb], prev_rel_chunk[chn]] if kcc == 0 else []))
                rel = p.op("scalar", lambda E, chn=chn, pst=pst, cols=cols, mb=mb, m=m:
                           E.activation(out=h2b[:, mb * D + chn * 512:mb * D + (chn + 1) * 512],
                                        in_=pst[:, cols:cols + 512],
                                        func=AF.Relu, bias=b2c[:, m:m + 1]),
                           waits=([mm, h2dma[mb]] if chn == 0 else [mm]))
                prev_rel_chunk[chn] = rel
            rel_tok[mb] = rel
            h2dma[mb] = p.dma("scalar", "q_out", dscH2[m], h2b[:, mb * D:mb * D + D],
                              waits=[rel])
        p.barrier()
        if phases == "B2":
            return _finish(nc, p, blk)

        w3Tv = dw3T.rearrange("(kc p) f -> p kc f", p=128)
        w3sv = w3Ts[:].rearrange("p (kc f) -> p kc f", f=T)
        w3tok_half = None
        w3tok = None
        for c8 in range(4):
            w3tok = p.dma("gpsimd", "q_wt", w3sv[:, c8 * 8:(c8 + 1) * 8, :],
                          w3Tv[:, c8 * 8:(c8 + 1) * 8, :])
            if c8 == 1:
                w3tok_half = w3tok
        b3ld = p.dma("sync", "q_in", b3r[:], db3r)
        btoks = []
        for oc in range(4):
            mmb = p.op("tensor", lambda E, oc=oc: E.matmul(
                psQ[:, (oc % 2) * 512:(oc % 2) * 512 + 512], lhsT=onesr[:],
                rhs=b3r[:, oc * 512:(oc + 1) * 512], start=True, stop=True),
                waits=[b3ld, tok_id, btoks[oc - 2] if oc >= 2 else None])
            btoks.append(p.op("vector", lambda E, oc=oc: E.tensor_copy(
                out=b3bc[:, oc * 512:(oc + 1) * 512],
                in_=psQ[:, (oc % 2) * 512:(oc % 2) * 512 + 512]), waits=[mmb]))
        b3tok = btoks[-1]
        scH2v = dscH2.rearrange("m p t -> p m t")
        b3h2v = b3h2[:].rearrange("p (b m c) -> p b m c", b=2, m=32)
        h2ld = [None, None]
        lastout = None
        prev_outdma = None
        pair_mm = [None, None]
        evs3_prev = [None] * 4
        for i in range(NT):
            ib2 = (i // 2) % 2
            if i % 2 == 0:
                p.dma("sync", "q_in", b3h2v[:, ib2, 0:16, :],
                      scH2v[:, 0:16, i * 128:i * 128 + 256],
                      waits=[pair_mm[ib2]])
                h2ld[ib2] = p.dma("sync", "q_in", b3h2v[:, ib2, 16:32, :],
                                  scH2v[:, 16:32, i * 128:i * 128 + 256])
            xmt = p.dma("sync", "q_in", b3xm[:], xmv[i], waits=[lastout])
            evs = []
            mm = None
            for oc in range(4):
                pst = psS if oc < 2 else psP
                cols = (oc % 2) * 512
                for kcc in range(32):
                    w = []
                    if kcc == 0:
                        w = [h2ld[ib2], w3tok_half, evs3_prev[oc]]
                    elif kcc == 16:
                        w = [w3tok]
                    mm = p.op("tensor", lambda E, kcc=kcc, oc=oc, i=i, pst=pst,
                              cols=cols, ib2=ib2: E.matmul(
                                  pst[:, cols:cols + 512],
                                  lhsT=b3h2v[:, ib2, kcc,
                                             (i % 2) * 128:(i % 2) * 128 + 128],
                                  rhs=w3sv[:, kcc, oc * 512:(oc + 1) * 512],
                                  start=(kcc == 0), stop=(kcc == 31)),
                              waits=w)
                evs.append(p.op("vector", lambda E, oc=oc, pst=pst, cols=cols:
                                E.tensor_add(out=b3y3[:, oc * 512:(oc + 1) * 512],
                                             in0=pst[:, cols:cols + 512],
                                             in1=b3bc[:, oc * 512:(oc + 1) * 512]),
                                waits=[mm, b3tok]))
                evs3_prev[oc] = evs[oc]
            pair_mm[ib2] = mm
            ap_ = emit_ln(b3st, (i % 2) * 30, b3y3[:], evs, b3y3[:])
            lastout = p.op("vector", lambda E: E.tensor_add(
                out=b3ob[:], in0=b3xm[:], in1=b3y3[:]),
                waits=[ap_, xmt, prev_outdma])
            prev_outdma = p.dma("scalar", "q_out",
                                dy.rearrange("(i p) d -> i p d", p=128)[i], b3ob[:],
                                waits=[lastout])
        p.barrier()
        return _finish(nc, p, blk)


def _finish(nc, p, blk):
    p.barrier()
    for ename in ENGINES:
        stream = p.stream[ename]

        def runner(E, stream=stream):
            for fn in stream:
                fn(E)
        getattr(blk, ename)(runner)
    return nc


def host_inputs(full_inputs):
    bf = ml_dtypes.bfloat16
    x = np.ascontiguousarray(full_inputs["L0_attn_x"], dtype=np.float32)
    wqkvT = np.ascontiguousarray(np.asarray(full_inputs["L0_attn_w_qkv"], np.float32).T)
    wpT = np.ascontiguousarray(np.asarray(full_inputs["L0_attn_w_proj"], np.float32).T)
    w1T = np.ascontiguousarray(np.asarray(full_inputs["L0_ffn_W1"], np.float32).T)
    w2T = np.ascontiguousarray(np.asarray(full_inputs["L0_ffn_W2"], np.float32).T)
    w3T = np.ascontiguousarray(np.asarray(full_inputs["L0_ffn_W3"], np.float32).T)
    b1c = np.ascontiguousarray(np.asarray(full_inputs["L0_ffn_b1"], np.float32).reshape(32, 128).T)
    b2c = np.ascontiguousarray(np.asarray(full_inputs["L0_ffn_b2"], np.float32).reshape(32, 128).T)
    b3r = np.ascontiguousarray(np.asarray(full_inputs["L0_ffn_b3"], np.float32).reshape(1, D))
    identt = np.eye(128, dtype=bf)
    ones_row = np.ones((1, 128), dtype=np.float32)

    inv = 1.0 / 10000.0 ** (np.arange(0, DH, 2, dtype=np.float64) / DH)
    rope = {}
    maskh = {}
    for h in range(2):
        start = h * T
        pos = start - 512 + np.arange(TK, dtype=np.float64)
        fr = pos[:, None] * inv[None, :]
        cos = np.repeat(np.cos(fr), 2, axis=-1)
        sin = np.repeat(np.sin(fr), 2, axis=-1)
        sgn = np.ones(DH)
        sgn[0::2] = -1.0
        ssgn = sin * sgn[None, :]
        rope[h] = (
            np.ascontiguousarray(cos.reshape(NTK, 128, DH).transpose(1, 0, 2)
                                 .reshape(128, NTK * DH).astype(np.float32)),
            np.ascontiguousarray(ssgn.reshape(NTK, 128, DH).transpose(1, 0, 2)
                                 .reshape(128, NTK * DH).astype(np.float32)))
        r = np.arange(128)[None, :]
        kc = np.arange(128)[:, None]
        mk = np.zeros((128, 5 * 640), dtype=bf)
        for m in range(5):
            for j in range(5):
                c = j * 128 + kc
                valid = (c > r) & (c <= r + 512)
                if h == 0 and m < 4:
                    valid = valid & (m * 128 + c >= 512)
                mk[:, m * 640 + j * 128:m * 640 + (j + 1) * 128] = valid.astype(bf)
        maskh[h] = mk

    in_maps = []
    for core in range(8):
        b, h = core // 2, core % 2
        x_own = np.ascontiguousarray(x[b, h * T:(h + 1) * T])
        x_ctx = (np.ascontiguousarray(x[b, T - 512:T]) if h == 1
                 else np.zeros((512, D), np.float32))
        in_maps.append({
            "x_own": x_own, "x_ctx": x_ctx,
            "wqkvT": wqkvT, "wpT": wpT, "w1T": w1T, "w2T": w2T, "w3T": w3T,
            "b1c": b1c, "b2c": b2c, "b3r": b3r,
            "masks": maskh[h], "ropeC": rope[h][0], "ropeS": rope[h][1],
            "ident": identt, "ones_row": ones_row,
        })
    return in_maps


_CACHE = {}


def _get(debug=False, phases="full"):
    key = (debug, phases)
    if key not in _CACHE:
        _CACHE[key] = build(debug, phases)
    return _CACHE[key]


def kernel(**inputs):
    nc = _get(False, "full")
    in_maps = host_inputs(inputs)
    res = run_bass_kernel_spmd(nc, in_maps, core_ids=list(range(8)))
    out = np.empty((4, 4096, D), dtype=np.float32)
    for core in range(8):
        b, h = core // 2, core % 2
        out[b, h * T:(h + 1) * T] = res.results[core]["y"]
    return out
